# version 17
# baseline (speedup 1.0000x reference)
import contextlib
import numpy as np
import ml_dtypes
import concourse.bass as bass
import concourse.mybir as mybir
from concourse.bass_utils import run_bass_kernel_spmd

F32 = mybir.dt.float32
BF16 = mybir.dt.bfloat16
AF = mybir.ActivationFunctionType
ALU = mybir.AluOpType
AX = mybir.AxisListType

T = 2048
D = 1024
NH = 16
NP = NH // 2
INTER = 8192
SHIFT = 20.0
MASKVAL = -30000.0
GC1 = 0.7978845608028654
GC3 = GC1 * 0.044715

ENGS = ("sync", "tensor", "vector", "scalar", "gpsimd")
ENG_SEM = {"sync": None, "tensor": "PE", "vector": "DV", "scalar": "AC", "gpsimd": "GP"}


class Buf:
    __slots__ = ("name", "writes", "reads")

    def __init__(self, name):
        self.name = name
        self.writes = {}
        self.reads = {}


class Sched:
    def __init__(self):
        self.ops = {e: [] for e in ENGS}
        self.cnt = {}
        self.waited = {e: {} for e in ENGS}

    def _wait(self, eng, sem, val):
        w = self.waited[eng]
        if w.get(sem, 0) >= val:
            return
        w[sem] = val
        self.ops[eng].append(("wait", sem, val))

    def wait_tok(self, eng, tok):
        if tok is not None:
            self._wait(eng, tok[0], tok[1])

    def op(self, eng, fn, reads=(), writes=(), sem=None, inc=None, extra_waits=(),
           scalar_reads=()):
        if sem is None:
            sem = ENG_SEM[eng]
            assert sem is not None, f"op on {eng} needs explicit sem"
            if inc is None:
                inc = 1
        own = ENG_SEM[eng]

        def w_(s_, v, force=False):
            if not force and s_ == own and s_ == "PE":
                return
            self._wait(eng, s_, v)
        for b in reads:
            for s_, v in b.writes.items():
                w_(s_, v)
        for b in scalar_reads:
            for s_, v in b.writes.items():
                w_(s_, v, force=True)
        for b in writes:
            for s_, v in b.writes.items():
                w_(s_, v)
            for s_, v in b.reads.items():
                w_(s_, v)
        for tok in extra_waits:
            self.wait_tok(eng, tok)
        self.cnt[sem] = self.cnt.get(sem, 0) + inc
        tok = (sem, self.cnt[sem])
        self.ops[eng].append(("op", fn, sem, inc))
        for b in tuple(reads) + tuple(scalar_reads):
            b.reads[sem] = tok[1]
        for b in writes:
            b.writes[sem] = tok[1]
            b.reads = {}
        return tok

    def raw_wait(self, eng, sem, val):
        self._wait(eng, sem, val)

    def play(self, name, e, sems):
        for item in self.ops[name]:
            if item[0] == "wait":
                _, sem, val = item
                e.wait_ge(sems[sem], val)
            else:
                _, fn, sem, inc = item
                inst = fn(e)
                if sem is not None:
                    assert inst is not None, f"op on {name} returned None ({sem})"
                    inst.then_inc(sems[sem], inc)


def ceil_div(a, b):
    return -(-a // b)


def build_program(n_seqs=4, n_pairs=NP, nch=16, do_ffn=True, n_ic2=16, dbg=False, seq1_phases='lqao', seq0_phases='lqao'):
    nc = bass.Bass("TRN2", target_bir_lowering=False, detect_race_conditions=False)

    x_hbm = nc.dram_tensor("x", [n_seqs, T, D], F32, kind="ExternalInput").ap()
    winT = nc.dram_tensor("winT", [D, 3072], F32, kind="ExternalInput").ap()
    bin_p = nc.dram_tensor("bin_p", [128, 24], F32, kind="ExternalInput").ap()
    woutT_h = nc.dram_tensor("woutT", [D, D], F32, kind="ExternalInput").ap()
    bout_h = nc.dram_tensor("bout", [1, D], F32, kind="ExternalInput").ap()
    cc_h = nc.dram_tensor("cc", [128, T], BF16, kind="ExternalInput").ap()
    ss_h = nc.dram_tensor("ss", [128, T], BF16, kind="ExternalInput").ap()
    mskb_h = nc.dram_tensor("mskb", [128, 128], F32, kind="ExternalInput").ap()
    ident_h = nc.dram_tensor("ident", [128, 128], BF16, kind="ExternalInput").ap()
    w1_h = nc.dram_tensor("w1", [D, INTER], F32, kind="ExternalInput").ap()
    w2_h = nc.dram_tensor("w2", [INTER, D], F32, kind="ExternalInput").ap()
    ew_h = nc.dram_tensor("ew", [128, 32], F32, kind="ExternalInput").ap()
    win_bf = nc.dram_tensor("win_bf", [24, 128, 1024], BF16, kind="Internal").ap()
    out_hbm = nc.dram_tensor("out", [n_seqs, T, D], F32, kind="ExternalOutput").ap()
    dbg_out = {}
    if dbg:
        for nm, shape, dt in [("d_xln", [128, 8 * T], BF16), ("d_qt", [128, T], BF16),
                              ("d_kt", [128, T], BF16), ("d_vt", [128, 16 * D], BF16),
                              ("d_bias", [128, 256], BF16), ("d_lparts", [128, 8], F32),
                              ("d_lsum", [128, 2], F32), ("d_rl", [128, 2], F32),
                              ("d_pa", [128, T], BF16), ("d_y", [128, 8 * T], BF16)]:
            dbg_out[nm] = nc.dram_tensor(nm, shape, dt, kind="ExternalOutput").ap()
        dbg_out["d_y2"] = nc.dram_tensor("d_y2", [128, 16 * 1024], F32, kind="ExternalOutput").ap()
        dbg_out["d_xln2"] = nc.dram_tensor("d_xln2", [128, 8 * T], BF16, kind="ExternalOutput").ap()

    s = Sched()
    ctx = contextlib.ExitStack()
    sbt = lambda name, shape, dt: ctx.enter_context(nc.sbuf_tensor(name, shape, dt))
    pst = lambda name, shape, dt: ctx.enter_context(nc.psum_tensor(name, shape, dt))

    ccs = sbt("ccs", [128, T], BF16)
    sss = sbt("sss", [128, T], BF16)
    mbias = sbt("mbias", [128, 128], F32)
    ident = sbt("ident_s", [128, 128], BF16)
    bcols = sbt("bcols", [128, 24], F32)
    bout_s = sbt("bout_s", [1, D], BF16)
    ones_r = sbt("ones_r", [1, 128], BF16)
    ones_c = sbt("ones_c", [128, 1], BF16)
    woutT = sbt("woutT_s", [128, 8 * D], BF16)

    XT = sbt("XT", [128, 8 * T], BF16)
    QK = sbt("QK", [128, NP * T + 2 * T], BF16)
    Qt = QK[:, 0:NP * T]
    Kt = QK[:, NP * T:NP * T + 2 * T]
    Vt = sbt("Vt", [128, 16 * D], BF16)

    xbuf = [sbt(f"xbuf{i}", [128, D], F32) for i in range(2)]
    scr0 = sbt("scr0", [128, D], F32)
    scr1 = sbt("scr1", [128, D], F32)
    stat = sbt("stat", [128, 16], F32)
    xln_c = [sbt(f"xln_c{i}", [128, D], BF16) for i in range(2)]

    wbuf = [sbt(f"wbuf{i}", [128, 8 * 128], BF16) for i in range(2)]
    qc_b = sbt("qc_b", [128, 1024], F32)
    qs_b = sbt("qs_b", [128, 1024], F32)
    qw_b = sbt("qw_b", [128, 1024], F32)

    bias_c = sbt("bias_c", [128, 256], BF16)
    lparts = sbt("lparts", [128, 8], F32)
    lsum = sbt("lsum", [128, 2], F32)
    rl = sbt("rl", [128, 2], F32)
    P_A = sbt("P_A", [128, T], BF16)
    P_B = sbt("P_B", [128, T], BF16)
    P_A2 = sbt("P_A2", [128, T], BF16)
    P_B2 = sbt("P_B2", [128, T], BF16)
    PT_A = xln_c[0]
    PT_B = xln_c[1]

    sqx = scr0
    xr = xbuf
    rbuf = [scr0, scr1]
    vfm = sbt("vfm", [128, 1024], BF16)
    sqv = qw_b[:].bitcast(BF16)

    psSA = pst("psSA", [128, 1024], F32)
    psSB = pst("psSB", [128, 1024], F32)
    psT = pst("psT", [128, 1024], BF16)
    psP2 = pst("psP2", [128, 512], F32)
    psY = pst("psY", [128, 512], F32)
    psP = pst("psP", [128, 512], F32)

    B = lambda n: Buf(n)
    b_consts = B("consts")
    b_XT = B("XT")
    b_Qt = [B(f"Qt{t}") for t in range(NP)]
    b_Kt = [B(f"Kt{t}") for t in range(2)]
    b_Vt = [B(f"Vt{t}") for t in range(NP)]
    b_xbuf = [B(f"xbuf{i}") for i in range(2)]
    b_scr0 = B("scr0")
    b_scr1 = B("scr1")
    b_sqx = b_scr0
    b_stat = [B(f"stat{i}") for i in range(2)]
    b_xln = [B(f"xln{i}") for i in range(2)]
    b_wbuf = [B(f"wbuf{i}") for i in range(2)]
    b_winbf = B("winbf")
    b_qc = B("qc"); b_qs = B("qs"); b_qw = B("qw")
    b_qc2 = [B("qc_h0"), B("qc_h1")]
    b_qs2 = [B("qs_h0"), B("qs_h1")]
    b_qw2 = [B("qw_h0"), B("qw_h1")]
    b_lparts = B("lparts"); b_lsum = B("lsum"); b_rl = B("rl")
    bl_P = [[(B("P_A"),), (B("P_B"),)], [(B("P_A2"),), (B("P_B2"),)]]
    b_vfm = B("vfm")
    b_sqv = B("sqv")
    b_PT = [b_xln[0], b_xln[1]]
    bias_all = bias_c
    b_biasP = [B(f"bias{p}") for p in range(NP)]
    b_xr = b_xbuf
    b_rbuf = [b_scr0, b_scr1]
    b_SA = [B("psSA0"), B("psSA1")]
    b_SB = [B("psSB0"), B("psSB1")]
    b_psT = B("psT")
    b_psP2 = B("psP2")
    b_psY = B("psY")
    b_psP = B("psP")
    b_out0 = B("out0_hbm")

    def dump(nm, ap, bufs):
        if not dbg:
            return
        s.op("sync", lambda e, nm=nm, ap=ap: e.dma_start(out=dbg_out[nm], in_=ap),
             reads=tuple(bufs), sem="ST", inc=16)

    def dump_part(nm, i, ap, bufs):
        if not dbg:
            return
        s.op("sync", lambda e, nm=nm, i=i, ap=ap: e.dma_start(
            out=dbg_out[nm][:, i * 1024:(i + 1) * 1024], in_=ap),
            reads=tuple(bufs), sem="ST", inc=16)

    for dst, src in [(ccs, cc_h), (sss, ss_h), (mbias, mskb_h), (ident, ident_h)]:
        s.op("sync", lambda e, d=dst, so=src: e.dma_start(out=d[:], in_=so),
             writes=(b_consts,), sem="LD", inc=16)
    s.op("sync", lambda e: e.dma_start(out=bcols[:], in_=bin_p),
         writes=(b_consts,), sem="LD", inc=16)
    s.op("gpsimd", lambda e: e.dma_start(out=bout_s[:], in_=bout_h),
         writes=(b_consts,), sem="GD", inc=16)
    s.op("gpsimd", lambda e: e.dma_start(
        out=woutT[:].rearrange("p (t d) -> p t d", t=8),
        in_=woutT_h.rearrange("(t p) d -> p t d", p=128)),
        writes=(b_consts,), sem="GD", inc=16)
    s.op("gpsimd", lambda e: e.memset(ones_r[:], 1.0), writes=(b_consts,))
    s.op("gpsimd", lambda e: e.memset(ones_c[:], 1.0), writes=(b_consts,))
    s.op("gpsimd", lambda e: e.memset(stat[:], 0.0), writes=(b_stat[0], b_stat[1]))
    if dbg:
        s.op("gpsimd", lambda e: e.memset(XT[:], 0.0), writes=(b_XT,))
        s.op("gpsimd", lambda e: e.memset(QK[:], 0.0), writes=tuple(b_Qt + b_Kt))
        s.op("gpsimd", lambda e: e.memset(Vt[:], 0.0), writes=tuple(b_Vt))
        s.op("gpsimd", lambda e: e.memset(P_A[:], 0.0), writes=(bl_P[0][0][0],))
        s.op("gpsimd", lambda e: e.memset(bias_c[:], 0.0), writes=tuple(b_biasP))
        s.op("gpsimd", lambda e: e.memset(lparts[:], 0.0), writes=(b_lparts,))
        s.op("gpsimd", lambda e: e.memset(lsum[:], 0.0), writes=(b_lsum,))
        s.op("gpsimd", lambda e: e.memset(rl[:], 0.0), writes=(b_rl,))

    def phase_ln(src_hbm, dst_XT, b_dst, src_dep=None):
        def ln_s1(tc):
            xb, bx = xbuf[tc % 2], b_xbuf[tc % 2]
            o = (tc % 2) * 8
            bst = b_stat[tc % 2]
            s.op("sync", lambda e, xb=xb, tc=tc: e.dma_start(
                out=xb[:], in_=src_hbm[tc * 128:(tc + 1) * 128, :]),
                reads=(src_dep,) if src_dep is not None else (),
                writes=(bx,), sem="LD", inc=16)
            s.op("vector", lambda e, xb=xb, o=o: e.tensor_reduce(
                stat[:, o:o + 1], xb[:], AX.X, ALU.add), reads=(bx,), writes=(bst,))
            s.op("scalar", lambda e, xb=xb, o=o: e.activation(
                sqx[:], xb[:], AF.Square, accum_out=stat[:, o + 1:o + 2]),
                reads=(bx, bst), writes=(b_sqx, bst))
            s.op("scalar", lambda e, o=o: e.copy(sqx[:, 0:1], stat[:, o + 1:o + 2]),
                 reads=(bst,), writes=(bst, b_sqx))

        def ln_s23(tc):
            xb, bx = xbuf[tc % 2], b_xbuf[tc % 2]
            o = (tc % 2) * 8
            bst = b_stat[tc % 2]
            xlc, bxl = xln_c[tc % 2], b_xln[tc % 2]
            s.op("vector", lambda e, o=o: e.tensor_scalar(
                stat[:, o + 2:o + 3], stat[:, o:o + 1], 1.0 / D, None, ALU.mult),
                reads=(bst,), writes=(bst,))
            s.op("vector", lambda e, o=o: e.tensor_scalar(
                stat[:, o + 3:o + 4], stat[:, o + 2:o + 3], stat[:, o + 2:o + 3],
                1e-5, ALU.mult, ALU.subtract), reads=(), scalar_reads=(bst,),
                writes=(bst,))
            s.op("vector", lambda e, o=o: e.scalar_tensor_tensor(
                stat[:, o + 4:o + 5], stat[:, o + 1:o + 2], 1.0 / D, stat[:, o + 3:o + 4],
                ALU.mult, ALU.subtract), reads=(bst,), writes=(bst,))
            s.op("scalar", lambda e, o=o: e.activation(
                stat[:, o + 5:o + 6], stat[:, o + 4:o + 5], AF.Ln),
                reads=(bst,), writes=(bst,))
            s.op("scalar", lambda e, o=o: e.activation(
                stat[:, o + 6:o + 7], stat[:, o + 5:o + 6], AF.Exp, scale=-0.5),
                reads=(bst,), writes=(bst,))
            s.op("vector", lambda e, o=o: e.tensor_scalar(
                stat[:, o + 7:o + 8], stat[:, o + 2:o + 3], -1.0, None, ALU.mult),
                reads=(bst,), writes=(bst,))
            s.op("vector", lambda e, xb=xb, xlc=xlc, o=o: e.tensor_scalar(
                xlc[:], xb[:], stat[:, o + 7:o + 8], stat[:, o + 6:o + 7],
                ALU.add, ALU.mult), reads=(bx,), scalar_reads=(bst,), writes=(bxl,))
            for dc in range(8):
                s.op("tensor", lambda e, dc=dc, xlc=xlc: e.transpose(
                    psT[:, dc * 128:(dc + 1) * 128], xlc[:, dc * 128:(dc + 1) * 128],
                    ident[:]), reads=(bxl, b_consts), writes=(b_psT,))
            s.op("vector", lambda e, tc=tc: e.tensor_copy(
                dst_XT[:].rearrange("p (dc t) -> p dc t", dc=8)[:, :, tc * 128:(tc + 1) * 128],
                psT[:].rearrange("p (dc t) -> p dc t", dc=8)),
                reads=(b_psT,), writes=(b_dst,))

        ln_s1(0)
        for tc in range(nch):
            if tc + 1 < nch:
                ln_s1(tc + 1)
            ln_s23(tc)

    def phase_qkv_start(si):
        ntc = nch // 4
        fc_order = []
        for t_ in range(n_pairs):
            fc_order += [t_, 8 + t_, 16 + t_]
        nfc = len(fc_order)
        state = {"fc_order": fc_order, "nfc": nfc, "ntc": ntc,
                 "pending": [None]}

        def wload(step):
            fc = fc_order[step]
            wb, bw = wbuf[step % 2], b_wbuf[step % 2]
            if si == 0:
                s.op("gpsimd", lambda e, fc=fc, wb=wb: e.dma_start(
                    out=wb[:].rearrange("p (dc n) -> p dc n", dc=8),
                    in_=winT[:, fc * 128:(fc + 1) * 128].rearrange("(dc p) n -> p dc n", p=128)),
                    writes=(bw,), sem="GD", inc=16)
                s.op("sync", lambda e, fc=fc, wb=wb: e.dma_start(
                    out=win_bf[fc], in_=wb[:]),
                    reads=(bw,), writes=(b_winbf,), sem="ST", inc=16)
            else:
                s.op("gpsimd", lambda e, fc=fc, wb=wb: e.dma_start(
                    out=wb[:], in_=win_bf[fc]),
                    reads=(b_winbf,), writes=(bw,), sem="GD", inc=16)

        state["wload"] = wload
        wload(0)
        if nfc > 1:
            wload(1)
        return state

    def phase_qkv_pair(si, pair_idx, state):
        fc_order, nfc, ntc = state["fc_order"], state["nfc"], state["ntc"]
        wload = state["wload"]
        pending_add = state["pending"]

        def flush_add():
            if pending_add[0] is not None:
                pending_add[0]()
                pending_add[0] = None

        for step in range(pair_idx * 3, pair_idx * 3 + 3):
            fc = fc_order[step]
            wb, bw = wbuf[step % 2], b_wbuf[step % 2]
            is_v = fc >= 16
            pair = fc % 8
            for tc4 in range(ntc):
                bank = (psP if tc4 % 2 == 0 else psP2)[:, 0:512]
                bbank = b_psP if tc4 % 2 == 0 else b_psP2
                loc = (tc4 % 2) * 512
                for dc in range(8):
                    s.op("tensor", lambda e, dc=dc, tc4=tc4, wb=wb, bank=bank: e.matmul(
                        bank, wb[:, dc * 128:(dc + 1) * 128],
                        XT[:, dc * T + tc4 * 512: dc * T + (tc4 + 1) * 512],
                        start=(dc == 0), stop=(dc == 7)),
                        reads=(bw, b_XT), writes=(bbank,) if dc in (0, 7) else ())
                if tc4 == ntc - 1 and step + 2 < nfc:
                    wload(step + 2)
                if is_v:
                    s.op("vector", lambda e, fc=fc, bank=bank, loc=loc: e.tensor_scalar(
                        vfm[:, loc:loc + 512], bank,
                        bcols[:, fc:fc + 1], None, ALU.add),
                        reads=(bbank, b_consts), writes=(b_vfm,))
                    for j in range(4):
                        s.op("tensor", lambda e, j=j, loc=loc: e.transpose(
                            psT[:, loc + j * 128:loc + (j + 1) * 128],
                            vfm[:, loc + j * 128:loc + (j + 1) * 128],
                            ident[:]), reads=(b_vfm, b_consts), writes=(b_psT,))
                    s.op("vector", lambda e, tc4=tc4, pair=pair, loc=loc: e.tensor_copy(
                        Vt[:].rearrange("p (tc hv) -> p tc hv", hv=1024)
                          [:, tc4 * 4:(tc4 + 1) * 4, pair * 128:(pair + 1) * 128],
                        psT[:, loc:loc + 512].rearrange("p (j n) -> p j n", j=4)),
                        reads=(b_psT,), writes=(b_Vt[pair],))
                else:
                    flush_add()
                    bqc = b_qc2[tc4 % 2]
                    bqs = b_qs2[tc4 % 2]
                    bqw = b_qw2[tc4 % 2]
                    s.op("vector", lambda e, tc4=tc4, fc=fc, bank=bank, loc=loc: e.scalar_tensor_tensor(
                        qc_b[:, loc:loc + 512], bank,
                        bcols[:, fc:fc + 1], ccs[:, tc4 * 512:(tc4 + 1) * 512],
                        ALU.add, ALU.mult), reads=(bbank, b_consts), writes=(bqc,))
                    s.op("vector", lambda e, tc4=tc4, fc=fc, bank=bank, loc=loc: e.scalar_tensor_tensor(
                        qs_b[:, loc:loc + 512], bank,
                        bcols[:, fc:fc + 1], sss[:, tc4 * 512:(tc4 + 1) * 512],
                        ALU.add, ALU.mult), reads=(bbank, b_consts), writes=(bqs,))
                    for g in range(4):
                        src_g = [1, 0, 3, 2][g]
                        eng_c = "vector" if g < 2 else "gpsimd"
                        s.op(eng_c, lambda e, g=g, src_g=src_g, loc=loc: e.tensor_copy(
                            qw_b[g * 32:(g + 1) * 32, loc:loc + 512],
                            qs_b[src_g * 32:(src_g + 1) * 32, loc:loc + 512]),
                            reads=(bqs,), writes=(bqw,))
                    if fc < 8:
                        dst, bdst, dbase = Qt, b_Qt[pair], pair * T
                    else:
                        dst, bdst, dbase = Kt, b_Kt[pair % 2], (pair % 2) * T
                    eng = "vector" if fc < 8 else "gpsimd"

                    def mk_add(dst=dst, bdst=bdst, dbase=dbase, tc4=tc4, loc=loc,
                               bqc=bqc, bqw=bqw, eng=eng):
                        def do():
                            s.op(eng, lambda e: e.tensor_tensor(
                                dst[:, dbase + tc4 * 512: dbase + tc4 * 512 + 512],
                                qc_b[:, loc:loc + 512], qw_b[:, loc:loc + 512], ALU.add),
                                reads=(bqc, bqw), writes=(bdst,))
                        return do
                    pending_add[0] = mk_add()
            if is_v:
                flush_add()
                nqb = nch
                s.op("scalar", lambda e, pair=pair: e.activation(
                    sqv[:, 0:nch * 128], Qt[:, pair * T: pair * T + nch * 128],
                    AF.Square),
                    reads=(b_Qt[pair],),
                    writes=(b_sqv, b_qw2[0], b_qw2[1]))
                for hh in range(2):
                    for qb in range(nqb):
                        s.op("tensor", lambda e, qb=qb, hh=hh: e.matmul(
                            psP[:, hh * 16 + qb:hh * 16 + qb + 1],
                            sqv[64 * hh:64 * hh + 64, qb * 128:qb * 128 + 128],
                            ones_c[64 * hh:64 * hh + 64, :],
                            start=(qb == 0), stop=(qb == nqb - 1)),
                            reads=(b_sqv, b_consts),
                            writes=(b_psP,) if qb in (0, nqb - 1) else ())
                    s.op("vector", lambda e, hh=hh, pair=pair, nqb=nqb: e.tensor_scalar(
                        bias_all[:, pair * 32 + hh * 16:pair * 32 + hh * 16 + nqb],
                        psP[:, hh * 16:hh * 16 + nqb], -0.125, -SHIFT,
                        ALU.mult, ALU.add),
                        reads=(b_psP,), writes=(b_biasP[pair],))
                flush_add()
        flush_add()
    def phase_attn_pair(si, pair):
        nqb = nch
        if True:
            def emit_S_mm(qb, kc, pair=pair):
                L = (qb + 1) * 128
                w = min(512, L - kc * 512)
                for hh in (1, 0):
                    bankT, bbs = (psSA, b_SA) if hh == 0 else (psSB, b_SB)
                    bbank = bbs[kc % 2]
                    bank = bankT[:, (kc % 2) * 512:(kc % 2) * 512 + w]
                    rows = slice(64 * hh, 64 * hh + 64)
                    s.op("tensor", lambda e, rows=rows, qb=qb, kc=kc, w=w, bank=bank: e.matmul(
                        bank, Qt[rows, pair * T + qb * 128: pair * T + (qb + 1) * 128],
                        Kt[rows, (pair % 2) * T + kc * 512: (pair % 2) * T + kc * 512 + w],
                        start=True, stop=True),
                        reads=(b_Qt[pair], b_Kt[pair % 2]), writes=(bbank,))

            def stage_S(qb, pair=pair):
                nkc = ceil_div((qb + 1) * 128, 512)
                for kc in range(min(2, nkc)):
                    emit_S_mm(qb, kc)

            def softmax_stage(qb, pair=pair):
                L = (qb + 1) * 128
                nkc = ceil_div(L, 512)
                ngr = ceil_div(L, 1024)
                par = qb % 2
                PbA = P_A if par == 0 else P_A2
                PbB = P_B if par == 0 else P_B2
                for g in range(ngr):
                    w = min(1024, L - g * 1024)
                    kc_hi = min(2 * g + 1, nkc - 1)
                    for hh in range(2):
                        bankT, bbs = (psSA, b_SA) if hh == 0 else (psSB, b_SB)
                        bb0, bb1 = bbs[0], bbs[1]
                        if kc_hi == nkc - 1:
                            moff = qb * 128 - g * 1024
                            s.op("vector", lambda e, bankT=bankT, moff=moff: e.tensor_tensor(
                                bankT[:, moff:moff + 128], bankT[:, moff:moff + 128],
                                mbias[:], ALU.add),
                                reads=(bb0, bb1, b_consts),
                                writes=(bb0,) if moff < 512 else (bb1,))
                        s.op("scalar", lambda e, bankT=bankT, hh=hh, g=g, w=w, qb=qb, PbA=PbA, PbB=PbB: e.activation(
                            (PbA if hh == 0 else PbB)[:, g * 1024:g * 1024 + w],
                            bankT[:, 0:w], AF.Exp, scale=0.125,
                            bias=bias_all[:, pair * 32 + hh * 16 + qb:pair * 32 + hh * 16 + qb + 1],
                            accum_out=lparts[:, hh * 4 + g:hh * 4 + g + 1]),
                            reads=(bb0, bb1, b_biasP[pair], b_lparts),
                            writes=bl_P[par][hh] + (b_lparts,))
                    if 2 * g + 2 < nkc:
                        emit_S_mm(qb, 2 * g + 2)
                    if 2 * g + 3 < nkc:
                        emit_S_mm(qb, 2 * g + 3)
                s.op("scalar", lambda e: e.copy(lsum[:, 0:1], lparts[:, 0:1]),
                     reads=(b_lparts,), writes=(b_lparts, b_lsum))
                s.op("vector", lambda e, ngr=ngr: e.tensor_reduce(
                    lsum[:, 0:2], lparts[:].rearrange("p (h k) -> p h k", h=2)[:, :, 0:ngr],
                    AX.X, ALU.add), reads=(b_lparts,), writes=(b_lsum,))
                s.op("vector", lambda e: e.reciprocal(rl[:, 0:2], lsum[:, 0:2]),
                     reads=(b_lsum,), writes=(b_rl,))
                for hh in range(2):
                    Pbuf = PbA if hh == 0 else PbB
                    s.op("vector", lambda e, hh=hh, Pbuf=Pbuf, L=L: e.tensor_scalar(
                        Pbuf[:, 0:L], Pbuf[:, 0:L], rl[:, hh:hh + 1], None, ALU.mult),
                        reads=bl_P[par][hh], scalar_reads=(b_rl,),
                        writes=bl_P[par][hh])

            def pv_stage(qb, pair=pair):
                L = (qb + 1) * 128
                npk = ceil_div(L, 1024)
                par = qb % 2
                PbA = P_A if par == 0 else P_A2
                PbB = P_B if par == 0 else P_B2
                for pk in range(npk2 := ceil_div(L, 512)):
                    wpk = min(4, L // 128 - pk * 4)
                    PTs = PT_A if pk % 2 == 0 else PT_B
                    bpt = b_PT[pk % 2]
                    for hh in range(2):
                        Pbuf = PbA if hh == 0 else PbB
                        for j in range(wpk):
                            kcb = pk * 4 + j
                            s.op("tensor", lambda e, j=j, kcb=kcb, hh=hh, Pbuf=Pbuf: e.transpose(
                                psT[:, j * 256 + hh * 128:j * 256 + hh * 128 + 128],
                                Pbuf[:, kcb * 128:(kcb + 1) * 128], ident[:]),
                                reads=bl_P[par][hh] + (b_consts,), writes=(b_psT,))
                    s.op("vector", lambda e, PTs=PTs, wpk=wpk: e.tensor_copy(
                        PTs[:, 0:wpk * 256], psT[:, 0:wpk * 256]),
                        reads=(b_psT,), writes=(bpt,))
                    for hh in range(2):
                        ybase = 64 * hh
                        for j in range(wpk):
                            kcb = pk * 4 + j
                            s.op("tensor", lambda e, j=j, kcb=kcb, hh=hh, PTs=PTs, ybase=ybase, L=L: e.matmul(
                                psY[ybase:ybase + 64, 0:128],
                                Vt[:, kcb * 1024 + (2 * pair + hh) * 64: kcb * 1024 + (2 * pair + hh) * 64 + 64],
                                PTs[:, j * 256 + hh * 128:j * 256 + hh * 128 + 128],
                                start=(kcb == 0), stop=(kcb == L // 128 - 1),
                                skip_group_check=True),
                                reads=(bpt, b_Vt[pair]),
                                writes=(b_psY,) if kcb in (0, L // 128 - 1) else ())
                s.op("vector", lambda e, qb=qb: e.tensor_copy(
                    Qt[:, pair * T + qb * 128: pair * T + (qb + 1) * 128], psY[:, 0:128]),
                    reads=(b_psY,), writes=(b_Qt[pair],))

            stage_S(0)
            softmax_stage(0)
            if nqb > 1:
                stage_S(1)
            for qb in range(1, nqb):
                softmax_stage(qb)
                if qb + 1 < nqb:
                    stage_S(qb + 1)
                pv_stage(qb - 1)
            pv_stage(nqb - 1)

    def phase_out(si):
        for tb in range(nch):
            xb, bx = xr[tb % 2], b_xr[tb % 2]
            rb, br = rbuf[tb % 2], b_rbuf[tb % 2]
            s.op("sync", lambda e, xb=xb, si=si, tb=tb: e.dma_start(
                out=xb[:], in_=x_hbm[si, tb * 128:(tb + 1) * 128, :]),
                writes=(bx,), sem="LD", inc=16)
            for dh in range(2):
                bankT, bbs = (psSA, b_SA) if dh == 0 else (psSB, b_SB)
                bank, bbank = bankT[:, 0:512], bbs[0]
                for t_ in range(n_pairs):
                    s.op("tensor", lambda e, t_=t_, tb=tb, dh=dh, bank=bank: e.matmul(
                        bank, Qt[:, t_ * T + tb * 128: t_ * T + tb * 128 + 128],
                        woutT[:, t_ * D + dh * 512: t_ * D + dh * 512 + 512],
                        start=(t_ == 0), stop=False),
                        reads=(b_Qt[t_], b_consts), writes=(bbank,) if t_ == 0 else ())
                s.op("tensor", lambda e, dh=dh, bank=bank: e.matmul(
                    bank, ones_r[0:1, :], bout_s[0:1, dh * 512:dh * 512 + 512],
                    start=False, stop=True), reads=(b_consts,), writes=(bbank,))
                s.op("vector", lambda e, rb=rb, xb=xb, dh=dh, bank=bank: e.tensor_tensor(
                    rb[:, dh * 512:dh * 512 + 512], bank,
                    xb[:, dh * 512:dh * 512 + 512], ALU.add),
                    reads=(bbank, bx), writes=(br,))
            tok = s.op("sync", lambda e, rb=rb, si=si, tb=tb: e.dma_start(
                out=out_hbm[si, tb * 128:(tb + 1) * 128, :], in_=rb[:]),
                reads=(br,), writes=(b_out0,) if si == 0 else (), sem="ST", inc=16)
        return tok

    def phase_ffn():
        ew_ap = qc_b[:, 992:1024]
        sv = stat[:, 0:16]
        w1b = [Vt[:, 0:4096], Vt[:, 4096:8192]]
        w2b = [Vt[:, 8192:12288], Vt[:, 12288:16384]]
        _qtf = Qt[:].bitcast(F32)
        _ktf = Kt[:].bitcast(F32)
        y2c = [_qtf[:, i * 1024:(i + 1) * 1024] for i in range(8)]
        y2c += [_ktf[:, i * 1024:(i + 1) * 1024] for i in range(2)]
        y2c += [P_A[:].bitcast(F32), P_B[:].bitcast(F32),
                P_A2[:].bitcast(F32), P_B2[:].bitcast(F32),
                ccs[:].bitcast(F32), sss[:].bitcast(F32)]
        sq_g = qc_b
        u_g = qs_b
        t_g = qw_b
        h_sb = PT_A[:]
        b_ew = B("ew")
        b_sv = b_PT[1]
        b_w1b = [b_Vt[0], b_Vt[1]]
        b_w2b = [b_Vt[2], b_Vt[3]]
        b_y2 = B("y2")
        b_g = [B(f"g{i}") for i in range(6)]
        b_psH = [b_SA[0], b_SA[1], b_SB[0], b_SB[1]]
        b_psO = [b_psY, b_psP]

        s.op("sync", lambda e: e.dma_start(out=ew_ap, in_=ew_h),
            writes=(b_ew, b_qc2[0], b_qc2[1]), sem="LD", inc=16)
        _ties = ([b_Qt[i] for i in range(8)] + b_Kt +
                 [bl_P[0][0][0], bl_P[0][1][0], bl_P[1][0][0], bl_P[1][1][0],
                  b_consts, b_consts])
        for _mi in range(16):
            s.op("gpsimd", lambda e, _mi=_mi: e.memset(y2c[_mi][:], 0.0),
                 writes=(b_y2, _ties[_mi]))
        import os as _os
        _stage = int(_os.environ.get("FFN_STAGE", "4"))
        if _stage <= 1:
            return

        phase_ln(out_hbm[0], XT, b_XT, src_dep=b_out0)
        dump("d_xln2", XT[:], [b_XT])
        if _stage <= 2:
            return
        s.op("vector", lambda e: e.tensor_tensor(
            sv, ew_ap.rearrange("p (c k) -> p c k", k=2)[:, :, 0],
            ew_ap.rearrange("p (c k) -> p c k", k=2)[:, :, 1], ALU.add),
            reads=(b_ew, b_qc2[0], b_qc2[1]),
            writes=(b_sv, b_stat[0], b_stat[1]))

        def ffn_mm1(t):
            ic2, tw = divmod(t, nch // 2)
            wb1, bw1 = w1b[ic2 % 2], b_w1b[ic2 % 2]
            if tw == 0:
                s.op("gpsimd", lambda e, ic2=ic2, wb1=wb1: e.dma_start(
                    out=wb1.rearrange("p (dc i) -> p dc i", dc=8),
                    in_=w1_h[:, ic2 * 512:(ic2 + 1) * 512].rearrange("(dc p) i -> p dc i", p=128)),
                    writes=(bw1,), sem="GD", inc=16)
                wb2, bw2 = w2b[ic2 % 2], b_w2b[ic2 % 2]
                s.op("gpsimd", lambda e, ic2=ic2, wb2=wb2: e.dma_start(
                    out=wb2.rearrange("p (i4 d) -> p i4 d", i4=4),
                    in_=w2_h[ic2 * 512:(ic2 + 1) * 512, :].rearrange("(i4 p) d -> p i4 d", p=128)),
                    writes=(bw2,), sem="GD", inc=16)
            for icsub in range(4):
                bankT = (psSA, psSA, psSB, psSB)[icsub]
                bank = bankT[:, (icsub % 2) * 512:(icsub % 2) * 512 + 256]
                for dc in range(8):
                    s.op("tensor", lambda e, icsub=icsub, dc=dc, tw=tw, wb1=wb1, bank=bank: e.matmul(
                        bank, wb1[:, dc * 512 + icsub * 128: dc * 512 + (icsub + 1) * 128],
                        XT[:, dc * T + tw * 256:dc * T + tw * 256 + 256],
                        start=(dc == 0), stop=(dc == 7)),
                        reads=(bw1, b_XT), writes=(b_psH[icsub],) if dc in (0, 7) else ())

        def ffn_gelu(t):
            h_sb = (PT_A if t % 2 == 0 else PT_B)[:]
            bh = b_PT[t % 2]
            for half, bankT, rd in ((0, psSA, b_psH[0:2]), (1, psSB, b_psH[2:4])):
                hv = bankT[:].rearrange("p (i w) -> p i w", i=2)[:, :, 0:256]
                s.op("scalar", lambda e, hv=hv, half=half: e.activation(
                    sq_g[:, half * 512:half * 512 + 512].rearrange("p (i w) -> p i w", i=2),
                    hv, AF.Square), reads=tuple(rd), writes=(b_g[0], b_qc2[0], b_qc2[1]))
            s.op("vector", lambda e: e.tensor_scalar(
                sq_g[:], sq_g[:], GC3, GC1, ALU.mult, ALU.add),
                reads=(b_g[0],), writes=(b_g[1],))
            for half, bankT, rd in ((0, psSA, b_psH[0:2]), (1, psSB, b_psH[2:4])):
                hv = bankT[:].rearrange("p (i w) -> p i w", i=2)[:, :, 0:256]
                s.op("vector", lambda e, hv=hv, half=half: e.tensor_tensor(
                    u_g[:, half * 512:half * 512 + 512].rearrange("p (i w) -> p i w", i=2),
                    sq_g[:, half * 512:half * 512 + 512].rearrange("p (i w) -> p i w", i=2),
                    hv, ALU.mult),
                    reads=(b_g[1],) + tuple(rd), writes=(b_g[2], b_qs2[0], b_qs2[1]))
            s.op("scalar", lambda e: e.activation(t_g[:], u_g[:], AF.Tanh),
                reads=(b_g[2],), writes=(b_g[3], b_qw2[0], b_qw2[1], b_sqv))
            s.op("vector", lambda e: e.tensor_scalar(
                t_g[:], t_g[:], 0.5, 0.5, ALU.mult, ALU.add),
                reads=(b_g[3],), writes=(b_g[4],))
            for half, bankT, rd in ((0, psSA, b_psH[0:2]), (1, psSB, b_psH[2:4])):
                hv = bankT[:].rearrange("p (i w) -> p i w", i=2)[:, :, 0:256]
                s.op("vector", lambda e, hv=hv, half=half, h_sb=h_sb: e.tensor_tensor(
                    h_sb[:, half * 512:half * 512 + 512].rearrange("p (i w) -> p i w", i=2),
                    t_g[:, half * 512:half * 512 + 512].rearrange("p (i w) -> p i w", i=2),
                    hv, ALU.mult),
                    reads=(b_g[4],) + tuple(rd), writes=(bh,))

        def ffn_mm2(t):
            ic2, tw = divmod(t, nch // 2)
            wb2, bw2 = w2b[ic2 % 2], b_w2b[ic2 % 2]
            h_sb = (PT_A if t % 2 == 0 else PT_B)[:]
            bh = b_PT[t % 2]
            for dh in range(2):
                for tb2 in range(2):
                    psO = psY if tb2 == 0 else psP
                    bO = b_psO[tb2]
                    for icsub in range(4):
                        s.op("tensor", lambda e, icsub=icsub, dh=dh, tb2=tb2, psO=psO, wb2=wb2, h_sb=h_sb: e.matmul(
                            psO[:, 0:512],
                            h_sb[:, icsub * 256 + tb2 * 128: icsub * 256 + tb2 * 128 + 128],
                            wb2[:, icsub * 1024 + dh * 512: icsub * 1024 + dh * 512 + 512],
                            start=(icsub == 0), stop=(icsub == 3)),
                            reads=(bh, bw2), writes=(bO,) if icsub in (0, 3) else ())
                    s.op("vector", lambda e, tw=tw, dh=dh, tb2=tb2, psO=psO: e.tensor_tensor(
                        y2c[tw * 2 + tb2][:, dh * 512:dh * 512 + 512],
                        y2c[tw * 2 + tb2][:, dh * 512:dh * 512 + 512],
                        psO[:, 0:512], ALU.add),
                        reads=(b_psO[tb2], b_y2), writes=(b_y2,))

        n_iter = n_ic2 * (nch // 2)
        ffn_mm1(0)
        ffn_gelu(0)
        for t in range(1, n_iter + 1):
            if t < n_iter:
                ffn_mm1(t)
            ffn_mm2(t - 1)
            if t < n_iter:
                ffn_gelu(t)
        if dbg:
            for _mi in range(16):
                dump_part("d_y2", _mi, y2c[_mi], [b_y2])
        for tb in range(nch):
            xb, bx = xr[tb % 2], b_xr[tb % 2]
            rb, br = rbuf[tb % 2], b_rbuf[tb % 2]
            s.op("sync", lambda e, xb=xb, tb=tb: e.dma_start(
                out=xb[:], in_=out_hbm[0, tb * 128:(tb + 1) * 128, :]),
                reads=(b_out0,), writes=(bx,), sem="LD", inc=16)
            s.op("vector", lambda e, rb=rb, xb=xb, tb=tb: e.scalar_tensor_tensor(
                rb[:], y2c[tb][:], sv[:, tb:tb + 1], xb[:],
                ALU.mult, ALU.add), reads=(b_y2, bx), scalar_reads=(b_sv,),
                writes=(br,))
            s.op("sync", lambda e, rb=rb, tb=tb: e.dma_start(
                out=out_hbm[0, tb * 128:(tb + 1) * 128, :], in_=rb[:]),
                reads=(br,), writes=(b_out0,), sem="ST", inc=16)

    for si in range(n_seqs):
        ph = seq1_phases if si > 0 else seq0_phases
        if 'l' not in ph:
            continue
        phase_ln(x_hbm[si], XT, b_XT)
        if 'q' not in ph:
            continue
        if 'a' in ph:
            qkv_state = phase_qkv_start(si)
            for pair in range(n_pairs):
                phase_qkv_pair(si, pair, qkv_state)
                if pair >= 1:
                    phase_attn_pair(si, pair - 1)
            phase_attn_pair(si, n_pairs - 1)
        else:
            qkv_state = phase_qkv_start(si)
            for pair in range(n_pairs):
                phase_qkv_pair(si, pair, qkv_state)
        if 'a' not in ph or 'o' not in ph:
            continue
        phase_out(si)
    if do_ffn:
        phase_ffn()

    if "ST" in s.cnt:
        s.raw_wait("sync", "ST", s.cnt["ST"])

    sems = {}
    for name in ["LD", "SW", "ST", "GD", "PE", "DV", "AC", "GP"]:
        sems[name] = ctx.enter_context(nc.semaphore(name))
    with nc.Block() as blk:
        @blk.sync
        def _(e):
            s.play("sync", e, sems)

        @blk.tensor
        def _(e):
            s.play("tensor", e, sems)

        @blk.vector
        def _(e):
            s.play("vector", e, sems)

        @blk.scalar
        def _(e):
            s.play("scalar", e, sems)

        @blk.gpsimd
        def _(e):
            s.play("gpsimd", e, sems)

    ctx.close()
    return nc



def _prep_consts():
    half = 32
    p = np.arange(half, dtype=np.float64) / (half - 1)
    thetas = 1.0 * (10000.0 / 1.0) ** p
    pos = np.arange(T, dtype=np.float64)
    ang = pos[None, :] / thetas[:, None]
    cos32 = np.cos(ang).astype(np.float32)
    sin32 = np.sin(ang).astype(np.float32)
    CC = np.tile(cos32, (4, 1)).astype(ml_dtypes.bfloat16)
    SS = np.concatenate([-sin32, sin32, -sin32, sin32], axis=0).astype(ml_dtypes.bfloat16)
    i = np.arange(128)
    M = np.where(i[None, :] <= i[:, None], 0.0, MASKVAL).astype(np.float32)
    I = np.eye(128, dtype=np.float32).astype(ml_dtypes.bfloat16)
    return CC, SS, M, I


def _perm3072():
    perm = []
    for sec in range(2):
        base = sec * 1024
        for h in range(NH):
            perm += list(range(base + h * 64 + 32, base + h * 64 + 64))
            perm += list(range(base + h * 64, base + h * 64 + 32))
    perm += list(range(2048, 3072))
    return np.array(perm)


def make_in_maps(x, W_in, b_in, W_out, b_out_, ew, w1, w2, n_cores=8):
    perm = _perm3072()
    winT = np.ascontiguousarray(W_in[perm].T.astype(np.float32))
    bin_p = np.ascontiguousarray(b_in[perm].astype(np.float32).reshape(24, 128).T)
    woutT = np.ascontiguousarray(W_out.T.astype(np.float32))
    bout = np.ascontiguousarray(b_out_.astype(np.float32)).reshape(1, D)
    CC, SS, M, I = _prep_consts()
    in_maps = []
    for c in range(n_cores):
        batches = [c, c + 8, c + 16, c + 24]
        ewc = (ew[c * T:(c + 1) * T].astype(np.float32) if c < 2
               else np.zeros((T, 2), np.float32))
        ewc = np.ascontiguousarray(
            ewc.reshape(16, 128, 2).transpose(1, 0, 2).reshape(128, 32))
        in_maps.append({
            "x": np.ascontiguousarray(x[batches].astype(np.float32)),
            "winT": winT, "bin_p": bin_p, "woutT": woutT, "bout": bout,
            "cc": CC, "ss": SS, "mskb": M, "ident": I,
            "w1": np.ascontiguousarray(w1.astype(np.float32)),
            "w2": np.ascontiguousarray(w2.astype(np.float32)),
            "ew": np.ascontiguousarray(ewc),
        })
    return in_maps


def kernel(L0_attn_x, L0_attn_W_in, L0_attn_b_in, L0_attn_W_out, L0_attn_b_out,
           L0_ffn_scores, L0_ffn_expert_weights, L0_ffn_top_experts,
           L0_ffn_w1, L0_ffn_w2):
    x = np.asarray(L0_attn_x)
    in_maps = make_in_maps(x, np.asarray(L0_attn_W_in), np.asarray(L0_attn_b_in),
                           np.asarray(L0_attn_W_out), np.asarray(L0_attn_b_out),
                           np.asarray(L0_ffn_expert_weights),
                           np.asarray(L0_ffn_w1), np.asarray(L0_ffn_w2))
    nc = build_program()
    res = run_bass_kernel_spmd(nc, in_maps, core_ids=list(range(8)))
    out = np.empty((32, T, D), np.float32)
    for c in range(8):
        out[[c, c + 8, c + 16, c + 24]] = res.results[c]["out"]
    return out


# version 18
# speedup vs baseline: 1.2073x; 1.2073x over previous
import contextlib
import numpy as np
import ml_dtypes
import concourse.bass as bass
import concourse.mybir as mybir
from concourse.bass_utils import run_bass_kernel_spmd

F32 = mybir.dt.float32
BF16 = mybir.dt.bfloat16
AF = mybir.ActivationFunctionType
ALU = mybir.AluOpType
AX = mybir.AxisListType

T = 2048
D = 1024
NH = 16
NP = NH // 2
INTER = 8192
SHIFT = 20.0
MASKVAL = -30000.0
GC1 = 0.7978845608028654
GC3 = GC1 * 0.044715

ENGS = ("sync", "tensor", "vector", "scalar", "gpsimd")
ENG_SEM = {"sync": None, "tensor": "PE", "vector": "DV", "scalar": "AC", "gpsimd": "GP"}


class Buf:
    __slots__ = ("name", "writes", "reads")

    def __init__(self, name):
        self.name = name
        self.writes = {}
        self.reads = {}


class Sched:
    def __init__(self):
        self.ops = {e: [] for e in ENGS}
        self.cnt = {}
        self.waited = {e: {} for e in ENGS}

    def _wait(self, eng, sem, val):
        w = self.waited[eng]
        if w.get(sem, 0) >= val:
            return
        w[sem] = val
        self.ops[eng].append(("wait", sem, val))

    def wait_tok(self, eng, tok):
        if tok is not None:
            self._wait(eng, tok[0], tok[1])

    def op(self, eng, fn, reads=(), writes=(), sem=None, inc=None, extra_waits=(),
           scalar_reads=()):
        if sem is None:
            sem = ENG_SEM[eng]
            assert sem is not None, f"op on {eng} needs explicit sem"
            if inc is None:
                inc = 1
        own = ENG_SEM[eng]

        def w_(s_, v, force=False):
            if not force and s_ == own and s_ == "PE":
                return
            self._wait(eng, s_, v)
        for b in reads:
            for s_, v in b.writes.items():
                w_(s_, v)
        for b in scalar_reads:
            for s_, v in b.writes.items():
                w_(s_, v, force=True)
        for b in writes:
            for s_, v in b.writes.items():
                w_(s_, v)
            for s_, v in b.reads.items():
                w_(s_, v)
        for tok in extra_waits:
            self.wait_tok(eng, tok)
        self.cnt[sem] = self.cnt.get(sem, 0) + inc
        tok = (sem, self.cnt[sem])
        self.ops[eng].append(("op", fn, sem, inc))
        for b in tuple(reads) + tuple(scalar_reads):
            b.reads[sem] = tok[1]
        for b in writes:
            b.writes[sem] = tok[1]
            b.reads = {}
        return tok

    def raw_wait(self, eng, sem, val):
        self._wait(eng, sem, val)

    def play(self, name, e, sems):
        for item in self.ops[name]:
            if item[0] == "wait":
                _, sem, val = item
                e.wait_ge(sems[sem], val)
            else:
                _, fn, sem, inc = item
                inst = fn(e)
                if sem is not None:
                    assert inst is not None, f"op on {name} returned None ({sem})"
                    inst.then_inc(sems[sem], inc)


def ceil_div(a, b):
    return -(-a // b)


def build_program(n_seqs=4, n_pairs=NP, nch=16, do_ffn=True, n_ic2=16, dbg=False, seq1_phases='lqao', seq0_phases='lqao'):
    nc = bass.Bass("TRN2", target_bir_lowering=False, detect_race_conditions=False)

    x_hbm = nc.dram_tensor("x", [n_seqs, T, D], F32, kind="ExternalInput").ap()
    winT = nc.dram_tensor("winT", [D, 3072], F32, kind="ExternalInput").ap()
    bin_p = nc.dram_tensor("bin_p", [128, 24], F32, kind="ExternalInput").ap()
    woutT_h = nc.dram_tensor("woutT", [D, D], F32, kind="ExternalInput").ap()
    bout_h = nc.dram_tensor("bout", [1, D], F32, kind="ExternalInput").ap()
    cc_h = nc.dram_tensor("cc", [128, T], BF16, kind="ExternalInput").ap()
    ss_h = nc.dram_tensor("ss", [128, T], BF16, kind="ExternalInput").ap()
    mskb_h = nc.dram_tensor("mskb", [128, 128], F32, kind="ExternalInput").ap()
    ident_h = nc.dram_tensor("ident", [128, 128], BF16, kind="ExternalInput").ap()
    w1_h = nc.dram_tensor("w1", [D, INTER], F32, kind="ExternalInput").ap()
    w2_h = nc.dram_tensor("w2", [INTER, D], F32, kind="ExternalInput").ap()
    ew_h = nc.dram_tensor("ew", [128, 32], F32, kind="ExternalInput").ap()
    win_bf = nc.dram_tensor("win_bf", [24, 128, 1024], BF16, kind="Internal").ap()
    out_hbm = nc.dram_tensor("out", [n_seqs, T, D], F32, kind="ExternalOutput").ap()
    dbg_out = {}
    if dbg:
        for nm, shape, dt in [("d_xln", [128, 8 * T], BF16), ("d_qt", [128, T], BF16),
                              ("d_kt", [128, T], BF16), ("d_vt", [128, 16 * D], BF16),
                              ("d_bias", [128, 256], BF16), ("d_lparts", [128, 8], F32),
                              ("d_lsum", [128, 2], F32), ("d_rl", [128, 2], F32),
                              ("d_pa", [128, T], BF16), ("d_y", [128, 8 * T], BF16)]:
            dbg_out[nm] = nc.dram_tensor(nm, shape, dt, kind="ExternalOutput").ap()
        dbg_out["d_y2"] = nc.dram_tensor("d_y2", [128, 16 * 1024], F32, kind="ExternalOutput").ap()
        dbg_out["d_xln2"] = nc.dram_tensor("d_xln2", [128, 8 * T], BF16, kind="ExternalOutput").ap()

    s = Sched()
    ctx = contextlib.ExitStack()
    sbt = lambda name, shape, dt: ctx.enter_context(nc.sbuf_tensor(name, shape, dt))
    pst = lambda name, shape, dt: ctx.enter_context(nc.psum_tensor(name, shape, dt))

    ccs = sbt("ccs", [128, T], BF16)
    sss = sbt("sss", [128, T], BF16)
    mbias = sbt("mbias", [128, 128], F32)
    ident = sbt("ident_s", [128, 128], BF16)
    bcols = sbt("bcols", [128, 24], F32)
    bout_s = sbt("bout_s", [1, D], BF16)
    ones_r = sbt("ones_r", [1, 128], BF16)
    ones_c = sbt("ones_c", [128, 1], BF16)
    woutT = sbt("woutT_s", [128, 8 * D], BF16)

    XT = sbt("XT", [128, 8 * T], BF16)
    QK = sbt("QK", [128, NP * T + 2 * T], BF16)
    Qt = QK[:, 0:NP * T]
    Kt = QK[:, NP * T:NP * T + 2 * T]
    Vt = sbt("Vt", [128, 16 * D], BF16)

    xbuf = [sbt(f"xbuf{i}", [128, D], F32) for i in range(2)]
    scr0 = sbt("scr0", [128, D], F32)
    scr1 = sbt("scr1", [128, D], F32)
    stat = sbt("stat", [128, 16], F32)
    xln_c = [sbt(f"xln_c{i}", [128, D], BF16) for i in range(2)]

    wbuf = [sbt(f"wbuf{i}", [128, 8 * 128], BF16) for i in range(2)]
    qc_b = sbt("qc_b", [128, 1024], F32)
    qs_b = sbt("qs_b", [128, 1024], F32)
    qw_b = sbt("qw_b", [128, 1024], F32)

    bias_c = sbt("bias_c", [128, 256], BF16)
    lparts = sbt("lparts", [128, 8], F32)
    lsum = sbt("lsum", [128, 2], F32)
    rl = sbt("rl", [128, 2], F32)
    P_A = sbt("P_A", [128, T], BF16)
    P_B = sbt("P_B", [128, T], BF16)
    P_A2 = sbt("P_A2", [128, T], BF16)
    P_B2 = sbt("P_B2", [128, T], BF16)
    PT_A = xln_c[0]
    PT_B = xln_c[1]

    sqx = scr0
    xr = xbuf
    rbuf = [scr0, scr1]
    vfm = sbt("vfm", [128, 1024], BF16)
    sqv = qw_b[:].bitcast(BF16)

    psSA = pst("psSA", [128, 1024], F32)
    psSB = pst("psSB", [128, 1024], F32)
    psTA = pst("psTA", [128, 1024], BF16)
    psTB = pst("psTB", [128, 1024], BF16)
    psY = pst("psY", [128, 512], F32)
    psU = pst("psU", [128, 512], F32)

    B = lambda n: Buf(n)
    b_consts = B("consts")
    b_XT = B("XT")
    b_Qt = [B(f"Qt{t}") for t in range(NP)]
    b_Kt = [B(f"Kt{t}") for t in range(2)]
    b_Vt = [B(f"Vt{t}") for t in range(NP)]
    b_xbuf = [B(f"xbuf{i}") for i in range(2)]
    b_scr0 = B("scr0")
    b_scr1 = B("scr1")
    b_sqx = b_scr0
    b_stat = [B(f"stat{i}") for i in range(2)]
    b_xln = [B(f"xln{i}") for i in range(2)]
    b_wbuf = [B(f"wbuf{i}") for i in range(2)]
    b_winbf = B("winbf")
    b_qc = B("qc"); b_qs = B("qs"); b_qw = B("qw")
    b_qc2 = [B("qc_h0"), B("qc_h1")]
    b_qs2 = [B("qs_h0"), B("qs_h1")]
    b_qw2 = [B("qw_h0"), B("qw_h1")]
    b_lparts = B("lparts"); b_lsum = B("lsum"); b_rl = B("rl")
    bl_P = [[(B("P_A"),), (B("P_B"),)], [(B("P_A2"),), (B("P_B2"),)]]
    b_vfm = B("vfm")
    b_sqv = B("sqv")
    b_PT = [b_xln[0], b_xln[1]]
    bias_all = bias_c
    b_biasP = [B(f"bias{p}") for p in range(NP)]
    b_xr = b_xbuf
    b_rbuf = [b_scr0, b_scr1]
    b_SA = [B("psSA0"), B("psSA1")]
    b_SB = [B("psSB0"), B("psSB1")]
    b_psT = [B("psTA"), B("psTB")]
    b_psY = B("psY")
    b_psU = B("psU")
    b_out0 = B("out0_hbm")

    def dump(nm, ap, bufs):
        if not dbg:
            return
        s.op("sync", lambda e, nm=nm, ap=ap: e.dma_start(out=dbg_out[nm], in_=ap),
             reads=tuple(bufs), sem="ST", inc=16)

    def dump_part(nm, i, ap, bufs):
        if not dbg:
            return
        s.op("sync", lambda e, nm=nm, i=i, ap=ap: e.dma_start(
            out=dbg_out[nm][:, i * 1024:(i + 1) * 1024], in_=ap),
            reads=tuple(bufs), sem="ST", inc=16)

    for dst, src in [(ccs, cc_h), (sss, ss_h), (mbias, mskb_h), (ident, ident_h)]:
        s.op("sync", lambda e, d=dst, so=src: e.dma_start(out=d[:], in_=so),
             writes=(b_consts,), sem="LD", inc=16)
    s.op("sync", lambda e: e.dma_start(out=bcols[:], in_=bin_p),
         writes=(b_consts,), sem="LD", inc=16)
    s.op("gpsimd", lambda e: e.dma_start(out=bout_s[:], in_=bout_h),
         writes=(b_consts,), sem="GD", inc=16)
    s.op("gpsimd", lambda e: e.dma_start(
        out=woutT[:].rearrange("p (t d) -> p t d", t=8),
        in_=woutT_h.rearrange("(t p) d -> p t d", p=128)),
        writes=(b_consts,), sem="GD", inc=16)
    s.op("gpsimd", lambda e: e.memset(ones_r[:], 1.0), writes=(b_consts,))
    s.op("gpsimd", lambda e: e.memset(ones_c[:], 1.0), writes=(b_consts,))
    s.op("gpsimd", lambda e: e.memset(stat[:], 0.0), writes=(b_stat[0], b_stat[1]))
    if dbg:
        s.op("gpsimd", lambda e: e.memset(XT[:], 0.0), writes=(b_XT,))
        s.op("gpsimd", lambda e: e.memset(QK[:], 0.0), writes=tuple(b_Qt + b_Kt))
        s.op("gpsimd", lambda e: e.memset(Vt[:], 0.0), writes=tuple(b_Vt))
        s.op("gpsimd", lambda e: e.memset(P_A[:], 0.0), writes=(bl_P[0][0][0],))
        s.op("gpsimd", lambda e: e.memset(bias_c[:], 0.0), writes=tuple(b_biasP))
        s.op("gpsimd", lambda e: e.memset(lparts[:], 0.0), writes=(b_lparts,))
        s.op("gpsimd", lambda e: e.memset(lsum[:], 0.0), writes=(b_lsum,))
        s.op("gpsimd", lambda e: e.memset(rl[:], 0.0), writes=(b_rl,))

    def phase_ln(src_hbm, dst_XT, b_dst, src_dep=None):
        def ln_s1(tc):
            xb, bx = xbuf[tc % 2], b_xbuf[tc % 2]
            o = (tc % 2) * 8
            bst = b_stat[tc % 2]
            s.op("sync", lambda e, xb=xb, tc=tc: e.dma_start(
                out=xb[:], in_=src_hbm[tc * 128:(tc + 1) * 128, :]),
                reads=(src_dep,) if src_dep is not None else (),
                writes=(bx,), sem="LD", inc=16)
            s.op("vector", lambda e, xb=xb, o=o: e.tensor_reduce(
                stat[:, o:o + 1], xb[:], AX.X, ALU.add), reads=(bx,), writes=(bst,))
            s.op("scalar", lambda e, xb=xb, o=o: e.activation(
                sqx[:], xb[:], AF.Square, accum_out=stat[:, o + 1:o + 2]),
                reads=(bx, bst), writes=(b_sqx, bst))
            s.op("scalar", lambda e, o=o: e.copy(sqx[:, 0:1], stat[:, o + 1:o + 2]),
                 reads=(bst,), writes=(bst, b_sqx))

        def ln_s23(tc):
            xb, bx = xbuf[tc % 2], b_xbuf[tc % 2]
            o = (tc % 2) * 8
            bst = b_stat[tc % 2]
            xlc, bxl = xln_c[tc % 2], b_xln[tc % 2]
            s.op("vector", lambda e, o=o: e.tensor_scalar(
                stat[:, o + 2:o + 3], stat[:, o:o + 1], 1.0 / D, None, ALU.mult),
                reads=(bst,), writes=(bst,))
            s.op("vector", lambda e, o=o: e.tensor_scalar(
                stat[:, o + 3:o + 4], stat[:, o + 2:o + 3], stat[:, o + 2:o + 3],
                1e-5, ALU.mult, ALU.subtract), reads=(), scalar_reads=(bst,),
                writes=(bst,))
            s.op("vector", lambda e, o=o: e.scalar_tensor_tensor(
                stat[:, o + 4:o + 5], stat[:, o + 1:o + 2], 1.0 / D, stat[:, o + 3:o + 4],
                ALU.mult, ALU.subtract), reads=(bst,), writes=(bst,))
            s.op("scalar", lambda e, o=o: e.activation(
                stat[:, o + 5:o + 6], stat[:, o + 4:o + 5], AF.Ln),
                reads=(bst,), writes=(bst,))
            s.op("scalar", lambda e, o=o: e.activation(
                stat[:, o + 6:o + 7], stat[:, o + 5:o + 6], AF.Exp, scale=-0.5),
                reads=(bst,), writes=(bst,))
            s.op("vector", lambda e, o=o: e.tensor_scalar(
                stat[:, o + 7:o + 8], stat[:, o + 2:o + 3], -1.0, None, ALU.mult),
                reads=(bst,), writes=(bst,))
            s.op("vector", lambda e, xb=xb, xlc=xlc, o=o: e.tensor_scalar(
                xlc[:], xb[:], stat[:, o + 7:o + 8], stat[:, o + 6:o + 7],
                ALU.add, ALU.mult), reads=(bx,), scalar_reads=(bst,), writes=(bxl,))
            for dc in range(8):
                s.op("tensor", lambda e, dc=dc, xlc=xlc: e.transpose(
                    psTA[:, dc * 128:(dc + 1) * 128], xlc[:, dc * 128:(dc + 1) * 128],
                    ident[:]), reads=(bxl, b_consts), writes=(b_psT[0],))
            s.op("vector", lambda e, tc=tc: e.tensor_copy(
                dst_XT[:].rearrange("p (dc t) -> p dc t", dc=8)[:, :, tc * 128:(tc + 1) * 128],
                psTA[:].rearrange("p (dc t) -> p dc t", dc=8)),
                reads=(b_psT[0],), writes=(b_dst,))

        ln_s1(0)
        for tc in range(nch):
            if tc + 1 < nch:
                ln_s1(tc + 1)
            ln_s23(tc)

    def phase_qkv_start(si):
        ntc = nch // 4
        fc_order = []
        for t_ in range(n_pairs):
            fc_order += [t_, 8 + t_, 16 + t_]
        nfc = len(fc_order)
        state = {"fc_order": fc_order, "nfc": nfc, "ntc": ntc,
                 "pending": [None]}

        def wload(step):
            fc = fc_order[step]
            wb, bw = wbuf[step % 2], b_wbuf[step % 2]
            if si == 0:
                s.op("gpsimd", lambda e, fc=fc, wb=wb: e.dma_start(
                    out=wb[:].rearrange("p (dc n) -> p dc n", dc=8),
                    in_=winT[:, fc * 128:(fc + 1) * 128].rearrange("(dc p) n -> p dc n", p=128)),
                    writes=(bw,), sem="GD", inc=16)
                s.op("sync", lambda e, fc=fc, wb=wb: e.dma_start(
                    out=win_bf[fc], in_=wb[:]),
                    reads=(bw,), writes=(b_winbf,), sem="ST", inc=16)
            else:
                s.op("gpsimd", lambda e, fc=fc, wb=wb: e.dma_start(
                    out=wb[:], in_=win_bf[fc]),
                    reads=(b_winbf,), writes=(bw,), sem="GD", inc=16)

        state["wload"] = wload
        wload(0)
        if nfc > 1:
            wload(1)
        return state

    def phase_qkv_pair(si, pair_idx, state):
        fc_order, nfc, ntc = state["fc_order"], state["nfc"], state["ntc"]
        wload = state["wload"]
        pending_add = state["pending"]

        def flush_add():
            if pending_add[0] is not None:
                pending_add[0]()
                pending_add[0] = None

        for step in range(pair_idx * 3, pair_idx * 3 + 3):
            fc = fc_order[step]
            wb, bw = wbuf[step % 2], b_wbuf[step % 2]
            is_v = fc >= 16
            pair = fc % 8
            for tc4 in range(ntc):
                bankT, bb = (psSA, b_SA) if tc4 % 2 == 0 else (psSB, b_SB)
                bank, bbank = bankT[:, 0:512], bb[0]
                loc = (tc4 % 2) * 512
                for dc in range(8):
                    s.op("tensor", lambda e, dc=dc, tc4=tc4, wb=wb, bank=bank: e.matmul(
                        bank, wb[:, dc * 128:(dc + 1) * 128],
                        XT[:, dc * T + tc4 * 512: dc * T + (tc4 + 1) * 512],
                        start=(dc == 0), stop=(dc == 7)),
                        reads=(bw, b_XT), writes=(bbank,) if dc in (0, 7) else ())
                if tc4 == ntc - 1 and step + 2 < nfc:
                    wload(step + 2)
                if is_v:
                    s.op("vector", lambda e, fc=fc, bank=bank, loc=loc: e.tensor_scalar(
                        vfm[:, loc:loc + 512], bank,
                        bcols[:, fc:fc + 1], None, ALU.add),
                        reads=(bbank, b_consts), writes=(b_vfm,))
                    for j in range(4):
                        s.op("tensor", lambda e, j=j, loc=loc: e.transpose(
                            psTB[:, loc + j * 128:loc + (j + 1) * 128],
                            vfm[:, loc + j * 128:loc + (j + 1) * 128],
                            ident[:]), reads=(b_vfm, b_consts), writes=(b_psT[1],))
                    s.op("vector", lambda e, tc4=tc4, pair=pair, loc=loc: e.tensor_copy(
                        Vt[:].rearrange("p (tc hv) -> p tc hv", hv=1024)
                          [:, tc4 * 4:(tc4 + 1) * 4, pair * 128:(pair + 1) * 128],
                        psTB[:, loc:loc + 512].rearrange("p (j n) -> p j n", j=4)),
                        reads=(b_psT[1],), writes=(b_Vt[pair],))
                else:
                    flush_add()
                    bqc = b_qc2[tc4 % 2]
                    bqs = b_qs2[tc4 % 2]
                    bqw = b_qw2[tc4 % 2]
                    s.op("vector", lambda e, tc4=tc4, fc=fc, bank=bank, loc=loc: e.scalar_tensor_tensor(
                        qc_b[:, loc:loc + 512], bank,
                        bcols[:, fc:fc + 1], ccs[:, tc4 * 512:(tc4 + 1) * 512],
                        ALU.add, ALU.mult), reads=(bbank, b_consts), writes=(bqc,))
                    s.op("vector", lambda e, tc4=tc4, fc=fc, bank=bank, loc=loc: e.scalar_tensor_tensor(
                        qs_b[:, loc:loc + 512], bank,
                        bcols[:, fc:fc + 1], sss[:, tc4 * 512:(tc4 + 1) * 512],
                        ALU.add, ALU.mult), reads=(bbank, b_consts), writes=(bqs,))
                    for g in range(4):
                        src_g = [1, 0, 3, 2][g]
                        eng_c = "vector" if g < 2 else "gpsimd"
                        s.op(eng_c, lambda e, g=g, src_g=src_g, loc=loc: e.tensor_copy(
                            qw_b[g * 32:(g + 1) * 32, loc:loc + 512],
                            qs_b[src_g * 32:(src_g + 1) * 32, loc:loc + 512]),
                            reads=(bqs,), writes=(bqw,))
                    if fc < 8:
                        dst, bdst, dbase = Qt, b_Qt[pair], pair * T
                    else:
                        dst, bdst, dbase = Kt, b_Kt[pair % 2], (pair % 2) * T
                    eng = "vector" if fc < 8 else "gpsimd"

                    def mk_add(dst=dst, bdst=bdst, dbase=dbase, tc4=tc4, loc=loc,
                               bqc=bqc, bqw=bqw, eng=eng):
                        def do():
                            s.op(eng, lambda e: e.tensor_tensor(
                                dst[:, dbase + tc4 * 512: dbase + tc4 * 512 + 512],
                                qc_b[:, loc:loc + 512], qw_b[:, loc:loc + 512], ALU.add),
                                reads=(bqc, bqw), writes=(bdst,))
                        return do
                    pending_add[0] = mk_add()
            if is_v:
                flush_add()
                nqb = nch
                s.op("scalar", lambda e, pair=pair: e.activation(
                    sqv[:, 0:nch * 128], Qt[:, pair * T: pair * T + nch * 128],
                    AF.Square),
                    reads=(b_Qt[pair],),
                    writes=(b_sqv, b_qw2[0], b_qw2[1]))
                for hh in range(2):
                    bank = psU if hh == 0 else psY
                    bb = b_psU if hh == 0 else b_psY
                    for qb in range(nqb):
                        s.op("tensor", lambda e, qb=qb, hh=hh, bank=bank: e.matmul(
                            bank[:, qb:qb + 1],
                            sqv[64 * hh:64 * hh + 64, qb * 128:qb * 128 + 128],
                            ones_c[64 * hh:64 * hh + 64, :],
                            start=(qb == 0), stop=(qb == nqb - 1)),
                            reads=(b_sqv, b_consts),
                            writes=(bb,) if qb in (0, nqb - 1) else ())
                    s.op("vector", lambda e, hh=hh, bank=bank, bb=bb, pair=pair, nqb=nqb: e.tensor_scalar(
                        bias_all[:, pair * 32 + hh * 16:pair * 32 + hh * 16 + nqb],
                        bank[:, 0:nqb], -0.125, -SHIFT, ALU.mult, ALU.add),
                        reads=(bb,), writes=(b_biasP[pair],))
                flush_add()
        flush_add()
    def phase_attn_pair(si, pair):
        nqb = nch
        if True:
            def emit_S_mm(qb, kc, pair=pair):
                L = (qb + 1) * 128
                w = min(512, L - kc * 512)
                for hh in (1, 0):
                    bankT, bbs = (psSA, b_SA) if hh == 0 else (psSB, b_SB)
                    bbank = bbs[kc % 2]
                    bank = bankT[:, (kc % 2) * 512:(kc % 2) * 512 + w]
                    rows = slice(64 * hh, 64 * hh + 64)
                    s.op("tensor", lambda e, rows=rows, qb=qb, kc=kc, w=w, bank=bank: e.matmul(
                        bank, Qt[rows, pair * T + qb * 128: pair * T + (qb + 1) * 128],
                        Kt[rows, (pair % 2) * T + kc * 512: (pair % 2) * T + kc * 512 + w],
                        start=True, stop=True),
                        reads=(b_Qt[pair], b_Kt[pair % 2]), writes=(bbank,))

            def stage_S(qb, pair=pair):
                nkc = ceil_div((qb + 1) * 128, 512)
                for kc in range(min(2, nkc)):
                    emit_S_mm(qb, kc)

            def softmax_stage(qb, pair=pair):
                L = (qb + 1) * 128
                nkc = ceil_div(L, 512)
                ngr = ceil_div(L, 1024)
                par = qb % 2
                PbA = P_A if par == 0 else P_A2
                PbB = P_B if par == 0 else P_B2
                for g in range(ngr):
                    w = min(1024, L - g * 1024)
                    kc_hi = min(2 * g + 1, nkc - 1)
                    for hh in range(2):
                        bankT, bbs = (psSA, b_SA) if hh == 0 else (psSB, b_SB)
                        bb0, bb1 = bbs[0], bbs[1]
                        if kc_hi == nkc - 1:
                            moff = qb * 128 - g * 1024
                            s.op("vector", lambda e, bankT=bankT, moff=moff: e.tensor_tensor(
                                bankT[:, moff:moff + 128], bankT[:, moff:moff + 128],
                                mbias[:], ALU.add),
                                reads=(bb0, bb1, b_consts),
                                writes=(bb0,) if moff < 512 else (bb1,))
                        s.op("scalar", lambda e, bankT=bankT, hh=hh, g=g, w=w, qb=qb, PbA=PbA, PbB=PbB: e.activation(
                            (PbA if hh == 0 else PbB)[:, g * 1024:g * 1024 + w],
                            bankT[:, 0:w], AF.Exp, scale=0.125,
                            bias=bias_all[:, pair * 32 + hh * 16 + qb:pair * 32 + hh * 16 + qb + 1],
                            accum_out=lparts[:, hh * 4 + g:hh * 4 + g + 1]),
                            reads=(bb0, bb1, b_biasP[pair], b_lparts),
                            writes=bl_P[par][hh] + (b_lparts,))
                    if 2 * g + 2 < nkc:
                        emit_S_mm(qb, 2 * g + 2)
                    if 2 * g + 3 < nkc:
                        emit_S_mm(qb, 2 * g + 3)
                s.op("scalar", lambda e: e.copy(lsum[:, 0:1], lparts[:, 0:1]),
                     reads=(b_lparts,), writes=(b_lparts, b_lsum))
                s.op("vector", lambda e, ngr=ngr: e.tensor_reduce(
                    lsum[:, 0:2], lparts[:].rearrange("p (h k) -> p h k", h=2)[:, :, 0:ngr],
                    AX.X, ALU.add), reads=(b_lparts,), writes=(b_lsum,))
                s.op("vector", lambda e: e.reciprocal(rl[:, 0:2], lsum[:, 0:2]),
                     reads=(b_lsum,), writes=(b_rl,))
                for hh in range(2):
                    Pbuf = PbA if hh == 0 else PbB
                    s.op("vector", lambda e, hh=hh, Pbuf=Pbuf, L=L: e.tensor_scalar(
                        Pbuf[:, 0:L], Pbuf[:, 0:L], rl[:, hh:hh + 1], None, ALU.mult),
                        reads=bl_P[par][hh], scalar_reads=(b_rl,),
                        writes=bl_P[par][hh])

            def pv_stage(qb, pair=pair):
                L = (qb + 1) * 128
                npk = ceil_div(L, 1024)
                par = qb % 2
                PbA = P_A if par == 0 else P_A2
                PbB = P_B if par == 0 else P_B2
                for pk in range(npk):
                    wpk = min(8, L // 128 - pk * 8)
                    for hh in range(2):
                        Pbuf = PbA if hh == 0 else PbB
                        psTx = psTA if hh == 0 else psTB
                        for j in range(wpk):
                            kcb = pk * 8 + j
                            s.op("tensor", lambda e, j=j, kcb=kcb, Pbuf=Pbuf, psTx=psTx: e.transpose(
                                psTx[:, j * 128:(j + 1) * 128],
                                Pbuf[:, kcb * 128:(kcb + 1) * 128], ident[:]),
                                reads=bl_P[par][hh] + (b_consts,), writes=(b_psT[hh],))
                    for hh in range(2):
                        psTx = psTA if hh == 0 else psTB
                        PTs = PT_A if hh == 0 else PT_B
                        s.op("vector", lambda e, psTx=psTx, PTs=PTs, wpk=wpk: e.tensor_copy(
                            PTs[:, 0:wpk * 128], psTx[:, 0:wpk * 128]),
                            reads=(b_psT[hh],), writes=(b_PT[hh],))
                    for hh in range(2):
                        PTs = PT_A if hh == 0 else PT_B
                        ybase = 64 * hh
                        for j in range(wpk):
                            kcb = pk * 8 + j
                            s.op("tensor", lambda e, j=j, kcb=kcb, hh=hh, PTs=PTs, ybase=ybase, L=L: e.matmul(
                                psY[ybase:ybase + 64, 0:128],
                                Vt[:, kcb * 1024 + (2 * pair + hh) * 64: kcb * 1024 + (2 * pair + hh) * 64 + 64],
                                PTs[:, j * 128:(j + 1) * 128],
                                start=(kcb == 0), stop=(kcb == L // 128 - 1),
                                skip_group_check=True),
                                reads=(b_PT[hh], b_Vt[pair]),
                                writes=(b_psY,) if kcb in (0, L // 128 - 1) else ())
                s.op("vector", lambda e, qb=qb: e.tensor_copy(
                    Qt[:, pair * T + qb * 128: pair * T + (qb + 1) * 128], psY[:, 0:128]),
                    reads=(b_psY,), writes=(b_Qt[pair],))

            stage_S(0)
            softmax_stage(0)
            if nqb > 1:
                stage_S(1)
            for qb in range(1, nqb):
                softmax_stage(qb)
                if qb + 1 < nqb:
                    stage_S(qb + 1)
                pv_stage(qb - 1)
            pv_stage(nqb - 1)

    def phase_out(si):
        for tb in range(nch):
            xb, bx = xr[tb % 2], b_xr[tb % 2]
            rb, br = rbuf[tb % 2], b_rbuf[tb % 2]
            s.op("sync", lambda e, xb=xb, si=si, tb=tb: e.dma_start(
                out=xb[:], in_=x_hbm[si, tb * 128:(tb + 1) * 128, :]),
                writes=(bx,), sem="LD", inc=16)
            for dh in range(2):
                bankT, bbs = (psSA, b_SA) if dh == 0 else (psSB, b_SB)
                bank, bbank = bankT[:, 0:512], bbs[0]
                for t_ in range(n_pairs):
                    s.op("tensor", lambda e, t_=t_, tb=tb, dh=dh, bank=bank: e.matmul(
                        bank, Qt[:, t_ * T + tb * 128: t_ * T + tb * 128 + 128],
                        woutT[:, t_ * D + dh * 512: t_ * D + dh * 512 + 512],
                        start=(t_ == 0), stop=False),
                        reads=(b_Qt[t_], b_consts), writes=(bbank,) if t_ == 0 else ())
                s.op("tensor", lambda e, dh=dh, bank=bank: e.matmul(
                    bank, ones_r[0:1, :], bout_s[0:1, dh * 512:dh * 512 + 512],
                    start=False, stop=True), reads=(b_consts,), writes=(bbank,))
                s.op("vector", lambda e, rb=rb, xb=xb, dh=dh, bank=bank: e.tensor_tensor(
                    rb[:, dh * 512:dh * 512 + 512], bank,
                    xb[:, dh * 512:dh * 512 + 512], ALU.add),
                    reads=(bbank, bx), writes=(br,))
            tok = s.op("sync", lambda e, rb=rb, si=si, tb=tb: e.dma_start(
                out=out_hbm[si, tb * 128:(tb + 1) * 128, :], in_=rb[:]),
                reads=(br,), writes=(b_out0,) if si == 0 else (), sem="ST", inc=16)
        return tok

    def phase_ffn():
        ew_ap = qc_b[:, 992:1024]
        sv = stat[:, 0:16]
        w1b = [Vt[:, 0:4096], Vt[:, 4096:8192]]
        w2b = [Vt[:, 8192:12288], Vt[:, 12288:16384]]
        _qtf = Qt[:].bitcast(F32)
        _ktf = Kt[:].bitcast(F32)
        y2c = [_qtf[:, i * 1024:(i + 1) * 1024] for i in range(8)]
        y2c += [_ktf[:, i * 1024:(i + 1) * 1024] for i in range(2)]
        y2c += [P_A[:].bitcast(F32), P_B[:].bitcast(F32),
                P_A2[:].bitcast(F32), P_B2[:].bitcast(F32),
                ccs[:].bitcast(F32), sss[:].bitcast(F32)]
        sq_g = qc_b
        u_g = qs_b
        t_g = qw_b
        h_sb = PT_A[:]
        b_ew = B("ew")
        b_sv = b_PT[1]
        b_w1b = [b_Vt[0], b_Vt[1]]
        b_w2b = [b_Vt[2], b_Vt[3]]
        b_y2 = B("y2")
        b_g = [B(f"g{i}") for i in range(6)]
        b_psH = [b_SA[0], b_SA[1], b_SB[0], b_SB[1]]
        b_psO = [b_psY, b_psU]

        s.op("sync", lambda e: e.dma_start(out=ew_ap, in_=ew_h),
            writes=(b_ew, b_qc2[0], b_qc2[1]), sem="LD", inc=16)
        _ties = ([b_Qt[i] for i in range(8)] + b_Kt +
                 [bl_P[0][0][0], bl_P[0][1][0], bl_P[1][0][0], bl_P[1][1][0],
                  b_consts, b_consts])
        for _mi in range(16):
            s.op("gpsimd", lambda e, _mi=_mi: e.memset(y2c[_mi][:], 0.0),
                 writes=(b_y2, _ties[_mi]))
        import os as _os
        _stage = int(_os.environ.get("FFN_STAGE", "4"))
        if _stage <= 1:
            return

        phase_ln(out_hbm[0], XT, b_XT, src_dep=b_out0)
        dump("d_xln2", XT[:], [b_XT])
        if _stage <= 2:
            return
        s.op("vector", lambda e: e.tensor_tensor(
            sv, ew_ap.rearrange("p (c k) -> p c k", k=2)[:, :, 0],
            ew_ap.rearrange("p (c k) -> p c k", k=2)[:, :, 1], ALU.add),
            reads=(b_ew, b_qc2[0], b_qc2[1]),
            writes=(b_sv, b_stat[0], b_stat[1]))

        def ffn_mm1(t):
            ic2, tw = divmod(t, nch // 2)
            wb1, bw1 = w1b[ic2 % 2], b_w1b[ic2 % 2]
            if tw == 0:
                s.op("gpsimd", lambda e, ic2=ic2, wb1=wb1: e.dma_start(
                    out=wb1.rearrange("p (dc i) -> p dc i", dc=8),
                    in_=w1_h[:, ic2 * 512:(ic2 + 1) * 512].rearrange("(dc p) i -> p dc i", p=128)),
                    writes=(bw1,), sem="GD", inc=16)
                wb2, bw2 = w2b[ic2 % 2], b_w2b[ic2 % 2]
                s.op("gpsimd", lambda e, ic2=ic2, wb2=wb2: e.dma_start(
                    out=wb2.rearrange("p (i4 d) -> p i4 d", i4=4),
                    in_=w2_h[ic2 * 512:(ic2 + 1) * 512, :].rearrange("(i4 p) d -> p i4 d", p=128)),
                    writes=(bw2,), sem="GD", inc=16)
            for icsub in range(4):
                bankT = (psSA, psSA, psSB, psSB)[icsub]
                bank = bankT[:, (icsub % 2) * 512:(icsub % 2) * 512 + 256]
                for dc in range(8):
                    s.op("tensor", lambda e, icsub=icsub, dc=dc, tw=tw, wb1=wb1, bank=bank: e.matmul(
                        bank, wb1[:, dc * 512 + icsub * 128: dc * 512 + (icsub + 1) * 128],
                        XT[:, dc * T + tw * 256:dc * T + tw * 256 + 256],
                        start=(dc == 0), stop=(dc == 7)),
                        reads=(bw1, b_XT), writes=(b_psH[icsub],) if dc in (0, 7) else ())

        def ffn_gelu(t):
            h_sb = (PT_A if t % 2 == 0 else PT_B)[:]
            bh = b_PT[t % 2]
            for half, bankT, rd in ((0, psSA, b_psH[0:2]), (1, psSB, b_psH[2:4])):
                hv = bankT[:].rearrange("p (i w) -> p i w", i=2)[:, :, 0:256]
                s.op("scalar", lambda e, hv=hv, half=half: e.activation(
                    sq_g[:, half * 512:half * 512 + 512].rearrange("p (i w) -> p i w", i=2),
                    hv, AF.Square), reads=tuple(rd), writes=(b_g[0], b_qc2[0], b_qc2[1]))
            s.op("vector", lambda e: e.tensor_scalar(
                sq_g[:], sq_g[:], GC3, GC1, ALU.mult, ALU.add),
                reads=(b_g[0],), writes=(b_g[1],))
            for half, bankT, rd in ((0, psSA, b_psH[0:2]), (1, psSB, b_psH[2:4])):
                hv = bankT[:].rearrange("p (i w) -> p i w", i=2)[:, :, 0:256]
                s.op("vector", lambda e, hv=hv, half=half: e.tensor_tensor(
                    u_g[:, half * 512:half * 512 + 512].rearrange("p (i w) -> p i w", i=2),
                    sq_g[:, half * 512:half * 512 + 512].rearrange("p (i w) -> p i w", i=2),
                    hv, ALU.mult),
                    reads=(b_g[1],) + tuple(rd), writes=(b_g[2], b_qs2[0], b_qs2[1]))
            s.op("scalar", lambda e: e.activation(t_g[:], u_g[:], AF.Tanh),
                reads=(b_g[2],), writes=(b_g[3], b_qw2[0], b_qw2[1], b_sqv))
            s.op("vector", lambda e: e.tensor_scalar(
                t_g[:], t_g[:], 0.5, 0.5, ALU.mult, ALU.add),
                reads=(b_g[3],), writes=(b_g[4],))
            for half, bankT, rd in ((0, psSA, b_psH[0:2]), (1, psSB, b_psH[2:4])):
                hv = bankT[:].rearrange("p (i w) -> p i w", i=2)[:, :, 0:256]
                s.op("vector", lambda e, hv=hv, half=half, h_sb=h_sb: e.tensor_tensor(
                    h_sb[:, half * 512:half * 512 + 512].rearrange("p (i w) -> p i w", i=2),
                    t_g[:, half * 512:half * 512 + 512].rearrange("p (i w) -> p i w", i=2),
                    hv, ALU.mult),
                    reads=(b_g[4],) + tuple(rd), writes=(bh,))

        def ffn_mm2(t):
            ic2, tw = divmod(t, nch // 2)
            wb2, bw2 = w2b[ic2 % 2], b_w2b[ic2 % 2]
            h_sb = (PT_A if t % 2 == 0 else PT_B)[:]
            bh = b_PT[t % 2]
            for dh in range(2):
                for tb2 in range(2):
                    psO = psY if tb2 == 0 else psU
                    bO = b_psO[tb2]
                    for icsub in range(4):
                        s.op("tensor", lambda e, icsub=icsub, dh=dh, tb2=tb2, psO=psO, wb2=wb2, h_sb=h_sb: e.matmul(
                            psO[:, 0:512],
                            h_sb[:, icsub * 256 + tb2 * 128: icsub * 256 + tb2 * 128 + 128],
                            wb2[:, icsub * 1024 + dh * 512: icsub * 1024 + dh * 512 + 512],
                            start=(icsub == 0), stop=(icsub == 3)),
                            reads=(bh, bw2), writes=(bO,) if icsub in (0, 3) else ())
                    s.op("vector", lambda e, tw=tw, dh=dh, tb2=tb2, psO=psO: e.tensor_tensor(
                        y2c[tw * 2 + tb2][:, dh * 512:dh * 512 + 512],
                        y2c[tw * 2 + tb2][:, dh * 512:dh * 512 + 512],
                        psO[:, 0:512], ALU.add),
                        reads=(b_psO[tb2], b_y2), writes=(b_y2,))

        n_iter = n_ic2 * (nch // 2)
        ffn_mm1(0)
        ffn_gelu(0)
        for t in range(1, n_iter + 1):
            if t < n_iter:
                ffn_mm1(t)
            ffn_mm2(t - 1)
            if t < n_iter:
                ffn_gelu(t)
        if dbg:
            for _mi in range(16):
                dump_part("d_y2", _mi, y2c[_mi], [b_y2])
        for tb in range(nch):
            xb, bx = xr[tb % 2], b_xr[tb % 2]
            rb, br = rbuf[tb % 2], b_rbuf[tb % 2]
            s.op("sync", lambda e, xb=xb, tb=tb: e.dma_start(
                out=xb[:], in_=out_hbm[0, tb * 128:(tb + 1) * 128, :]),
                reads=(b_out0,), writes=(bx,), sem="LD", inc=16)
            s.op("vector", lambda e, rb=rb, xb=xb, tb=tb: e.scalar_tensor_tensor(
                rb[:], y2c[tb][:], sv[:, tb:tb + 1], xb[:],
                ALU.mult, ALU.add), reads=(b_y2, bx), scalar_reads=(b_sv,),
                writes=(br,))
            s.op("sync", lambda e, rb=rb, tb=tb: e.dma_start(
                out=out_hbm[0, tb * 128:(tb + 1) * 128, :], in_=rb[:]),
                reads=(br,), writes=(b_out0,), sem="ST", inc=16)

    for si in range(n_seqs):
        ph = seq1_phases if si > 0 else seq0_phases
        if 'l' not in ph:
            continue
        phase_ln(x_hbm[si], XT, b_XT)
        if 'q' not in ph:
            continue
        if 'a' in ph:
            qkv_state = phase_qkv_start(si)
            for pair in range(n_pairs):
                phase_qkv_pair(si, pair, qkv_state)
                if pair >= 1:
                    phase_attn_pair(si, pair - 1)
            phase_attn_pair(si, n_pairs - 1)
        else:
            qkv_state = phase_qkv_start(si)
            for pair in range(n_pairs):
                phase_qkv_pair(si, pair, qkv_state)
        if 'a' not in ph or 'o' not in ph:
            continue
        phase_out(si)
    if do_ffn:
        phase_ffn()

    if "ST" in s.cnt:
        s.raw_wait("sync", "ST", s.cnt["ST"])

    sems = {}
    for name in ["LD", "SW", "ST", "GD", "PE", "DV", "AC", "GP"]:
        sems[name] = ctx.enter_context(nc.semaphore(name))
    with nc.Block() as blk:
        @blk.sync
        def _(e):
            s.play("sync", e, sems)

        @blk.tensor
        def _(e):
            s.play("tensor", e, sems)

        @blk.vector
        def _(e):
            s.play("vector", e, sems)

        @blk.scalar
        def _(e):
            s.play("scalar", e, sems)

        @blk.gpsimd
        def _(e):
            s.play("gpsimd", e, sems)

    ctx.close()
    return nc



def _prep_consts():
    half = 32
    p = np.arange(half, dtype=np.float64) / (half - 1)
    thetas = 1.0 * (10000.0 / 1.0) ** p
    pos = np.arange(T, dtype=np.float64)
    ang = pos[None, :] / thetas[:, None]
    cos32 = np.cos(ang).astype(np.float32)
    sin32 = np.sin(ang).astype(np.float32)
    CC = np.tile(cos32, (4, 1)).astype(ml_dtypes.bfloat16)
    SS = np.concatenate([-sin32, sin32, -sin32, sin32], axis=0).astype(ml_dtypes.bfloat16)
    i = np.arange(128)
    M = np.where(i[None, :] <= i[:, None], 0.0, MASKVAL).astype(np.float32)
    I = np.eye(128, dtype=np.float32).astype(ml_dtypes.bfloat16)
    return CC, SS, M, I


def _perm3072():
    perm = []
    for sec in range(2):
        base = sec * 1024
        for h in range(NH):
            perm += list(range(base + h * 64 + 32, base + h * 64 + 64))
            perm += list(range(base + h * 64, base + h * 64 + 32))
    perm += list(range(2048, 3072))
    return np.array(perm)


def make_in_maps(x, W_in, b_in, W_out, b_out_, ew, w1, w2, n_cores=8):
    perm = _perm3072()
    winT = np.ascontiguousarray(W_in[perm].T.astype(np.float32))
    bin_p = np.ascontiguousarray(b_in[perm].astype(np.float32).reshape(24, 128).T)
    woutT = np.ascontiguousarray(W_out.T.astype(np.float32))
    bout = np.ascontiguousarray(b_out_.astype(np.float32)).reshape(1, D)
    CC, SS, M, I = _prep_consts()
    in_maps = []
    for c in range(n_cores):
        batches = [c, c + 8, c + 16, c + 24]
        ewc = (ew[c * T:(c + 1) * T].astype(np.float32) if c < 2
               else np.zeros((T, 2), np.float32))
        ewc = np.ascontiguousarray(
            ewc.reshape(16, 128, 2).transpose(1, 0, 2).reshape(128, 32))
        in_maps.append({
            "x": np.ascontiguousarray(x[batches].astype(np.float32)),
            "winT": winT, "bin_p": bin_p, "woutT": woutT, "bout": bout,
            "cc": CC, "ss": SS, "mskb": M, "ident": I,
            "w1": np.ascontiguousarray(w1.astype(np.float32)),
            "w2": np.ascontiguousarray(w2.astype(np.float32)),
            "ew": np.ascontiguousarray(ewc),
        })
    return in_maps


def kernel(L0_attn_x, L0_attn_W_in, L0_attn_b_in, L0_attn_W_out, L0_attn_b_out,
           L0_ffn_scores, L0_ffn_expert_weights, L0_ffn_top_experts,
           L0_ffn_w1, L0_ffn_w2):
    x = np.asarray(L0_attn_x)
    in_maps = make_in_maps(x, np.asarray(L0_attn_W_in), np.asarray(L0_attn_b_in),
                           np.asarray(L0_attn_W_out), np.asarray(L0_attn_b_out),
                           np.asarray(L0_ffn_expert_weights),
                           np.asarray(L0_ffn_w1), np.asarray(L0_ffn_w2))
    nc = build_program()
    res = run_bass_kernel_spmd(nc, in_maps, core_ids=list(range(8)))
    out = np.empty((32, T, D), np.float32)
    for c in range(8):
        out[[c, c + 8, c + 16, c + 24]] = res.results[c]["out"]
    return out


# version 19
# speedup vs baseline: 1.2203x; 1.0107x over previous
import contextlib
import numpy as np
import ml_dtypes
import concourse.bass as bass
import concourse.mybir as mybir
from concourse.bass_utils import run_bass_kernel_spmd

F32 = mybir.dt.float32
BF16 = mybir.dt.bfloat16
AF = mybir.ActivationFunctionType
ALU = mybir.AluOpType
AX = mybir.AxisListType

T = 2048
D = 1024
NH = 16
NP = NH // 2
INTER = 8192
SHIFT = 20.0
MASKVAL = -30000.0
GC1 = 0.7978845608028654
GC3 = GC1 * 0.044715

ENGS = ("sync", "tensor", "vector", "scalar", "gpsimd")
ENG_SEM = {"sync": None, "tensor": "PE", "vector": "DV", "scalar": "AC", "gpsimd": "GP"}


class Buf:
    __slots__ = ("name", "writes", "reads")

    def __init__(self, name):
        self.name = name
        self.writes = {}
        self.reads = {}


class Sched:
    def __init__(self):
        self.ops = {e: [] for e in ENGS}
        self.cnt = {}
        self.waited = {e: {} for e in ENGS}

    def _wait(self, eng, sem, val):
        w = self.waited[eng]
        if w.get(sem, 0) >= val:
            return
        w[sem] = val
        self.ops[eng].append(("wait", sem, val))

    def wait_tok(self, eng, tok):
        if tok is not None:
            self._wait(eng, tok[0], tok[1])

    def op(self, eng, fn, reads=(), writes=(), sem=None, inc=None, extra_waits=(),
           scalar_reads=()):
        if sem is None:
            sem = ENG_SEM[eng]
            assert sem is not None, f"op on {eng} needs explicit sem"
            if inc is None:
                inc = 1
        own = ENG_SEM[eng]

        def w_(s_, v, force=False):
            if not force and s_ == own and s_ == "PE":
                return
            self._wait(eng, s_, v)
        for b in reads:
            for s_, v in b.writes.items():
                w_(s_, v)
        for b in scalar_reads:
            for s_, v in b.writes.items():
                w_(s_, v, force=True)
        for b in writes:
            for s_, v in b.writes.items():
                w_(s_, v)
            for s_, v in b.reads.items():
                w_(s_, v)
        for tok in extra_waits:
            self.wait_tok(eng, tok)
        self.cnt[sem] = self.cnt.get(sem, 0) + inc
        tok = (sem, self.cnt[sem])
        self.ops[eng].append(("op", fn, sem, inc))
        for b in tuple(reads) + tuple(scalar_reads):
            b.reads[sem] = tok[1]
        for b in writes:
            b.writes[sem] = tok[1]
            b.reads = {}
        return tok

    def raw_wait(self, eng, sem, val):
        self._wait(eng, sem, val)

    def play(self, name, e, sems):
        for item in self.ops[name]:
            if item[0] == "wait":
                _, sem, val = item
                e.wait_ge(sems[sem], val)
            else:
                _, fn, sem, inc = item
                inst = fn(e)
                if sem is not None:
                    assert inst is not None, f"op on {name} returned None ({sem})"
                    inst.then_inc(sems[sem], inc)


def ceil_div(a, b):
    return -(-a // b)


def build_program(n_seqs=4, n_pairs=NP, nch=16, do_ffn=True, n_ic2=16, dbg=False, seq1_phases='lqao', seq0_phases='lqao'):
    nc = bass.Bass("TRN2", target_bir_lowering=False, detect_race_conditions=False)

    x_hbm = nc.dram_tensor("x", [n_seqs, T, D], F32, kind="ExternalInput").ap()
    winT = nc.dram_tensor("winT", [D, 3072], F32, kind="ExternalInput").ap()
    bin_p = nc.dram_tensor("bin_p", [128, 24], F32, kind="ExternalInput").ap()
    woutT_h = nc.dram_tensor("woutT", [D, D], F32, kind="ExternalInput").ap()
    bout_h = nc.dram_tensor("bout", [1, D], F32, kind="ExternalInput").ap()
    cc_h = nc.dram_tensor("cc", [128, T], BF16, kind="ExternalInput").ap()
    ss_h = nc.dram_tensor("ss", [128, T], BF16, kind="ExternalInput").ap()
    mskb_h = nc.dram_tensor("mskb", [128, 128], F32, kind="ExternalInput").ap()
    ident_h = nc.dram_tensor("ident", [128, 128], BF16, kind="ExternalInput").ap()
    w1_h = nc.dram_tensor("w1", [D, INTER], F32, kind="ExternalInput").ap()
    w2_h = nc.dram_tensor("w2", [INTER, D], F32, kind="ExternalInput").ap()
    ew_h = nc.dram_tensor("ew", [128, 32], F32, kind="ExternalInput").ap()
    win_bf = nc.dram_tensor("win_bf", [24, 128, 1024], BF16, kind="Internal").ap()
    out_hbm = nc.dram_tensor("out", [n_seqs, T, D], F32, kind="ExternalOutput").ap()
    dbg_out = {}
    if dbg:
        for nm, shape, dt in [("d_xln", [128, 8 * T], BF16), ("d_qt", [128, T], BF16),
                              ("d_kt", [128, T], BF16), ("d_vt", [128, 16 * D], BF16),
                              ("d_bias", [128, 256], BF16), ("d_lparts", [128, 8], F32),
                              ("d_lsum", [128, 2], F32), ("d_rl", [128, 2], F32),
                              ("d_pa", [128, T], BF16), ("d_y", [128, 8 * T], BF16)]:
            dbg_out[nm] = nc.dram_tensor(nm, shape, dt, kind="ExternalOutput").ap()
        dbg_out["d_y2"] = nc.dram_tensor("d_y2", [128, 16 * 1024], F32, kind="ExternalOutput").ap()
        dbg_out["d_xln2"] = nc.dram_tensor("d_xln2", [128, 8 * T], BF16, kind="ExternalOutput").ap()

    s = Sched()
    ctx = contextlib.ExitStack()
    sbt = lambda name, shape, dt: ctx.enter_context(nc.sbuf_tensor(name, shape, dt))
    pst = lambda name, shape, dt: ctx.enter_context(nc.psum_tensor(name, shape, dt))

    ccs = sbt("ccs", [128, T], BF16)
    sss = sbt("sss", [128, T], BF16)
    mbias = sbt("mbias", [128, 128], F32)
    ident = sbt("ident_s", [128, 128], BF16)
    bcols = sbt("bcols", [128, 24], F32)
    bout_s = sbt("bout_s", [1, D], BF16)
    ones_r = sbt("ones_r", [1, 128], BF16)
    ones_c = sbt("ones_c", [128, 1], BF16)
    woutT = sbt("woutT_s", [128, 8 * D], BF16)

    XT = sbt("XT", [128, 8 * T], BF16)
    QK = sbt("QK", [128, NP * T + 2 * T], BF16)
    Qt = QK[:, 0:NP * T]
    Kt = QK[:, NP * T:NP * T + 2 * T]
    Vt = sbt("Vt", [128, 16 * D], BF16)

    xbuf = [sbt(f"xbuf{i}", [128, D], F32) for i in range(2)]
    scr0 = sbt("scr0", [128, D], F32)
    scr1 = sbt("scr1", [128, D], F32)
    stat = sbt("stat", [128, 16], F32)
    xln_c = [sbt(f"xln_c{i}", [128, D], BF16) for i in range(2)]

    wbuf = [sbt(f"wbuf{i}", [128, 8 * 128], BF16) for i in range(2)]
    qc_b = sbt("qc_b", [128, 1024], F32)
    qs_b = sbt("qs_b", [128, 1024], F32)
    qw_b = sbt("qw_b", [128, 1024], F32)

    bias_c = sbt("bias_c", [128, 256], BF16)
    lparts = sbt("lparts", [128, 8], F32)
    lsum = sbt("lsum", [128, 2], F32)
    rl = sbt("rl", [128, 2], F32)
    P_A = sbt("P_A", [128, T], BF16)
    P_B = sbt("P_B", [128, T], BF16)
    P_A2 = sbt("P_A2", [128, T], BF16)
    P_B2 = sbt("P_B2", [128, T], BF16)
    PT_A = xln_c[0]
    PT_B = xln_c[1]

    sqx = scr0
    xr = xbuf
    rbuf = [scr0, scr1]
    vfm = sbt("vfm", [128, T], BF16)
    sqv = qw_b[:].bitcast(BF16)

    psSA = pst("psSA", [128, 1024], F32)
    psSB = pst("psSB", [128, 1024], F32)
    psTA = pst("psTA", [128, 1024], BF16)
    psTB = pst("psTB", [128, 1024], BF16)
    psY = pst("psY", [128, 512], F32)
    psU = pst("psU", [128, 512], F32)

    B = lambda n: Buf(n)
    b_consts = B("consts")
    b_XT = B("XT")
    b_Qt = [B(f"Qt{t}") for t in range(NP)]
    b_Kt = [B(f"Kt{t}") for t in range(2)]
    b_Vt = [B(f"Vt{t}") for t in range(NP)]
    b_xbuf = [B(f"xbuf{i}") for i in range(2)]
    b_scr0 = B("scr0")
    b_scr1 = B("scr1")
    b_sqx = b_scr0
    b_stat = [B(f"stat{i}") for i in range(2)]
    b_xln = [B(f"xln{i}") for i in range(2)]
    b_wbuf = [B(f"wbuf{i}") for i in range(2)]
    b_winbf = B("winbf")
    b_qc = B("qc"); b_qs = B("qs"); b_qw = B("qw")
    b_qc2 = [B("qc_h0"), B("qc_h1")]
    b_qs2 = [B("qs_h0"), B("qs_h1")]
    b_qw2 = [B("qw_h0"), B("qw_h1")]
    b_lparts = B("lparts"); b_lsum = B("lsum"); b_rl = B("rl")
    bl_P = [[(B("P_A"),), (B("P_B"),)], [(B("P_A2"),), (B("P_B2"),)]]
    b_vfm = B("vfm")
    b_sqv = B("sqv")
    b_PT = [b_xln[0], b_xln[1]]
    bias_all = bias_c
    b_biasP = [B(f"bias{p}") for p in range(NP)]
    b_xr = b_xbuf
    b_rbuf = [b_scr0, b_scr1]
    b_SA = [B("psSA0"), B("psSA1")]
    b_SB = [B("psSB0"), B("psSB1")]
    b_psT = [B("psTA"), B("psTB")]
    b_psY = B("psY")
    b_psU = B("psU")
    b_out0 = B("out0_hbm")

    def dump(nm, ap, bufs):
        if not dbg:
            return
        s.op("sync", lambda e, nm=nm, ap=ap: e.dma_start(out=dbg_out[nm], in_=ap),
             reads=tuple(bufs), sem="ST", inc=16)

    def dump_part(nm, i, ap, bufs):
        if not dbg:
            return
        s.op("sync", lambda e, nm=nm, i=i, ap=ap: e.dma_start(
            out=dbg_out[nm][:, i * 1024:(i + 1) * 1024], in_=ap),
            reads=tuple(bufs), sem="ST", inc=16)

    for dst, src in [(ccs, cc_h), (sss, ss_h), (mbias, mskb_h), (ident, ident_h)]:
        s.op("sync", lambda e, d=dst, so=src: e.dma_start(out=d[:], in_=so),
             writes=(b_consts,), sem="LD", inc=16)
    s.op("sync", lambda e: e.dma_start(out=bcols[:], in_=bin_p),
         writes=(b_consts,), sem="LD", inc=16)
    s.op("gpsimd", lambda e: e.dma_start(out=bout_s[:], in_=bout_h),
         writes=(b_consts,), sem="GD", inc=16)
    s.op("gpsimd", lambda e: e.dma_start(
        out=woutT[:].rearrange("p (t d) -> p t d", t=8),
        in_=woutT_h.rearrange("(t p) d -> p t d", p=128)),
        writes=(b_consts,), sem="GD", inc=16)
    s.op("gpsimd", lambda e: e.memset(ones_r[:], 1.0), writes=(b_consts,))
    s.op("gpsimd", lambda e: e.memset(ones_c[:], 1.0), writes=(b_consts,))
    s.op("gpsimd", lambda e: e.memset(stat[:], 0.0), writes=(b_stat[0], b_stat[1]))
    if dbg:
        s.op("gpsimd", lambda e: e.memset(XT[:], 0.0), writes=(b_XT,))
        s.op("gpsimd", lambda e: e.memset(QK[:], 0.0), writes=tuple(b_Qt + b_Kt))
        s.op("gpsimd", lambda e: e.memset(Vt[:], 0.0), writes=tuple(b_Vt))
        s.op("gpsimd", lambda e: e.memset(P_A[:], 0.0), writes=(bl_P[0][0][0],))
        s.op("gpsimd", lambda e: e.memset(bias_c[:], 0.0), writes=tuple(b_biasP))
        s.op("gpsimd", lambda e: e.memset(lparts[:], 0.0), writes=(b_lparts,))
        s.op("gpsimd", lambda e: e.memset(lsum[:], 0.0), writes=(b_lsum,))
        s.op("gpsimd", lambda e: e.memset(rl[:], 0.0), writes=(b_rl,))

    def phase_ln(src_hbm, dst_XT, b_dst, src_dep=None):
        def ln_s1(tc):
            xb, bx = xbuf[tc % 2], b_xbuf[tc % 2]
            o = (tc % 2) * 8
            bst = b_stat[tc % 2]
            s.op("sync", lambda e, xb=xb, tc=tc: e.dma_start(
                out=xb[:], in_=src_hbm[tc * 128:(tc + 1) * 128, :]),
                reads=(src_dep,) if src_dep is not None else (),
                writes=(bx,), sem="LD", inc=16)
            s.op("vector", lambda e, xb=xb, o=o: e.tensor_reduce(
                stat[:, o:o + 1], xb[:], AX.X, ALU.add), reads=(bx,), writes=(bst,))
            s.op("scalar", lambda e, xb=xb, o=o: e.activation(
                sqx[:], xb[:], AF.Square, accum_out=stat[:, o + 1:o + 2]),
                reads=(bx, bst), writes=(b_sqx, bst))
            s.op("scalar", lambda e, o=o: e.copy(sqx[:, 0:1], stat[:, o + 1:o + 2]),
                 reads=(bst,), writes=(bst, b_sqx))

        def ln_s23(tc):
            xb, bx = xbuf[tc % 2], b_xbuf[tc % 2]
            o = (tc % 2) * 8
            bst = b_stat[tc % 2]
            xlc, bxl = xln_c[tc % 2], b_xln[tc % 2]
            s.op("vector", lambda e, o=o: e.tensor_scalar(
                stat[:, o + 2:o + 3], stat[:, o:o + 1], 1.0 / D, None, ALU.mult),
                reads=(bst,), writes=(bst,))
            s.op("vector", lambda e, o=o: e.tensor_scalar(
                stat[:, o + 3:o + 4], stat[:, o + 2:o + 3], stat[:, o + 2:o + 3],
                1e-5, ALU.mult, ALU.subtract), reads=(), scalar_reads=(bst,),
                writes=(bst,))
            s.op("vector", lambda e, o=o: e.scalar_tensor_tensor(
                stat[:, o + 4:o + 5], stat[:, o + 1:o + 2], 1.0 / D, stat[:, o + 3:o + 4],
                ALU.mult, ALU.subtract), reads=(bst,), writes=(bst,))
            s.op("scalar", lambda e, o=o: e.activation(
                stat[:, o + 5:o + 6], stat[:, o + 4:o + 5], AF.Ln),
                reads=(bst,), writes=(bst,))
            s.op("scalar", lambda e, o=o: e.activation(
                stat[:, o + 6:o + 7], stat[:, o + 5:o + 6], AF.Exp, scale=-0.5),
                reads=(bst,), writes=(bst,))
            s.op("vector", lambda e, o=o: e.tensor_scalar(
                stat[:, o + 7:o + 8], stat[:, o + 2:o + 3], -1.0, None, ALU.mult),
                reads=(bst,), writes=(bst,))
            s.op("vector", lambda e, xb=xb, xlc=xlc, o=o: e.tensor_scalar(
                xlc[:], xb[:], stat[:, o + 7:o + 8], stat[:, o + 6:o + 7],
                ALU.add, ALU.mult), reads=(bx,), scalar_reads=(bst,), writes=(bxl,))
            for dc in range(8):
                s.op("tensor", lambda e, dc=dc, xlc=xlc: e.transpose(
                    psTA[:, dc * 128:(dc + 1) * 128], xlc[:, dc * 128:(dc + 1) * 128],
                    ident[:]), reads=(bxl, b_consts), writes=(b_psT[0],))
            s.op("vector", lambda e, tc=tc: e.tensor_copy(
                dst_XT[:].rearrange("p (dc t) -> p dc t", dc=8)[:, :, tc * 128:(tc + 1) * 128],
                psTA[:].rearrange("p (dc t) -> p dc t", dc=8)),
                reads=(b_psT[0],), writes=(b_dst,))

        ln_s1(0)
        for tc in range(nch):
            if tc + 1 < nch:
                ln_s1(tc + 1)
            ln_s23(tc)

    def phase_qkv_start(si):
        ntc = nch // 4
        fc_order = []
        for t_ in range(n_pairs):
            fc_order += [t_, 8 + t_, 16 + t_]
        nfc = len(fc_order)
        state = {"fc_order": fc_order, "nfc": nfc, "ntc": ntc,
                 "pending": [None]}

        def wload(step):
            fc = fc_order[step]
            wb, bw = wbuf[step % 2], b_wbuf[step % 2]
            if si == 0:
                s.op("gpsimd", lambda e, fc=fc, wb=wb: e.dma_start(
                    out=wb[:].rearrange("p (dc n) -> p dc n", dc=8),
                    in_=winT[:, fc * 128:(fc + 1) * 128].rearrange("(dc p) n -> p dc n", p=128)),
                    writes=(bw,), sem="GD", inc=16)
                s.op("sync", lambda e, fc=fc, wb=wb: e.dma_start(
                    out=win_bf[fc], in_=wb[:]),
                    reads=(bw,), writes=(b_winbf,), sem="ST", inc=16)
            else:
                s.op("gpsimd", lambda e, fc=fc, wb=wb: e.dma_start(
                    out=wb[:], in_=win_bf[fc]),
                    reads=(b_winbf,), writes=(bw,), sem="GD", inc=16)

        state["wload"] = wload
        wload(0)
        if nfc > 1:
            wload(1)
        return state

    def phase_qkv_pair(si, pair_idx, state):
        fc_order, nfc, ntc = state["fc_order"], state["nfc"], state["ntc"]
        wload = state["wload"]
        pending_add = state["pending"]

        def flush_add():
            if pending_add[0] is not None:
                pending_add[0]()
                pending_add[0] = None

        for step in range(pair_idx * 3, pair_idx * 3 + 3):
            fc = fc_order[step]
            wb, bw = wbuf[step % 2], b_wbuf[step % 2]
            is_v = fc >= 16
            pair = fc % 8
            for tc4 in range(ntc):
                bankT, bb = (psSA, b_SA) if tc4 % 2 == 0 else (psSB, b_SB)
                bank, bbank = bankT[:, 0:512], bb[0]
                loc = (tc4 % 2) * 512
                for dc in range(8):
                    s.op("tensor", lambda e, dc=dc, tc4=tc4, wb=wb, bank=bank: e.matmul(
                        bank, wb[:, dc * 128:(dc + 1) * 128],
                        XT[:, dc * T + tc4 * 512: dc * T + (tc4 + 1) * 512],
                        start=(dc == 0), stop=(dc == 7)),
                        reads=(bw, b_XT), writes=(bbank,) if dc in (0, 7) else ())
                if tc4 == ntc - 1 and step + 2 < nfc:
                    wload(step + 2)
                if is_v:
                    s.op("vector", lambda e, tc4=tc4, fc=fc, bank=bank: e.tensor_scalar(
                        vfm[:, tc4 * 512:(tc4 + 1) * 512], bank,
                        bcols[:, fc:fc + 1], None, ALU.add),
                        reads=(bbank, b_consts), writes=(b_vfm,))
                else:
                    flush_add()
                    bqc = b_qc2[tc4 % 2]
                    bqs = b_qs2[tc4 % 2]
                    bqw = b_qw2[tc4 % 2]
                    s.op("vector", lambda e, tc4=tc4, fc=fc, bank=bank, loc=loc: e.scalar_tensor_tensor(
                        qc_b[:, loc:loc + 512], bank,
                        bcols[:, fc:fc + 1], ccs[:, tc4 * 512:(tc4 + 1) * 512],
                        ALU.add, ALU.mult), reads=(bbank, b_consts), writes=(bqc,))
                    s.op("vector", lambda e, tc4=tc4, fc=fc, bank=bank, loc=loc: e.scalar_tensor_tensor(
                        qs_b[:, loc:loc + 512], bank,
                        bcols[:, fc:fc + 1], sss[:, tc4 * 512:(tc4 + 1) * 512],
                        ALU.add, ALU.mult), reads=(bbank, b_consts), writes=(bqs,))
                    for g in range(4):
                        src_g = [1, 0, 3, 2][g]
                        eng_c = "vector" if g < 2 else "gpsimd"
                        s.op(eng_c, lambda e, g=g, src_g=src_g, loc=loc: e.tensor_copy(
                            qw_b[g * 32:(g + 1) * 32, loc:loc + 512],
                            qs_b[src_g * 32:(src_g + 1) * 32, loc:loc + 512]),
                            reads=(bqs,), writes=(bqw,))
                    if fc < 8:
                        dst, bdst, dbase = Qt, b_Qt[pair], pair * T
                    else:
                        dst, bdst, dbase = Kt, b_Kt[pair % 2], (pair % 2) * T
                    eng = "vector" if fc < 8 else "gpsimd"

                    def mk_add(dst=dst, bdst=bdst, dbase=dbase, tc4=tc4, loc=loc,
                               bqc=bqc, bqw=bqw, eng=eng):
                        def do():
                            s.op(eng, lambda e: e.tensor_tensor(
                                dst[:, dbase + tc4 * 512: dbase + tc4 * 512 + 512],
                                qc_b[:, loc:loc + 512], qw_b[:, loc:loc + 512], ALU.add),
                                reads=(bqc, bqw), writes=(bdst,))
                        return do
                    pending_add[0] = mk_add()
            if is_v:
                flush_add()
                for half in range(ceil_div(nch, 8)):
                    w8 = min(8, nch - half * 8)
                    for j in range(w8):
                        tcv = half * 8 + j
                        s.op("tensor", lambda e, tcv=tcv, j=j: e.transpose(
                            psTB[:, j * 128:(j + 1) * 128], vfm[:, tcv * 128:(tcv + 1) * 128],
                            ident[:]), reads=(b_vfm, b_consts), writes=(b_psT[1],))
                    s.op("vector", lambda e, half=half, pair=pair, w8=w8: e.tensor_copy(
                        Vt[:].rearrange("p (tc hv) -> p tc hv", hv=1024)
                          [:, half * 8:half * 8 + w8, pair * 128:(pair + 1) * 128],
                        psTB[:, 0:w8 * 128].rearrange("p (j n) -> p j n", j=w8)),
                        reads=(b_psT[1],), writes=(b_Vt[pair],))
                nqb = nch
                s.op("scalar", lambda e, pair=pair: e.activation(
                    sqv[:, 0:nch * 128], Qt[:, pair * T: pair * T + nch * 128],
                    AF.Square),
                    reads=(b_Qt[pair],),
                    writes=(b_sqv, b_qw2[0], b_qw2[1]))
                for hh in range(2):
                    bank = psU if hh == 0 else psY
                    bb = b_psU if hh == 0 else b_psY
                    for qb in range(nqb):
                        s.op("tensor", lambda e, qb=qb, hh=hh, bank=bank: e.matmul(
                            bank[:, qb:qb + 1],
                            sqv[64 * hh:64 * hh + 64, qb * 128:qb * 128 + 128],
                            ones_c[64 * hh:64 * hh + 64, :],
                            start=(qb == 0), stop=(qb == nqb - 1)),
                            reads=(b_sqv, b_consts),
                            writes=(bb,) if qb in (0, nqb - 1) else ())
                    s.op("vector", lambda e, hh=hh, bank=bank, bb=bb, pair=pair, nqb=nqb: e.tensor_scalar(
                        bias_all[:, pair * 32 + hh * 16:pair * 32 + hh * 16 + nqb],
                        bank[:, 0:nqb], -0.125, -SHIFT, ALU.mult, ALU.add),
                        reads=(bb,), writes=(b_biasP[pair],))
                flush_add()
        flush_add()
    def phase_attn_pair(si, pair):
        nqb = nch
        if True:
            def emit_S_mm(qb, kc, pair=pair):
                L = (qb + 1) * 128
                w = min(512, L - kc * 512)
                for hh in (1, 0):
                    bankT, bbs = (psSA, b_SA) if hh == 0 else (psSB, b_SB)
                    bbank = bbs[kc % 2]
                    bank = bankT[:, (kc % 2) * 512:(kc % 2) * 512 + w]
                    rows = slice(64 * hh, 64 * hh + 64)
                    s.op("tensor", lambda e, rows=rows, qb=qb, kc=kc, w=w, bank=bank: e.matmul(
                        bank, Qt[rows, pair * T + qb * 128: pair * T + (qb + 1) * 128],
                        Kt[rows, (pair % 2) * T + kc * 512: (pair % 2) * T + kc * 512 + w],
                        start=True, stop=True),
                        reads=(b_Qt[pair], b_Kt[pair % 2]), writes=(bbank,))

            def stage_S(qb, pair=pair):
                nkc = ceil_div((qb + 1) * 128, 512)
                for kc in range(min(2, nkc)):
                    emit_S_mm(qb, kc)

            def softmax_stage(qb, pair=pair):
                L = (qb + 1) * 128
                nkc = ceil_div(L, 512)
                ngr = ceil_div(L, 1024)
                par = qb % 2
                PbA = P_A if par == 0 else P_A2
                PbB = P_B if par == 0 else P_B2
                for g in range(ngr):
                    w = min(1024, L - g * 1024)
                    kc_hi = min(2 * g + 1, nkc - 1)
                    for hh in range(2):
                        bankT, bbs = (psSA, b_SA) if hh == 0 else (psSB, b_SB)
                        bb0, bb1 = bbs[0], bbs[1]
                        if kc_hi == nkc - 1:
                            moff = qb * 128 - g * 1024
                            s.op("vector", lambda e, bankT=bankT, moff=moff: e.tensor_tensor(
                                bankT[:, moff:moff + 128], bankT[:, moff:moff + 128],
                                mbias[:], ALU.add),
                                reads=(bb0, bb1, b_consts),
                                writes=(bb0,) if moff < 512 else (bb1,))
                        s.op("scalar", lambda e, bankT=bankT, hh=hh, g=g, w=w, qb=qb, PbA=PbA, PbB=PbB: e.activation(
                            (PbA if hh == 0 else PbB)[:, g * 1024:g * 1024 + w],
                            bankT[:, 0:w], AF.Exp, scale=0.125,
                            bias=bias_all[:, pair * 32 + hh * 16 + qb:pair * 32 + hh * 16 + qb + 1],
                            accum_out=lparts[:, hh * 4 + g:hh * 4 + g + 1]),
                            reads=(bb0, bb1, b_biasP[pair], b_lparts),
                            writes=bl_P[par][hh] + (b_lparts,))
                    if 2 * g + 2 < nkc:
                        emit_S_mm(qb, 2 * g + 2)
                    if 2 * g + 3 < nkc:
                        emit_S_mm(qb, 2 * g + 3)
                s.op("scalar", lambda e: e.copy(lsum[:, 0:1], lparts[:, 0:1]),
                     reads=(b_lparts,), writes=(b_lparts, b_lsum))
                s.op("vector", lambda e, ngr=ngr: e.tensor_reduce(
                    lsum[:, 0:2], lparts[:].rearrange("p (h k) -> p h k", h=2)[:, :, 0:ngr],
                    AX.X, ALU.add), reads=(b_lparts,), writes=(b_lsum,))
                s.op("vector", lambda e: e.reciprocal(rl[:, 0:2], lsum[:, 0:2]),
                     reads=(b_lsum,), writes=(b_rl,))
                for hh in range(2):
                    Pbuf = PbA if hh == 0 else PbB
                    s.op("vector", lambda e, hh=hh, Pbuf=Pbuf, L=L: e.tensor_scalar(
                        Pbuf[:, 0:L], Pbuf[:, 0:L], rl[:, hh:hh + 1], None, ALU.mult),
                        reads=bl_P[par][hh], scalar_reads=(b_rl,),
                        writes=bl_P[par][hh])

            def pv_stage(qb, pair=pair):
                L = (qb + 1) * 128
                npk = ceil_div(L, 1024)
                par = qb % 2
                PbA = P_A if par == 0 else P_A2
                PbB = P_B if par == 0 else P_B2
                for pk in range(npk):
                    wpk = min(8, L // 128 - pk * 8)
                    for hh in range(2):
                        Pbuf = PbA if hh == 0 else PbB
                        psTx = psTA if hh == 0 else psTB
                        for j in range(wpk):
                            kcb = pk * 8 + j
                            s.op("tensor", lambda e, j=j, kcb=kcb, Pbuf=Pbuf, psTx=psTx: e.transpose(
                                psTx[:, j * 128:(j + 1) * 128],
                                Pbuf[:, kcb * 128:(kcb + 1) * 128], ident[:]),
                                reads=bl_P[par][hh] + (b_consts,), writes=(b_psT[hh],))
                    for hh in range(2):
                        psTx = psTA if hh == 0 else psTB
                        PTs = PT_A if hh == 0 else PT_B
                        s.op("vector", lambda e, psTx=psTx, PTs=PTs, wpk=wpk: e.tensor_copy(
                            PTs[:, 0:wpk * 128], psTx[:, 0:wpk * 128]),
                            reads=(b_psT[hh],), writes=(b_PT[hh],))
                    for hh in range(2):
                        PTs = PT_A if hh == 0 else PT_B
                        ybase = 64 * hh
                        for j in range(wpk):
                            kcb = pk * 8 + j
                            s.op("tensor", lambda e, j=j, kcb=kcb, hh=hh, PTs=PTs, ybase=ybase, L=L: e.matmul(
                                psY[ybase:ybase + 64, 0:128],
                                Vt[:, kcb * 1024 + (2 * pair + hh) * 64: kcb * 1024 + (2 * pair + hh) * 64 + 64],
                                PTs[:, j * 128:(j + 1) * 128],
                                start=(kcb == 0), stop=(kcb == L // 128 - 1),
                                skip_group_check=True),
                                reads=(b_PT[hh], b_Vt[pair]),
                                writes=(b_psY,) if kcb in (0, L // 128 - 1) else ())
                s.op("vector", lambda e, qb=qb: e.tensor_copy(
                    Qt[:, pair * T + qb * 128: pair * T + (qb + 1) * 128], psY[:, 0:128]),
                    reads=(b_psY,), writes=(b_Qt[pair],))

            stage_S(0)
            softmax_stage(0)
            if nqb > 1:
                stage_S(1)
            for qb in range(1, nqb):
                softmax_stage(qb)
                if qb + 1 < nqb:
                    stage_S(qb + 1)
                pv_stage(qb - 1)
            pv_stage(nqb - 1)

    def phase_out(si):
        for tb in range(nch):
            xb, bx = xr[tb % 2], b_xr[tb % 2]
            rb, br = rbuf[tb % 2], b_rbuf[tb % 2]
            s.op("sync", lambda e, xb=xb, si=si, tb=tb: e.dma_start(
                out=xb[:], in_=x_hbm[si, tb * 128:(tb + 1) * 128, :]),
                writes=(bx,), sem="LD", inc=16)
            for dh in range(2):
                bankT, bbs = (psSA, b_SA) if dh == 0 else (psSB, b_SB)
                bank, bbank = bankT[:, 0:512], bbs[0]
                for t_ in range(n_pairs):
                    s.op("tensor", lambda e, t_=t_, tb=tb, dh=dh, bank=bank: e.matmul(
                        bank, Qt[:, t_ * T + tb * 128: t_ * T + tb * 128 + 128],
                        woutT[:, t_ * D + dh * 512: t_ * D + dh * 512 + 512],
                        start=(t_ == 0), stop=False),
                        reads=(b_Qt[t_], b_consts), writes=(bbank,) if t_ == 0 else ())
                s.op("tensor", lambda e, dh=dh, bank=bank: e.matmul(
                    bank, ones_r[0:1, :], bout_s[0:1, dh * 512:dh * 512 + 512],
                    start=False, stop=True), reads=(b_consts,), writes=(bbank,))
                s.op("vector", lambda e, rb=rb, xb=xb, dh=dh, bank=bank: e.tensor_tensor(
                    rb[:, dh * 512:dh * 512 + 512], bank,
                    xb[:, dh * 512:dh * 512 + 512], ALU.add),
                    reads=(bbank, bx), writes=(br,))
            tok = s.op("sync", lambda e, rb=rb, si=si, tb=tb: e.dma_start(
                out=out_hbm[si, tb * 128:(tb + 1) * 128, :], in_=rb[:]),
                reads=(br,), writes=(b_out0,) if si == 0 else (), sem="ST", inc=16)
        return tok

    def phase_ffn():
        ew_ap = qc_b[:, 992:1024]
        sv = stat[:, 0:16]
        w1b = [Vt[:, 0:4096], Vt[:, 4096:8192]]
        w2b = [Vt[:, 8192:12288], Vt[:, 12288:16384]]
        _qtf = Qt[:].bitcast(F32)
        _ktf = Kt[:].bitcast(F32)
        y2c = [_qtf[:, i * 1024:(i + 1) * 1024] for i in range(8)]
        y2c += [_ktf[:, i * 1024:(i + 1) * 1024] for i in range(2)]
        y2c += [P_A[:].bitcast(F32), P_B[:].bitcast(F32),
                P_A2[:].bitcast(F32), P_B2[:].bitcast(F32),
                ccs[:].bitcast(F32), sss[:].bitcast(F32)]
        sq_g = qc_b
        u_g = qs_b
        t_g = qw_b
        h_sb = PT_A[:]
        b_ew = B("ew")
        b_sv = b_PT[1]
        b_w1b = [b_Vt[0], b_Vt[1]]
        b_w2b = [b_Vt[2], b_Vt[3]]
        b_y2 = B("y2")
        b_g = [B(f"g{i}") for i in range(6)]
        b_psH = [b_SA[0], b_SA[1], b_SB[0], b_SB[1]]
        b_psO = [b_psY, b_psU]

        s.op("sync", lambda e: e.dma_start(out=ew_ap, in_=ew_h),
            writes=(b_ew, b_qc2[0], b_qc2[1]), sem="LD", inc=16)
        _ties = ([b_Qt[i] for i in range(8)] + b_Kt +
                 [bl_P[0][0][0], bl_P[0][1][0], bl_P[1][0][0], bl_P[1][1][0],
                  b_consts, b_consts])
        for _mi in range(16):
            s.op("gpsimd", lambda e, _mi=_mi: e.memset(y2c[_mi][:], 0.0),
                 writes=(b_y2, _ties[_mi]))
        import os as _os
        _stage = int(_os.environ.get("FFN_STAGE", "4"))
        if _stage <= 1:
            return

        phase_ln(out_hbm[0], XT, b_XT, src_dep=b_out0)
        dump("d_xln2", XT[:], [b_XT])
        if _stage <= 2:
            return
        s.op("vector", lambda e: e.tensor_tensor(
            sv, ew_ap.rearrange("p (c k) -> p c k", k=2)[:, :, 0],
            ew_ap.rearrange("p (c k) -> p c k", k=2)[:, :, 1], ALU.add),
            reads=(b_ew, b_qc2[0], b_qc2[1]),
            writes=(b_sv, b_stat[0], b_stat[1]))

        def ffn_mm1(t):
            ic2, tw = divmod(t, nch // 2)
            wb1, bw1 = w1b[ic2 % 2], b_w1b[ic2 % 2]
            if tw == 0:
                s.op("gpsimd", lambda e, ic2=ic2, wb1=wb1: e.dma_start(
                    out=wb1.rearrange("p (dc i) -> p dc i", dc=8),
                    in_=w1_h[:, ic2 * 512:(ic2 + 1) * 512].rearrange("(dc p) i -> p dc i", p=128)),
                    writes=(bw1,), sem="GD", inc=16)
                wb2, bw2 = w2b[ic2 % 2], b_w2b[ic2 % 2]
                s.op("gpsimd", lambda e, ic2=ic2, wb2=wb2: e.dma_start(
                    out=wb2.rearrange("p (i4 d) -> p i4 d", i4=4),
                    in_=w2_h[ic2 * 512:(ic2 + 1) * 512, :].rearrange("(i4 p) d -> p i4 d", p=128)),
                    writes=(bw2,), sem="GD", inc=16)
            for icsub in range(4):
                bankT = (psSA, psSA, psSB, psSB)[icsub]
                bank = bankT[:, (icsub % 2) * 512:(icsub % 2) * 512 + 256]
                for dc in range(8):
                    s.op("tensor", lambda e, icsub=icsub, dc=dc, tw=tw, wb1=wb1, bank=bank: e.matmul(
                        bank, wb1[:, dc * 512 + icsub * 128: dc * 512 + (icsub + 1) * 128],
                        XT[:, dc * T + tw * 256:dc * T + tw * 256 + 256],
                        start=(dc == 0), stop=(dc == 7)),
                        reads=(bw1, b_XT), writes=(b_psH[icsub],) if dc in (0, 7) else ())

        def ffn_gelu(t):
            h_sb = (PT_A if t % 2 == 0 else PT_B)[:]
            bh = b_PT[t % 2]
            for half, bankT, rd in ((0, psSA, b_psH[0:2]), (1, psSB, b_psH[2:4])):
                hv = bankT[:].rearrange("p (i w) -> p i w", i=2)[:, :, 0:256]
                s.op("scalar", lambda e, hv=hv, half=half: e.activation(
                    sq_g[:, half * 512:half * 512 + 512].rearrange("p (i w) -> p i w", i=2),
                    hv, AF.Square), reads=tuple(rd), writes=(b_g[0], b_qc2[0], b_qc2[1]))
            s.op("vector", lambda e: e.tensor_scalar(
                sq_g[:], sq_g[:], GC3, GC1, ALU.mult, ALU.add),
                reads=(b_g[0],), writes=(b_g[1],))
            for half, bankT, rd in ((0, psSA, b_psH[0:2]), (1, psSB, b_psH[2:4])):
                hv = bankT[:].rearrange("p (i w) -> p i w", i=2)[:, :, 0:256]
                s.op("vector", lambda e, hv=hv, half=half: e.tensor_tensor(
                    u_g[:, half * 512:half * 512 + 512].rearrange("p (i w) -> p i w", i=2),
                    sq_g[:, half * 512:half * 512 + 512].rearrange("p (i w) -> p i w", i=2),
                    hv, ALU.mult),
                    reads=(b_g[1],) + tuple(rd), writes=(b_g[2], b_qs2[0], b_qs2[1]))
            s.op("scalar", lambda e: e.activation(t_g[:], u_g[:], AF.Tanh),
                reads=(b_g[2],), writes=(b_g[3], b_qw2[0], b_qw2[1], b_sqv))
            s.op("vector", lambda e: e.tensor_scalar(
                t_g[:], t_g[:], 0.5, 0.5, ALU.mult, ALU.add),
                reads=(b_g[3],), writes=(b_g[4],))
            for half, bankT, rd in ((0, psSA, b_psH[0:2]), (1, psSB, b_psH[2:4])):
                hv = bankT[:].rearrange("p (i w) -> p i w", i=2)[:, :, 0:256]
                s.op("vector", lambda e, hv=hv, half=half, h_sb=h_sb: e.tensor_tensor(
                    h_sb[:, half * 512:half * 512 + 512].rearrange("p (i w) -> p i w", i=2),
                    t_g[:, half * 512:half * 512 + 512].rearrange("p (i w) -> p i w", i=2),
                    hv, ALU.mult),
                    reads=(b_g[4],) + tuple(rd), writes=(bh,))

        def ffn_mm2(t):
            ic2, tw = divmod(t, nch // 2)
            wb2, bw2 = w2b[ic2 % 2], b_w2b[ic2 % 2]
            h_sb = (PT_A if t % 2 == 0 else PT_B)[:]
            bh = b_PT[t % 2]
            for dh in range(2):
                for tb2 in range(2):
                    psO = psY if tb2 == 0 else psU
                    bO = b_psO[tb2]
                    for icsub in range(4):
                        s.op("tensor", lambda e, icsub=icsub, dh=dh, tb2=tb2, psO=psO, wb2=wb2, h_sb=h_sb: e.matmul(
                            psO[:, 0:512],
                            h_sb[:, icsub * 256 + tb2 * 128: icsub * 256 + tb2 * 128 + 128],
                            wb2[:, icsub * 1024 + dh * 512: icsub * 1024 + dh * 512 + 512],
                            start=(icsub == 0), stop=(icsub == 3)),
                            reads=(bh, bw2), writes=(bO,) if icsub in (0, 3) else ())
                    s.op("vector", lambda e, tw=tw, dh=dh, tb2=tb2, psO=psO: e.tensor_tensor(
                        y2c[tw * 2 + tb2][:, dh * 512:dh * 512 + 512],
                        y2c[tw * 2 + tb2][:, dh * 512:dh * 512 + 512],
                        psO[:, 0:512], ALU.add),
                        reads=(b_psO[tb2], b_y2), writes=(b_y2,))

        n_iter = n_ic2 * (nch // 2)
        ffn_mm1(0)
        ffn_gelu(0)
        for t in range(1, n_iter + 1):
            if t < n_iter:
                ffn_mm1(t)
            ffn_mm2(t - 1)
            if t < n_iter:
                ffn_gelu(t)
        if dbg:
            for _mi in range(16):
                dump_part("d_y2", _mi, y2c[_mi], [b_y2])
        for tb in range(nch):
            xb, bx = xr[tb % 2], b_xr[tb % 2]
            rb, br = rbuf[tb % 2], b_rbuf[tb % 2]
            s.op("sync", lambda e, xb=xb, tb=tb: e.dma_start(
                out=xb[:], in_=out_hbm[0, tb * 128:(tb + 1) * 128, :]),
                reads=(b_out0,), writes=(bx,), sem="LD", inc=16)
            s.op("vector", lambda e, rb=rb, xb=xb, tb=tb: e.scalar_tensor_tensor(
                rb[:], y2c[tb][:], sv[:, tb:tb + 1], xb[:],
                ALU.mult, ALU.add), reads=(b_y2, bx), scalar_reads=(b_sv,),
                writes=(br,))
            s.op("sync", lambda e, rb=rb, tb=tb: e.dma_start(
                out=out_hbm[0, tb * 128:(tb + 1) * 128, :], in_=rb[:]),
                reads=(br,), writes=(b_out0,), sem="ST", inc=16)

    for si in range(n_seqs):
        ph = seq1_phases if si > 0 else seq0_phases
        if 'l' not in ph:
            continue
        phase_ln(x_hbm[si], XT, b_XT)
        if 'q' not in ph:
            continue
        if 'a' in ph:
            qkv_state = phase_qkv_start(si)
            for pair in range(n_pairs):
                phase_qkv_pair(si, pair, qkv_state)
                if pair >= 1:
                    phase_attn_pair(si, pair - 1)
            phase_attn_pair(si, n_pairs - 1)
        else:
            qkv_state = phase_qkv_start(si)
            for pair in range(n_pairs):
                phase_qkv_pair(si, pair, qkv_state)
        if 'a' not in ph or 'o' not in ph:
            continue
        phase_out(si)
    if do_ffn:
        phase_ffn()

    if "ST" in s.cnt:
        s.raw_wait("sync", "ST", s.cnt["ST"])

    sems = {}
    for name in ["LD", "SW", "ST", "GD", "PE", "DV", "AC", "GP"]:
        sems[name] = ctx.enter_context(nc.semaphore(name))
    with nc.Block() as blk:
        @blk.sync
        def _(e):
            s.play("sync", e, sems)

        @blk.tensor
        def _(e):
            s.play("tensor", e, sems)

        @blk.vector
        def _(e):
            s.play("vector", e, sems)

        @blk.scalar
        def _(e):
            s.play("scalar", e, sems)

        @blk.gpsimd
        def _(e):
            s.play("gpsimd", e, sems)

    ctx.close()
    return nc



def _prep_consts():
    half = 32
    p = np.arange(half, dtype=np.float64) / (half - 1)
    thetas = 1.0 * (10000.0 / 1.0) ** p
    pos = np.arange(T, dtype=np.float64)
    ang = pos[None, :] / thetas[:, None]
    cos32 = np.cos(ang).astype(np.float32)
    sin32 = np.sin(ang).astype(np.float32)
    CC = np.tile(cos32, (4, 1)).astype(ml_dtypes.bfloat16)
    SS = np.concatenate([-sin32, sin32, -sin32, sin32], axis=0).astype(ml_dtypes.bfloat16)
    i = np.arange(128)
    M = np.where(i[None, :] <= i[:, None], 0.0, MASKVAL).astype(np.float32)
    I = np.eye(128, dtype=np.float32).astype(ml_dtypes.bfloat16)
    return CC, SS, M, I


def _perm3072():
    perm = []
    for sec in range(2):
        base = sec * 1024
        for h in range(NH):
            perm += list(range(base + h * 64 + 32, base + h * 64 + 64))
            perm += list(range(base + h * 64, base + h * 64 + 32))
    perm += list(range(2048, 3072))
    return np.array(perm)


def make_in_maps(x, W_in, b_in, W_out, b_out_, ew, w1, w2, n_cores=8):
    perm = _perm3072()
    winT = np.ascontiguousarray(W_in[perm].T.astype(np.float32))
    bin_p = np.ascontiguousarray(b_in[perm].astype(np.float32).reshape(24, 128).T)
    woutT = np.ascontiguousarray(W_out.T.astype(np.float32))
    bout = np.ascontiguousarray(b_out_.astype(np.float32)).reshape(1, D)
    CC, SS, M, I = _prep_consts()
    in_maps = []
    for c in range(n_cores):
        batches = [c, c + 8, c + 16, c + 24]
        ewc = (ew[c * T:(c + 1) * T].astype(np.float32) if c < 2
               else np.zeros((T, 2), np.float32))
        ewc = np.ascontiguousarray(
            ewc.reshape(16, 128, 2).transpose(1, 0, 2).reshape(128, 32))
        in_maps.append({
            "x": np.ascontiguousarray(x[batches].astype(np.float32)),
            "winT": winT, "bin_p": bin_p, "woutT": woutT, "bout": bout,
            "cc": CC, "ss": SS, "mskb": M, "ident": I,
            "w1": np.ascontiguousarray(w1.astype(np.float32)),
            "w2": np.ascontiguousarray(w2.astype(np.float32)),
            "ew": np.ascontiguousarray(ewc),
        })
    return in_maps


def kernel(L0_attn_x, L0_attn_W_in, L0_attn_b_in, L0_attn_W_out, L0_attn_b_out,
           L0_ffn_scores, L0_ffn_expert_weights, L0_ffn_top_experts,
           L0_ffn_w1, L0_ffn_w2):
    x = np.asarray(L0_attn_x)
    in_maps = make_in_maps(x, np.asarray(L0_attn_W_in), np.asarray(L0_attn_b_in),
                           np.asarray(L0_attn_W_out), np.asarray(L0_attn_b_out),
                           np.asarray(L0_ffn_expert_weights),
                           np.asarray(L0_ffn_w1), np.asarray(L0_ffn_w2))
    nc = build_program()
    res = run_bass_kernel_spmd(nc, in_maps, core_ids=list(range(8)))
    out = np.empty((32, T, D), np.float32)
    for c in range(8):
        out[[c, c + 8, c + 16, c + 24]] = res.results[c]["out"]
    return out


# version 20
# speedup vs baseline: 1.2428x; 1.0185x over previous
import contextlib
import numpy as np
import ml_dtypes
import concourse.bass as bass
import concourse.mybir as mybir
from concourse.bass_utils import run_bass_kernel_spmd

F32 = mybir.dt.float32
BF16 = mybir.dt.bfloat16
AF = mybir.ActivationFunctionType
ALU = mybir.AluOpType
AX = mybir.AxisListType

T = 2048
D = 1024
NH = 16
NP = NH // 2
INTER = 8192
SHIFT = 20.0
MASKVAL = -30000.0
GC1 = 0.7978845608028654
GC3 = GC1 * 0.044715

ENGS = ("sync", "tensor", "vector", "scalar", "gpsimd")
ENG_SEM = {"sync": None, "tensor": "PE", "vector": "DV", "scalar": "AC", "gpsimd": "GP"}


class Buf:
    __slots__ = ("name", "writes", "reads")

    def __init__(self, name):
        self.name = name
        self.writes = {}
        self.reads = {}


class Sched:
    def __init__(self):
        self.ops = {e: [] for e in ENGS}
        self.cnt = {}
        self.waited = {e: {} for e in ENGS}

    def _wait(self, eng, sem, val):
        w = self.waited[eng]
        if w.get(sem, 0) >= val:
            return
        w[sem] = val
        self.ops[eng].append(("wait", sem, val))

    def wait_tok(self, eng, tok):
        if tok is not None:
            self._wait(eng, tok[0], tok[1])

    def op(self, eng, fn, reads=(), writes=(), sem=None, inc=None, extra_waits=(),
           scalar_reads=()):
        if sem is None:
            sem = ENG_SEM[eng]
            assert sem is not None, f"op on {eng} needs explicit sem"
            if inc is None:
                inc = 1
        own = ENG_SEM[eng]

        def w_(s_, v, force=False):
            if not force and s_ == own and s_ == "PE":
                return
            self._wait(eng, s_, v)
        for b in reads:
            for s_, v in b.writes.items():
                w_(s_, v)
        for b in scalar_reads:
            for s_, v in b.writes.items():
                w_(s_, v, force=True)
        for b in writes:
            for s_, v in b.writes.items():
                w_(s_, v)
            for s_, v in b.reads.items():
                w_(s_, v)
        for tok in extra_waits:
            self.wait_tok(eng, tok)
        self.cnt[sem] = self.cnt.get(sem, 0) + inc
        tok = (sem, self.cnt[sem])
        self.ops[eng].append(("op", fn, sem, inc))
        for b in tuple(reads) + tuple(scalar_reads):
            b.reads[sem] = tok[1]
        for b in writes:
            b.writes[sem] = tok[1]
            b.reads = {}
        return tok

    def raw_wait(self, eng, sem, val):
        self._wait(eng, sem, val)

    def play(self, name, e, sems):
        for item in self.ops[name]:
            if item[0] == "wait":
                _, sem, val = item
                e.wait_ge(sems[sem], val)
            else:
                _, fn, sem, inc = item
                inst = fn(e)
                if sem is not None:
                    assert inst is not None, f"op on {name} returned None ({sem})"
                    inst.then_inc(sems[sem], inc)


def ceil_div(a, b):
    return -(-a // b)


def build_program(n_seqs=4, n_pairs=NP, nch=16, do_ffn=True, n_ic2=16, dbg=False, seq1_phases='lqao', seq0_phases='lqao'):
    nc = bass.Bass("TRN2", target_bir_lowering=False, detect_race_conditions=False)

    x_hbm = nc.dram_tensor("x", [n_seqs, T, D], F32, kind="ExternalInput").ap()
    winT = nc.dram_tensor("winT", [D, 3072], F32, kind="ExternalInput").ap()
    bin_p = nc.dram_tensor("bin_p", [128, 24], F32, kind="ExternalInput").ap()
    woutT_h = nc.dram_tensor("woutT", [D, D], F32, kind="ExternalInput").ap()
    bout_h = nc.dram_tensor("bout", [1, D], F32, kind="ExternalInput").ap()
    cc_h = nc.dram_tensor("cc", [128, T], BF16, kind="ExternalInput").ap()
    ss_h = nc.dram_tensor("ss", [128, T], BF16, kind="ExternalInput").ap()
    mskb_h = nc.dram_tensor("mskb", [128, 128], F32, kind="ExternalInput").ap()
    ident_h = nc.dram_tensor("ident", [128, 128], BF16, kind="ExternalInput").ap()
    w1_h = nc.dram_tensor("w1", [D, INTER], F32, kind="ExternalInput").ap()
    w2_h = nc.dram_tensor("w2", [INTER, D], F32, kind="ExternalInput").ap()
    ew_h = nc.dram_tensor("ew", [128, 32], F32, kind="ExternalInput").ap()
    out_hbm = nc.dram_tensor("out", [n_seqs, T, D], F32, kind="ExternalOutput").ap()
    dbg_out = {}
    if dbg:
        for nm, shape, dt in [("d_xln", [128, 8 * T], BF16), ("d_qt", [128, T], BF16),
                              ("d_kt", [128, T], BF16), ("d_vt", [128, 16 * D], BF16),
                              ("d_bias", [128, 256], F32), ("d_lparts", [128, 8], F32),
                              ("d_lsum", [128, 2], F32), ("d_rl", [128, 2], F32),
                              ("d_pa", [128, T], BF16), ("d_y", [128, 8 * T], BF16)]:
            dbg_out[nm] = nc.dram_tensor(nm, shape, dt, kind="ExternalOutput").ap()
        dbg_out["d_y2"] = nc.dram_tensor("d_y2", [128, 16 * 1024], F32, kind="ExternalOutput").ap()
        dbg_out["d_xln2"] = nc.dram_tensor("d_xln2", [128, 8 * T], BF16, kind="ExternalOutput").ap()

    s = Sched()
    ctx = contextlib.ExitStack()
    sbt = lambda name, shape, dt: ctx.enter_context(nc.sbuf_tensor(name, shape, dt))
    pst = lambda name, shape, dt: ctx.enter_context(nc.psum_tensor(name, shape, dt))

    ccs = sbt("ccs", [128, T], BF16)
    sss = sbt("sss", [128, T], BF16)
    mbias = sbt("mbias", [128, 128], F32)
    ident = sbt("ident_s", [128, 128], BF16)
    bcols = sbt("bcols", [128, 24], F32)
    bout_s = sbt("bout_s", [1, D], BF16)
    ones_r = sbt("ones_r", [1, 128], BF16)
    ones_c = sbt("ones_c", [128, 1], BF16)
    woutT = sbt("woutT_s", [128, 8 * D], BF16)

    XT = sbt("XT", [128, 8 * T], BF16)
    QK = sbt("QK", [128, 2 * NP * T], BF16)
    Qt = QK[:, 0:NP * T]
    Kt = QK[:, NP * T:2 * NP * T]
    Vt = sbt("Vt", [128, 16 * D], BF16)

    xbuf = [sbt(f"xbuf{i}", [128, D], F32) for i in range(2)]
    scr0 = sbt("scr0", [128, D], F32)
    scr1 = sbt("scr1", [128, D], F32)
    stat = sbt("stat", [128, 16], F32)
    xln_c = [sbt(f"xln_c{i}", [128, D], BF16) for i in range(2)]

    wbuf = [sbt(f"wbuf{i}", [128, 8 * 128], BF16) for i in range(2)]
    qc_b = sbt("qc_b", [128, 1024], F32)
    qs_b = sbt("qs_b", [128, 1024], F32)
    qw_b = sbt("qw_b", [128, 1024], F32)

    bias_c = sbt("bias_c", [128, 256], F32)
    lparts = sbt("lparts", [128, 8], F32)
    lsum = sbt("lsum", [128, 2], F32)
    rl = sbt("rl", [128, 2], F32)
    P_A = sbt("P_A", [128, T], BF16)
    P_B = sbt("P_B", [128, T], BF16)
    P_A2 = qs_b[:].bitcast(BF16)
    P_B2 = qc_b[:].bitcast(BF16)
    PT_A = xln_c[0]
    PT_B = xln_c[1]

    sqx = scr0
    xr = xbuf
    rbuf = [scr0, scr1]
    vfm = P_A

    psSA = pst("psSA", [128, 1024], F32)
    psSB = pst("psSB", [128, 1024], F32)
    psTA = pst("psTA", [128, 1024], BF16)
    psTB = pst("psTB", [128, 1024], BF16)
    psY = pst("psY", [128, 512], F32)
    psU = pst("psU", [128, 512], F32)

    B = lambda n: Buf(n)
    b_consts = B("consts")
    b_XT = B("XT")
    b_Qt = [B(f"Qt{t}") for t in range(NP)]
    b_Kt = [B(f"Kt{t}") for t in range(NP)]
    b_Vt = [B(f"Vt{t}") for t in range(NP)]
    b_xbuf = [B(f"xbuf{i}") for i in range(2)]
    b_scr0 = B("scr0")
    b_scr1 = B("scr1")
    b_sqx = b_scr0
    b_stat = [B(f"stat{i}") for i in range(2)]
    b_xln = [B(f"xln{i}") for i in range(2)]
    b_wbuf = [B(f"wbuf{i}") for i in range(2)]
    b_qc = B("qc"); b_qs = B("qs"); b_qw = B("qw")
    b_qc2 = [B("qc_h0"), B("qc_h1")]
    b_qs2 = [B("qs_h0"), B("qs_h1")]
    b_qw2 = [B("qw_h0"), B("qw_h1")]
    _blPA2 = (B("P_A2"), b_qs2[0], b_qs2[1])
    _blPB2 = (B("P_B2"), b_qc2[0], b_qc2[1])
    b_lparts = B("lparts"); b_lsum = B("lsum"); b_rl = B("rl")
    bl_P = [[(B("P_A"),), (B("P_B"),)], [_blPA2, _blPB2]]
    b_vfm = bl_P[0][0][0]
    b_PT = [b_xln[0], b_xln[1]]
    bias_all = bias_c
    b_biasP = [B(f"bias{p}") for p in range(NP)]
    b_xr = b_xbuf
    b_rbuf = [b_scr0, b_scr1]
    b_SA = [B("psSA0"), B("psSA1")]
    b_SB = [B("psSB0"), B("psSB1")]
    b_psT = [B("psTA"), B("psTB")]
    b_psY = B("psY")
    b_psU = [B("psU0"), B("psU1")]
    b_out0 = B("out0_hbm")

    def dump(nm, ap, bufs):
        if not dbg:
            return
        s.op("sync", lambda e, nm=nm, ap=ap: e.dma_start(out=dbg_out[nm], in_=ap),
             reads=tuple(bufs), sem="ST", inc=16)

    for dst, src in [(ccs, cc_h), (sss, ss_h), (mbias, mskb_h), (ident, ident_h)]:
        s.op("sync", lambda e, d=dst, so=src: e.dma_start(out=d[:], in_=so),
             writes=(b_consts,), sem="LD", inc=16)
    s.op("sync", lambda e: e.dma_start(out=bcols[:], in_=bin_p),
         writes=(b_consts,), sem="LD", inc=16)
    s.op("gpsimd", lambda e: e.dma_start(out=bout_s[:], in_=bout_h),
         writes=(b_consts,), sem="GD", inc=16)
    s.op("gpsimd", lambda e: e.dma_start(
        out=woutT[:].rearrange("p (t d) -> p t d", t=8),
        in_=woutT_h.rearrange("(t p) d -> p t d", p=128)),
        writes=(b_consts,), sem="GD", inc=16)
    s.op("gpsimd", lambda e: e.memset(ones_r[:], 1.0), writes=(b_consts,))
    s.op("gpsimd", lambda e: e.memset(ones_c[:], 1.0), writes=(b_consts,))
    s.op("gpsimd", lambda e: e.memset(stat[:], 0.0), writes=(b_stat[0], b_stat[1]))
    if dbg:
        s.op("gpsimd", lambda e: e.memset(XT[:], 0.0), writes=(b_XT,))
        s.op("gpsimd", lambda e: e.memset(QK[:], 0.0), writes=tuple(b_Qt + b_Kt))
        s.op("gpsimd", lambda e: e.memset(Vt[:], 0.0), writes=tuple(b_Vt))
        s.op("gpsimd", lambda e: e.memset(P_A[:], 0.0), writes=(bl_P[0][0][0],))
        s.op("gpsimd", lambda e: e.memset(bias_c[:], 0.0), writes=tuple(b_biasP))
        s.op("gpsimd", lambda e: e.memset(lparts[:], 0.0), writes=(b_lparts,))
        s.op("gpsimd", lambda e: e.memset(lsum[:], 0.0), writes=(b_lsum,))
        s.op("gpsimd", lambda e: e.memset(rl[:], 0.0), writes=(b_rl,))

    def phase_ln(src_hbm, dst_XT, b_dst, src_dep=None):
        def ln_s1(tc):
            xb, bx = xbuf[tc % 2], b_xbuf[tc % 2]
            o = (tc % 2) * 8
            bst = b_stat[tc % 2]
            s.op("sync", lambda e, xb=xb, tc=tc: e.dma_start(
                out=xb[:], in_=src_hbm[tc * 128:(tc + 1) * 128, :]),
                reads=(src_dep,) if src_dep is not None else (),
                writes=(bx,), sem="LD", inc=16)
            s.op("vector", lambda e, xb=xb, o=o: e.tensor_reduce(
                stat[:, o:o + 1], xb[:], AX.X, ALU.add), reads=(bx,), writes=(bst,))
            s.op("scalar", lambda e, xb=xb, o=o: e.activation(
                sqx[:], xb[:], AF.Square, accum_out=stat[:, o + 1:o + 2]),
                reads=(bx, bst), writes=(b_sqx, bst))
            s.op("scalar", lambda e, o=o: e.copy(sqx[:, 0:1], stat[:, o + 1:o + 2]),
                 reads=(bst,), writes=(bst, b_sqx))

        def ln_s23(tc):
            xb, bx = xbuf[tc % 2], b_xbuf[tc % 2]
            o = (tc % 2) * 8
            bst = b_stat[tc % 2]
            xlc, bxl = xln_c[tc % 2], b_xln[tc % 2]
            s.op("vector", lambda e, o=o: e.tensor_scalar(
                stat[:, o + 2:o + 3], stat[:, o:o + 1], 1.0 / D, None, ALU.mult),
                reads=(bst,), writes=(bst,))
            s.op("vector", lambda e, o=o: e.tensor_scalar(
                stat[:, o + 3:o + 4], stat[:, o + 2:o + 3], stat[:, o + 2:o + 3],
                1e-5, ALU.mult, ALU.subtract), reads=(), scalar_reads=(bst,),
                writes=(bst,))
            s.op("vector", lambda e, o=o: e.scalar_tensor_tensor(
                stat[:, o + 4:o + 5], stat[:, o + 1:o + 2], 1.0 / D, stat[:, o + 3:o + 4],
                ALU.mult, ALU.subtract), reads=(bst,), writes=(bst,))
            s.op("scalar", lambda e, o=o: e.activation(
                stat[:, o + 5:o + 6], stat[:, o + 4:o + 5], AF.Ln),
                reads=(bst,), writes=(bst,))
            s.op("scalar", lambda e, o=o: e.activation(
                stat[:, o + 6:o + 7], stat[:, o + 5:o + 6], AF.Exp, scale=-0.5),
                reads=(bst,), writes=(bst,))
            s.op("vector", lambda e, o=o: e.tensor_scalar(
                stat[:, o + 7:o + 8], stat[:, o + 2:o + 3], -1.0, None, ALU.mult),
                reads=(bst,), writes=(bst,))
            s.op("vector", lambda e, xb=xb, xlc=xlc, o=o: e.tensor_scalar(
                xlc[:], xb[:], stat[:, o + 7:o + 8], stat[:, o + 6:o + 7],
                ALU.add, ALU.mult), reads=(bx,), scalar_reads=(bst,), writes=(bxl,))
            for dc in range(8):
                s.op("tensor", lambda e, dc=dc, xlc=xlc: e.transpose(
                    psTA[:, dc * 128:(dc + 1) * 128], xlc[:, dc * 128:(dc + 1) * 128],
                    ident[:]), reads=(bxl, b_consts), writes=(b_psT[0],))
            s.op("vector", lambda e, tc=tc: e.tensor_copy(
                dst_XT[:].rearrange("p (dc t) -> p dc t", dc=8)[:, :, tc * 128:(tc + 1) * 128],
                psTA[:].rearrange("p (dc t) -> p dc t", dc=8)),
                reads=(b_psT[0],), writes=(b_dst,))

        ln_s1(0)
        for tc in range(nch):
            if tc + 1 < nch:
                ln_s1(tc + 1)
            ln_s23(tc)

    def phase_qkv_start(si):
        ntc = nch // 4
        fc_order = []
        for t_ in range(n_pairs):
            fc_order += [t_, 8 + t_, 16 + t_]
        nfc = len(fc_order)
        state = {"fc_order": fc_order, "nfc": nfc, "ntc": ntc,
                 "pending": [None]}

        def wload(step):
            fc = fc_order[step]
            wb, bw = wbuf[step % 2], b_wbuf[step % 2]
            s.op("gpsimd", lambda e, fc=fc, wb=wb: e.dma_start(
                out=wb[:].rearrange("p (dc n) -> p dc n", dc=8),
                in_=winT[:, fc * 128:(fc + 1) * 128].rearrange("(dc p) n -> p dc n", p=128)),
                writes=(bw,), sem="GD", inc=16)

        state["wload"] = wload
        wload(0)
        if nfc > 1:
            wload(1)
        return state

    def phase_qkv_pair(si, pair_idx, state):
        fc_order, nfc, ntc = state["fc_order"], state["nfc"], state["ntc"]
        wload = state["wload"]
        pending_add = state["pending"]

        def flush_add():
            if pending_add[0] is not None:
                pending_add[0]()
                pending_add[0] = None

        for step in range(pair_idx * 3, pair_idx * 3 + 3):
            fc = fc_order[step]
            wb, bw = wbuf[step % 2], b_wbuf[step % 2]
            is_v = fc >= 16
            pair = fc % 8
            for tc4 in range(ntc):
                bankT, bb = (psSA, b_SA) if tc4 % 2 == 0 else (psSB, b_SB)
                bank, bbank = bankT[:, 0:512], bb[0]
                loc = (tc4 % 2) * 512
                for dc in range(8):
                    s.op("tensor", lambda e, dc=dc, tc4=tc4, wb=wb, bank=bank: e.matmul(
                        bank, wb[:, dc * 128:(dc + 1) * 128],
                        XT[:, dc * T + tc4 * 512: dc * T + (tc4 + 1) * 512],
                        start=(dc == 0), stop=(dc == 7)),
                        reads=(bw, b_XT), writes=(bbank,) if dc in (0, 7) else ())
                if tc4 == ntc - 1 and step + 2 < nfc:
                    wload(step + 2)
                if is_v:
                    s.op("vector", lambda e, tc4=tc4, fc=fc, bank=bank: e.tensor_scalar(
                        vfm[:, tc4 * 512:(tc4 + 1) * 512], bank,
                        bcols[:, fc:fc + 1], None, ALU.add),
                        reads=(bbank, b_consts), writes=(b_vfm,))
                else:
                    flush_add()
                    bqc = b_qc2[tc4 % 2]
                    bqs = b_qs2[tc4 % 2]
                    bqw = b_qw2[tc4 % 2]
                    s.op("vector", lambda e, tc4=tc4, fc=fc, bank=bank, loc=loc: e.scalar_tensor_tensor(
                        qc_b[:, loc:loc + 512], bank,
                        bcols[:, fc:fc + 1], ccs[:, tc4 * 512:(tc4 + 1) * 512],
                        ALU.add, ALU.mult), reads=(bbank, b_consts), writes=(bqc,))
                    s.op("vector", lambda e, tc4=tc4, fc=fc, bank=bank, loc=loc: e.scalar_tensor_tensor(
                        qs_b[:, loc:loc + 512], bank,
                        bcols[:, fc:fc + 1], sss[:, tc4 * 512:(tc4 + 1) * 512],
                        ALU.add, ALU.mult), reads=(bbank, b_consts), writes=(bqs,))
                    for g in range(4):
                        src_g = [1, 0, 3, 2][g]
                        eng_c = "vector" if g < 2 else "gpsimd"
                        s.op(eng_c, lambda e, g=g, src_g=src_g, loc=loc: e.tensor_copy(
                            qw_b[g * 32:(g + 1) * 32, loc:loc + 512],
                            qs_b[src_g * 32:(src_g + 1) * 32, loc:loc + 512]),
                            reads=(bqs,), writes=(bqw,))
                    dst, bdst = (Qt, b_Qt[pair]) if fc < 8 else (Kt, b_Kt[pair])
                    eng = "vector" if fc < 8 else "gpsimd"

                    def mk_add(dst=dst, bdst=bdst, pair=pair, tc4=tc4, loc=loc,
                               bqc=bqc, bqw=bqw, eng=eng):
                        def do():
                            s.op(eng, lambda e: e.tensor_tensor(
                                dst[:, pair * T + tc4 * 512: pair * T + tc4 * 512 + 512],
                                qc_b[:, loc:loc + 512], qw_b[:, loc:loc + 512], ALU.add),
                                reads=(bqc, bqw), writes=(bdst,))
                        return do
                    pending_add[0] = mk_add()
            if is_v:
                flush_add()
                for half in range(ceil_div(nch, 8)):
                    w8 = min(8, nch - half * 8)
                    for j in range(w8):
                        tcv = half * 8 + j
                        s.op("tensor", lambda e, tcv=tcv, j=j: e.transpose(
                            psTB[:, j * 128:(j + 1) * 128], vfm[:, tcv * 128:(tcv + 1) * 128],
                            ident[:]), reads=(b_vfm, b_consts), writes=(b_psT[1],))
                    s.op("vector", lambda e, half=half, pair=pair, w8=w8: e.tensor_copy(
                        Vt[:].rearrange("p (tc hv) -> p tc hv", hv=1024)
                          [:, half * 8:half * 8 + w8, pair * 128:(pair + 1) * 128],
                        psTB[:, 0:w8 * 128].rearrange("p (j n) -> p j n", j=w8)),
                        reads=(b_psT[1],), writes=(b_Vt[pair],))
                nqb = nch
                w0 = min(1024, nch * 128)
                s.op("scalar", lambda e, pair=pair, w0=w0: e.activation(
                    PT_A[:, 0:w0], Qt[:, pair * T: pair * T + w0], AF.Square),
                    reads=(b_Qt[pair],), writes=(b_PT[0],))
                if nch * 128 > 1024:
                    s.op("scalar", lambda e, pair=pair: e.activation(
                        PT_B[:, 0:nch * 128 - 1024],
                        Qt[:, pair * T + 1024: pair * T + nch * 128], AF.Square),
                        reads=(b_Qt[pair],), writes=(b_PT[1],))
                for hh in range(2):
                    bank = psU if hh == 0 else psY
                    bb = b_psU[0] if hh == 0 else b_psY
                    for qb in range(nqb):
                        sq_t = PT_A if qb < 8 else PT_B
                        bsq = b_PT[0] if qb < 8 else b_PT[1]
                        s.op("tensor", lambda e, qb=qb, hh=hh, bank=bank, sq_t=sq_t: e.matmul(
                            bank[:, qb:qb + 1],
                            sq_t[64 * hh:64 * hh + 64, (qb % 8) * 128:(qb % 8) * 128 + 128],
                            ones_c[64 * hh:64 * hh + 64, :],
                            start=(qb == 0), stop=(qb == nqb - 1)),
                            reads=(bsq, b_consts),
                            writes=(bb,) if qb in (0, nqb - 1) else ())
                for hh in range(2):
                    bank = psU if hh == 0 else psY
                    bb = b_psU[0] if hh == 0 else b_psY
                    s.op("vector", lambda e, hh=hh, bank=bank, pair=pair, nqb=nqb: e.tensor_scalar(
                        bias_all[:, pair * 32 + hh * 16:pair * 32 + hh * 16 + nqb],
                        bank[:, 0:nqb], -0.125, -SHIFT, ALU.mult, ALU.add),
                        reads=(bb,), writes=(b_biasP[pair],))
                flush_add()
        flush_add()
    def phase_attn_pair(si, pair):
        nqb = nch
        if True:
            def emit_S_mm(qb, kc, pair=pair):
                L = (qb + 1) * 128
                w = min(512, L - kc * 512)
                for hh in (1, 0):
                    bankT, bbs = (psSA, b_SA) if hh == 0 else (psSB, b_SB)
                    bbank = bbs[kc % 2]
                    bank = bankT[:, (kc % 2) * 512:(kc % 2) * 512 + w]
                    rows = slice(64 * hh, 64 * hh + 64)
                    s.op("tensor", lambda e, rows=rows, qb=qb, kc=kc, w=w, bank=bank: e.matmul(
                        bank, Qt[rows, pair * T + qb * 128: pair * T + (qb + 1) * 128],
                        Kt[rows, pair * T + kc * 512: pair * T + kc * 512 + w],
                        start=True, stop=True),
                        reads=(b_Qt[pair], b_Kt[pair]), writes=(bbank,))

            def stage_S(qb, pair=pair):
                nkc = ceil_div((qb + 1) * 128, 512)
                for kc in range(min(2, nkc)):
                    emit_S_mm(qb, kc)

            def softmax_stage(qb, pair=pair):
                L = (qb + 1) * 128
                nkc = ceil_div(L, 512)
                ngr = ceil_div(L, 1024)
                par = qb % 2
                PbA = P_A if par == 0 else P_A2
                PbB = P_B if par == 0 else P_B2
                for g in range(ngr):
                    w = min(1024, L - g * 1024)
                    kc_hi = min(2 * g + 1, nkc - 1)
                    for hh in range(2):
                        bankT, bbs = (psSA, b_SA) if hh == 0 else (psSB, b_SB)
                        bb0, bb1 = bbs[0], bbs[1]
                        if kc_hi == nkc - 1:
                            moff = qb * 128 - g * 1024
                            s.op("vector", lambda e, bankT=bankT, moff=moff: e.tensor_tensor(
                                bankT[:, moff:moff + 128], bankT[:, moff:moff + 128],
                                mbias[:], ALU.add),
                                reads=(bb0, bb1, b_consts),
                                writes=(bb0,) if moff < 512 else (bb1,))
                        s.op("scalar", lambda e, bankT=bankT, hh=hh, g=g, w=w, qb=qb, PbA=PbA, PbB=PbB: e.activation(
                            (PbA if hh == 0 else PbB)[:, g * 1024:g * 1024 + w],
                            bankT[:, 0:w], AF.Exp, scale=0.125,
                            bias=bias_all[:, pair * 32 + hh * 16 + qb:pair * 32 + hh * 16 + qb + 1],
                            accum_out=lparts[:, hh * 4 + g:hh * 4 + g + 1]),
                            reads=(bb0, bb1, b_biasP[pair], b_lparts),
                            writes=bl_P[par][hh] + (b_lparts,))
                    if 2 * g + 2 < nkc:
                        emit_S_mm(qb, 2 * g + 2)
                    if 2 * g + 3 < nkc:
                        emit_S_mm(qb, 2 * g + 3)
                s.op("scalar", lambda e: e.copy(lsum[:, 0:1], lparts[:, 0:1]),
                     reads=(b_lparts,), writes=(b_lparts, b_lsum))
                s.op("vector", lambda e, ngr=ngr: e.tensor_reduce(
                    lsum[:, 0:2], lparts[:].rearrange("p (h k) -> p h k", h=2)[:, :, 0:ngr],
                    AX.X, ALU.add), reads=(b_lparts,), writes=(b_lsum,))
                s.op("vector", lambda e: e.reciprocal(rl[:, 0:2], lsum[:, 0:2]),
                     reads=(b_lsum,), writes=(b_rl,))
                for hh in range(2):
                    Pbuf = PbA if hh == 0 else PbB
                    s.op("vector", lambda e, hh=hh, Pbuf=Pbuf, L=L: e.tensor_scalar(
                        Pbuf[:, 0:L], Pbuf[:, 0:L], rl[:, hh:hh + 1], None, ALU.mult),
                        reads=bl_P[par][hh], scalar_reads=(b_rl,),
                        writes=bl_P[par][hh])

            def pv_stage(qb, pair=pair):
                L = (qb + 1) * 128
                npk = ceil_div(L, 1024)
                par = qb % 2
                PbA = P_A if par == 0 else P_A2
                PbB = P_B if par == 0 else P_B2
                for pk in range(npk):
                    wpk = min(8, L // 128 - pk * 8)
                    for hh in range(2):
                        Pbuf = PbA if hh == 0 else PbB
                        psT = psTA if hh == 0 else psTB
                        for j in range(wpk):
                            kcb = pk * 8 + j
                            s.op("tensor", lambda e, j=j, kcb=kcb, Pbuf=Pbuf, psT=psT: e.transpose(
                                psT[:, j * 128:(j + 1) * 128],
                                Pbuf[:, kcb * 128:(kcb + 1) * 128], ident[:]),
                                reads=bl_P[par][hh] + (b_consts,), writes=(b_psT[hh],))
                    for hh in range(2):
                        psT = psTA if hh == 0 else psTB
                        PTs = PT_A if hh == 0 else PT_B
                        s.op("vector", lambda e, psT=psT, PTs=PTs, wpk=wpk: e.tensor_copy(
                            PTs[:, 0:wpk * 128], psT[:, 0:wpk * 128]),
                            reads=(b_psT[hh],), writes=(b_PT[hh],))
                    for hh in range(2):
                        PTs = PT_A if hh == 0 else PT_B
                        ybase = 64 * hh
                        for j in range(wpk):
                            kcb = pk * 8 + j
                            s.op("tensor", lambda e, j=j, kcb=kcb, hh=hh, PTs=PTs, ybase=ybase, L=L: e.matmul(
                                psY[ybase:ybase + 64, 0:128],
                                Vt[:, kcb * 1024 + (2 * pair + hh) * 64: kcb * 1024 + (2 * pair + hh) * 64 + 64],
                                PTs[:, j * 128:(j + 1) * 128],
                                start=(kcb == 0), stop=(kcb == L // 128 - 1),
                                skip_group_check=True),
                                reads=(b_PT[hh], b_Vt[pair]),
                                writes=(b_psY,) if kcb in (0, L // 128 - 1) else ())
                s.op("vector", lambda e, qb=qb: e.tensor_copy(
                    Qt[:, pair * T + qb * 128: pair * T + (qb + 1) * 128], psY[:, 0:128]),
                    reads=(b_psY,), writes=(b_Qt[pair],))

            stage_S(0)
            softmax_stage(0)
            if nqb > 1:
                stage_S(1)
            for qb in range(1, nqb):
                softmax_stage(qb)
                if qb + 1 < nqb:
                    stage_S(qb + 1)
                pv_stage(qb - 1)
            pv_stage(nqb - 1)

    def phase_out(si):
        for t_ in range(n_pairs):
            for _sem, _v in b_Qt[t_].writes.items():
                s.raw_wait("tensor", _sem, _v)
        for tb in range(nch):
            xb, bx = xr[tb % 2], b_xr[tb % 2]
            rb, br = rbuf[tb % 2], b_rbuf[tb % 2]
            s.op("sync", lambda e, xb=xb, si=si, tb=tb: e.dma_start(
                out=xb[:], in_=x_hbm[si, tb * 128:(tb + 1) * 128, :]),
                writes=(bx,), sem="LD", inc=16)
            for dh in range(2):
                bankT, bbs = (psSA, b_SA) if dh == 0 else (psSB, b_SB)
                bank, bbank = bankT[:, 0:512], bbs[0]
                for t_ in range(n_pairs):
                    s.op("tensor", lambda e, t_=t_, tb=tb, dh=dh, bank=bank: e.matmul(
                        bank, Qt[:, t_ * T + tb * 128: t_ * T + tb * 128 + 128],
                        woutT[:, t_ * D + dh * 512: t_ * D + dh * 512 + 512],
                        start=(t_ == 0), stop=False),
                        reads=(b_Qt[t_], b_consts), writes=(bbank,) if t_ == 0 else ())
                s.op("tensor", lambda e, dh=dh, bank=bank: e.matmul(
                    bank, ones_r[0:1, :], bout_s[0:1, dh * 512:dh * 512 + 512],
                    start=False, stop=True), reads=(b_consts,), writes=(bbank,))
                s.op("vector", lambda e, rb=rb, xb=xb, dh=dh, bank=bank: e.tensor_tensor(
                    rb[:, dh * 512:dh * 512 + 512], bank,
                    xb[:, dh * 512:dh * 512 + 512], ALU.add),
                    reads=(bbank, bx), writes=(br,))
            tok = s.op("sync", lambda e, rb=rb, si=si, tb=tb: e.dma_start(
                out=out_hbm[si, tb * 128:(tb + 1) * 128, :], in_=rb[:]),
                reads=(br,), writes=(b_out0,) if si == 0 else (), sem="ST", inc=16)
        return tok

    def phase_ffn():
        ew_ap = bias_c[:, 0:32]
        sv = stat[:, 0:16]
        w1b = [Vt[:, 0:4096], Vt[:, 4096:8192]]
        w2b = [Vt[:, 8192:12288], Vt[:, 12288:16384]]
        y2 = QK[:].bitcast(F32)
        sq_g = qc_b
        u_g = qs_b
        t_g = qw_b
        h_sb = PT_A[:]
        b_ew = B("ew")
        b_sv = b_PT[1]
        b_w1b = [b_Vt[0], b_Vt[1]]
        b_w2b = [b_Vt[2], b_Vt[3]]
        b_y2 = B("y2")
        b_g = [B(f"g{i}") for i in range(6)]
        b_psH = [b_SA[0], b_SA[1], b_SB[0], b_SB[1]]
        b_psO = [b_psY, b_psU[0]]

        s.op("sync", lambda e: e.dma_start(out=ew_ap, in_=ew_h),
            writes=tuple([b_ew] + b_biasP), sem="LD", inc=16)
        for _mi in range(4):
            s.op("gpsimd", lambda e, _mi=_mi: e.memset(y2[:, _mi * 4096:(_mi + 1) * 4096], 0.0),
                 writes=tuple([b_y2] + b_Qt + b_Kt))
        import os as _os
        _stage = int(_os.environ.get("FFN_STAGE", "4"))
        if _stage <= 1:
            return

        phase_ln(out_hbm[0], XT, b_XT, src_dep=b_out0)
        dump("d_xln2", XT[:], [b_XT])
        if _stage <= 2:
            return
        s.op("vector", lambda e: e.tensor_tensor(
            sv, ew_ap.rearrange("p (c k) -> p c k", k=2)[:, :, 0],
            ew_ap.rearrange("p (c k) -> p c k", k=2)[:, :, 1], ALU.add),
            reads=(b_ew,), writes=(b_sv, b_stat[0], b_stat[1]))

        def ffn_mm1(t):
            ic2, tw = divmod(t, nch // 2)
            wb1, bw1 = w1b[ic2 % 2], b_w1b[ic2 % 2]
            if tw == 0:
                s.op("gpsimd", lambda e, ic2=ic2, wb1=wb1: e.dma_start(
                    out=wb1.rearrange("p (dc i) -> p dc i", dc=8),
                    in_=w1_h[:, ic2 * 512:(ic2 + 1) * 512].rearrange("(dc p) i -> p dc i", p=128)),
                    writes=(bw1,), sem="GD", inc=16)
                wb2, bw2 = w2b[ic2 % 2], b_w2b[ic2 % 2]
                s.op("gpsimd", lambda e, ic2=ic2, wb2=wb2: e.dma_start(
                    out=wb2.rearrange("p (i4 d) -> p i4 d", i4=4),
                    in_=w2_h[ic2 * 512:(ic2 + 1) * 512, :].rearrange("(i4 p) d -> p i4 d", p=128)),
                    writes=(bw2,), sem="GD", inc=16)
            for icsub in range(4):
                bankT = (psSA, psSA, psSB, psSB)[icsub]
                bank = bankT[:, (icsub % 2) * 512:(icsub % 2) * 512 + 256]
                for dc in range(8):
                    s.op("tensor", lambda e, icsub=icsub, dc=dc, tw=tw, wb1=wb1, bank=bank: e.matmul(
                        bank, wb1[:, dc * 512 + icsub * 128: dc * 512 + (icsub + 1) * 128],
                        XT[:, dc * T + tw * 256:dc * T + tw * 256 + 256],
                        start=(dc == 0), stop=(dc == 7)),
                        reads=(bw1, b_XT), writes=(b_psH[icsub],) if dc in (0, 7) else ())

        def ffn_gelu(t):
            h_sb = (PT_A if t % 2 == 0 else PT_B)[:]
            bh = b_PT[t % 2]
            for half, bankT, rd in ((0, psSA, b_psH[0:2]), (1, psSB, b_psH[2:4])):
                hv = bankT[:].rearrange("p (i w) -> p i w", i=2)[:, :, 0:256]
                s.op("scalar", lambda e, hv=hv, half=half: e.activation(
                    sq_g[:, half * 512:half * 512 + 512].rearrange("p (i w) -> p i w", i=2),
                    hv, AF.Square), reads=tuple(rd), writes=(b_g[0],) + bl_P[1][1])
            s.op("vector", lambda e: e.tensor_scalar(
                sq_g[:], sq_g[:], GC3, GC1, ALU.mult, ALU.add),
                reads=(b_g[0],), writes=(b_g[1],))
            for half, bankT, rd in ((0, psSA, b_psH[0:2]), (1, psSB, b_psH[2:4])):
                hv = bankT[:].rearrange("p (i w) -> p i w", i=2)[:, :, 0:256]
                s.op("vector", lambda e, hv=hv, half=half: e.tensor_tensor(
                    u_g[:, half * 512:half * 512 + 512].rearrange("p (i w) -> p i w", i=2),
                    sq_g[:, half * 512:half * 512 + 512].rearrange("p (i w) -> p i w", i=2),
                    hv, ALU.mult),
                    reads=(b_g[1],) + tuple(rd), writes=(b_g[2],) + bl_P[1][0])
            s.op("scalar", lambda e: e.activation(t_g[:], u_g[:], AF.Tanh),
                reads=(b_g[2],), writes=(b_g[3], b_qw2[0], b_qw2[1]))
            s.op("vector", lambda e: e.tensor_scalar(
                t_g[:], t_g[:], 0.5, 0.5, ALU.mult, ALU.add),
                reads=(b_g[3],), writes=(b_g[4],))
            for half, bankT, rd in ((0, psSA, b_psH[0:2]), (1, psSB, b_psH[2:4])):
                hv = bankT[:].rearrange("p (i w) -> p i w", i=2)[:, :, 0:256]
                s.op("vector", lambda e, hv=hv, half=half, h_sb=h_sb: e.tensor_tensor(
                    h_sb[:, half * 512:half * 512 + 512].rearrange("p (i w) -> p i w", i=2),
                    t_g[:, half * 512:half * 512 + 512].rearrange("p (i w) -> p i w", i=2),
                    hv, ALU.mult),
                    reads=(b_g[4],) + tuple(rd), writes=(bh,))

        def ffn_mm2(t):
            ic2, tw = divmod(t, nch // 2)
            wb2, bw2 = w2b[ic2 % 2], b_w2b[ic2 % 2]
            h_sb = (PT_A if t % 2 == 0 else PT_B)[:]
            bh = b_PT[t % 2]
            for dh in range(2):
                for tb2 in range(2):
                    psO = psY if tb2 == 0 else psU
                    bO = b_psO[tb2]
                    for icsub in range(4):
                        s.op("tensor", lambda e, icsub=icsub, dh=dh, tb2=tb2, psO=psO, wb2=wb2, h_sb=h_sb: e.matmul(
                            psO[:, 0:512],
                            h_sb[:, icsub * 256 + tb2 * 128: icsub * 256 + tb2 * 128 + 128],
                            wb2[:, icsub * 1024 + dh * 512: icsub * 1024 + dh * 512 + 512],
                            start=(icsub == 0), stop=(icsub == 3)),
                            reads=(bh, bw2), writes=(bO,) if icsub in (0, 3) else ())
                    s.op("vector", lambda e, tw=tw, dh=dh, tb2=tb2, psO=psO: e.tensor_tensor(
                        y2[:, (tw * 2 + tb2) * 1024 + dh * 512: (tw * 2 + tb2) * 1024 + dh * 512 + 512],
                        y2[:, (tw * 2 + tb2) * 1024 + dh * 512: (tw * 2 + tb2) * 1024 + dh * 512 + 512],
                        psO[:, 0:512], ALU.add),
                        reads=(b_psO[tb2], b_y2), writes=(b_y2,))

        n_iter = n_ic2 * (nch // 2)
        ffn_mm1(0)
        ffn_gelu(0)
        for t in range(1, n_iter + 1):
            if t < n_iter:
                ffn_mm1(t)
            ffn_mm2(t - 1)
            if t < n_iter:
                ffn_gelu(t)
        dump("d_y2", y2, [b_y2])
        for tb in range(nch):
            xb, bx = xr[tb % 2], b_xr[tb % 2]
            rb, br = rbuf[tb % 2], b_rbuf[tb % 2]
            s.op("sync", lambda e, xb=xb, tb=tb: e.dma_start(
                out=xb[:], in_=out_hbm[0, tb * 128:(tb + 1) * 128, :]),
                reads=(b_out0,), writes=(bx,), sem="LD", inc=16)
            s.op("vector", lambda e, rb=rb, xb=xb, tb=tb: e.scalar_tensor_tensor(
                rb[:], y2[:, tb * 1024:(tb + 1) * 1024], sv[:, tb:tb + 1], xb[:],
                ALU.mult, ALU.add), reads=(b_y2, bx), scalar_reads=(b_sv,),
                writes=(br,))
            s.op("sync", lambda e, rb=rb, tb=tb: e.dma_start(
                out=out_hbm[0, tb * 128:(tb + 1) * 128, :], in_=rb[:]),
                reads=(br,), writes=(b_out0,), sem="ST", inc=16)

    for si in range(n_seqs):
        ph = seq1_phases if si > 0 else seq0_phases
        if 'l' not in ph:
            continue
        phase_ln(x_hbm[si], XT, b_XT)
        if 'q' not in ph:
            continue
        if 'a' in ph:
            qkv_state = phase_qkv_start(si)
            for pair in range(n_pairs):
                phase_qkv_pair(si, pair, qkv_state)
                if pair >= 1:
                    phase_attn_pair(si, pair - 1)
            phase_attn_pair(si, n_pairs - 1)
        else:
            qkv_state = phase_qkv_start(si)
            for pair in range(n_pairs):
                phase_qkv_pair(si, pair, qkv_state)
        if 'a' not in ph or 'o' not in ph:
            continue
        phase_out(si)
    if do_ffn:
        phase_ffn()

    if "ST" in s.cnt:
        s.raw_wait("sync", "ST", s.cnt["ST"])

    sems = {}
    for name in ["LD", "SW", "ST", "GD", "PE", "DV", "AC", "GP"]:
        sems[name] = ctx.enter_context(nc.semaphore(name))
    with nc.Block() as blk:
        @blk.sync
        def _(e):
            s.play("sync", e, sems)

        @blk.tensor
        def _(e):
            s.play("tensor", e, sems)

        @blk.vector
        def _(e):
            s.play("vector", e, sems)

        @blk.scalar
        def _(e):
            s.play("scalar", e, sems)

        @blk.gpsimd
        def _(e):
            s.play("gpsimd", e, sems)

    ctx.close()
    return nc



def _prep_consts():
    half = 32
    p = np.arange(half, dtype=np.float64) / (half - 1)
    thetas = 1.0 * (10000.0 / 1.0) ** p
    pos = np.arange(T, dtype=np.float64)
    ang = pos[None, :] / thetas[:, None]
    cos32 = np.cos(ang).astype(np.float32)
    sin32 = np.sin(ang).astype(np.float32)
    CC = np.tile(cos32, (4, 1)).astype(ml_dtypes.bfloat16)
    SS = np.concatenate([-sin32, sin32, -sin32, sin32], axis=0).astype(ml_dtypes.bfloat16)
    i = np.arange(128)
    M = np.where(i[None, :] <= i[:, None], 0.0, MASKVAL).astype(np.float32)
    I = np.eye(128, dtype=np.float32).astype(ml_dtypes.bfloat16)
    return CC, SS, M, I


def _perm3072():
    perm = []
    for sec in range(2):
        base = sec * 1024
        for h in range(NH):
            perm += list(range(base + h * 64 + 32, base + h * 64 + 64))
            perm += list(range(base + h * 64, base + h * 64 + 32))
    perm += list(range(2048, 3072))
    return np.array(perm)


def make_in_maps(x, W_in, b_in, W_out, b_out_, ew, w1, w2, n_cores=8):
    perm = _perm3072()
    winT = np.ascontiguousarray(W_in[perm].T.astype(np.float32))
    bin_p = np.ascontiguousarray(b_in[perm].astype(np.float32).reshape(24, 128).T)
    woutT = np.ascontiguousarray(W_out.T.astype(np.float32))
    bout = np.ascontiguousarray(b_out_.astype(np.float32)).reshape(1, D)
    CC, SS, M, I = _prep_consts()
    in_maps = []
    for c in range(n_cores):
        batches = [c, c + 8, c + 16, c + 24]
        ewc = (ew[c * T:(c + 1) * T].astype(np.float32) if c < 2
               else np.zeros((T, 2), np.float32))
        ewc = np.ascontiguousarray(
            ewc.reshape(16, 128, 2).transpose(1, 0, 2).reshape(128, 32))
        in_maps.append({
            "x": np.ascontiguousarray(x[batches].astype(np.float32)),
            "winT": winT, "bin_p": bin_p, "woutT": woutT, "bout": bout,
            "cc": CC, "ss": SS, "mskb": M, "ident": I,
            "w1": np.ascontiguousarray(w1.astype(np.float32)),
            "w2": np.ascontiguousarray(w2.astype(np.float32)),
            "ew": np.ascontiguousarray(ewc),
        })
    return in_maps


def kernel(L0_attn_x, L0_attn_W_in, L0_attn_b_in, L0_attn_W_out, L0_attn_b_out,
           L0_ffn_scores, L0_ffn_expert_weights, L0_ffn_top_experts,
           L0_ffn_w1, L0_ffn_w2):
    x = np.asarray(L0_attn_x)
    in_maps = make_in_maps(x, np.asarray(L0_attn_W_in), np.asarray(L0_attn_b_in),
                           np.asarray(L0_attn_W_out), np.asarray(L0_attn_b_out),
                           np.asarray(L0_ffn_expert_weights),
                           np.asarray(L0_ffn_w1), np.asarray(L0_ffn_w2))
    nc = build_program()
    res = run_bass_kernel_spmd(nc, in_maps, core_ids=list(range(8)))
    out = np.empty((32, T, D), np.float32)
    for c in range(8):
        out[[c, c + 8, c + 16, c + 24]] = res.results[c]["out"]
    return out


# version 21
# speedup vs baseline: 1.3073x; 1.0519x over previous
import contextlib
import numpy as np
import ml_dtypes
import concourse.bass as bass
import concourse.mybir as mybir
from concourse.bass_utils import run_bass_kernel_spmd

F32 = mybir.dt.float32
BF16 = mybir.dt.bfloat16
AF = mybir.ActivationFunctionType
ALU = mybir.AluOpType
AX = mybir.AxisListType

T = 2048
D = 1024
NH = 16
NP = NH // 2
INTER = 8192
SHIFT = 20.0
MASKVAL = -30000.0
GC1 = 0.7978845608028654
GC3 = GC1 * 0.044715

ENGS = ("sync", "tensor", "vector", "scalar", "gpsimd")
ENG_SEM = {"sync": None, "tensor": "PE", "vector": "DV", "scalar": "AC", "gpsimd": "GP"}


class Buf:
    __slots__ = ("name", "writes", "reads")

    def __init__(self, name):
        self.name = name
        self.writes = {}
        self.reads = {}


class Sched:
    def __init__(self):
        self.ops = {e: [] for e in ENGS}
        self.cnt = {}
        self.waited = {e: {} for e in ENGS}

    def _wait(self, eng, sem, val):
        w = self.waited[eng]
        if w.get(sem, 0) >= val:
            return
        w[sem] = val
        self.ops[eng].append(("wait", sem, val))

    def wait_tok(self, eng, tok):
        if tok is not None:
            self._wait(eng, tok[0], tok[1])

    def op(self, eng, fn, reads=(), writes=(), sem=None, inc=None, extra_waits=(),
           scalar_reads=()):
        if sem is None:
            sem = ENG_SEM[eng]
            assert sem is not None, f"op on {eng} needs explicit sem"
            if inc is None:
                inc = 1
        own = ENG_SEM[eng]

        def w_(s_, v, force=False):
            if not force and s_ == own and s_ == "PE":
                return
            self._wait(eng, s_, v)
        for b in reads:
            for s_, v in b.writes.items():
                w_(s_, v)
        for b in scalar_reads:
            for s_, v in b.writes.items():
                w_(s_, v, force=True)
        for b in writes:
            for s_, v in b.writes.items():
                w_(s_, v)
            for s_, v in b.reads.items():
                w_(s_, v)
        for tok in extra_waits:
            self.wait_tok(eng, tok)
        self.cnt[sem] = self.cnt.get(sem, 0) + inc
        tok = (sem, self.cnt[sem])
        self.ops[eng].append(("op", fn, sem, inc))
        for b in tuple(reads) + tuple(scalar_reads):
            b.reads[sem] = tok[1]
        for b in writes:
            b.writes[sem] = tok[1]
            b.reads = {}
        return tok

    def raw_wait(self, eng, sem, val):
        self._wait(eng, sem, val)

    def play(self, name, e, sems):
        for item in self.ops[name]:
            if item[0] == "wait":
                _, sem, val = item
                e.wait_ge(sems[sem], val)
            else:
                _, fn, sem, inc = item
                inst = fn(e)
                if sem is not None:
                    assert inst is not None, f"op on {name} returned None ({sem})"
                    inst.then_inc(sems[sem], inc)


def ceil_div(a, b):
    return -(-a // b)


def build_program(n_seqs=4, n_pairs=NP, nch=16, do_ffn=True, n_ic2=16, dbg=False, seq1_phases='lqao', seq0_phases='lqao'):
    nc = bass.Bass("TRN2", target_bir_lowering=False, detect_race_conditions=False)

    x_hbm = nc.dram_tensor("x", [n_seqs, T, D], F32, kind="ExternalInput").ap()
    winT = nc.dram_tensor("winT", [D, 3072], F32, kind="ExternalInput").ap()
    bin_p = nc.dram_tensor("bin_p", [128, 24], F32, kind="ExternalInput").ap()
    woutT_h = nc.dram_tensor("woutT", [D, D], F32, kind="ExternalInput").ap()
    bout_h = nc.dram_tensor("bout", [1, D], F32, kind="ExternalInput").ap()
    cc_h = nc.dram_tensor("cc", [128, T], BF16, kind="ExternalInput").ap()
    ss_h = nc.dram_tensor("ss", [128, T], BF16, kind="ExternalInput").ap()
    mskb_h = nc.dram_tensor("mskb", [128, 128], F32, kind="ExternalInput").ap()
    ident_h = nc.dram_tensor("ident", [128, 128], BF16, kind="ExternalInput").ap()
    w1_h = nc.dram_tensor("w1", [D, INTER], F32, kind="ExternalInput").ap()
    w2_h = nc.dram_tensor("w2", [INTER, D], F32, kind="ExternalInput").ap()
    ew_h = nc.dram_tensor("ew", [128, 32], F32, kind="ExternalInput").ap()
    out_hbm = nc.dram_tensor("out", [n_seqs, T, D], F32, kind="ExternalOutput").ap()
    dbg_out = {}
    if dbg:
        for nm, shape, dt in [("d_xln", [128, 8 * T], BF16), ("d_qt", [128, T], BF16),
                              ("d_kt", [128, T], BF16), ("d_vt", [128, 16 * D], BF16),
                              ("d_bias", [128, 256], F32), ("d_lparts", [128, 8], F32),
                              ("d_lsum", [128, 2], F32), ("d_rl", [128, 2], F32),
                              ("d_pa", [128, T], BF16), ("d_y", [128, 8 * T], BF16)]:
            dbg_out[nm] = nc.dram_tensor(nm, shape, dt, kind="ExternalOutput").ap()
        dbg_out["d_y2"] = nc.dram_tensor("d_y2", [128, 16 * 1024], F32, kind="ExternalOutput").ap()
        dbg_out["d_xln2"] = nc.dram_tensor("d_xln2", [128, 8 * T], BF16, kind="ExternalOutput").ap()

    s = Sched()
    ctx = contextlib.ExitStack()
    sbt = lambda name, shape, dt: ctx.enter_context(nc.sbuf_tensor(name, shape, dt))
    pst = lambda name, shape, dt: ctx.enter_context(nc.psum_tensor(name, shape, dt))

    ccs = sbt("ccs", [128, T], BF16)
    sss = sbt("sss", [128, T], BF16)
    mbias = sbt("mbias", [128, 128], F32)
    ident = sbt("ident_s", [128, 128], BF16)
    bcols = sbt("bcols", [128, 24], F32)
    bout_s = sbt("bout_s", [1, D], BF16)
    ones_r = sbt("ones_r", [1, 128], BF16)
    ones_c = sbt("ones_c", [128, 1], BF16)
    woutT = sbt("woutT_s", [128, 8 * D], BF16)

    XT = sbt("XT", [128, 8 * T], BF16)
    QK = sbt("QK", [128, 2 * NP * T], BF16)
    Qt = QK[:, 0:NP * T]
    Kt = QK[:, NP * T:2 * NP * T]
    Vt = sbt("Vt", [128, 16 * D], BF16)

    xbuf = [sbt(f"xbuf{i}", [128, D], F32) for i in range(2)]
    scr0 = sbt("scr0", [128, D], F32)
    scr1 = sbt("scr1", [128, D], F32)
    stat = sbt("stat", [128, 16], F32)
    xln_c = [sbt(f"xln_c{i}", [128, D], BF16) for i in range(2)]

    wbuf = [sbt(f"wbuf{i}", [128, 8 * 128], BF16) for i in range(2)]
    qc_b = sbt("qc_b", [128, 1024], F32)
    qs_b = sbt("qs_b", [128, 1024], F32)
    qw_b = sbt("qw_b", [128, 1024], F32)

    bias_c = sbt("bias_c", [128, 256], F32)
    lparts = sbt("lparts", [128, 8], F32)
    lsum = sbt("lsum", [128, 2], F32)
    rl = sbt("rl", [128, 2], F32)
    P_A = sbt("P_A", [128, T], BF16)
    P_B = sbt("P_B", [128, T], BF16)
    P_A2 = qs_b[:].bitcast(BF16)
    P_B2 = qc_b[:].bitcast(BF16)
    PT_A = xln_c[0]
    PT_B = xln_c[1]

    sqx = scr0
    xr = xbuf
    rbuf = [scr0, scr1]
    vfm = P_A

    psSA = pst("psSA", [128, 1024], F32)
    psSB = pst("psSB", [128, 1024], F32)
    psTA = pst("psTA", [128, 1024], BF16)
    psTB = pst("psTB", [128, 1024], BF16)
    psY = pst("psY", [128, 512], F32)
    psU = pst("psU", [128, 512], F32)

    B = lambda n: Buf(n)
    b_consts = B("consts")
    b_XT = B("XT")
    b_Qt = [B(f"Qt{t}") for t in range(NP)]
    b_Kt = [B(f"Kt{t}") for t in range(NP)]
    b_Vt = [B(f"Vt{t}") for t in range(NP)]
    b_xbuf = [B(f"xbuf{i}") for i in range(2)]
    b_scr0 = B("scr0")
    b_scr1 = B("scr1")
    b_sqx = b_scr0
    b_stat = [B(f"stat{i}") for i in range(2)]
    b_xln = [B(f"xln{i}") for i in range(2)]
    b_wbuf = [B(f"wbuf{i}") for i in range(2)]
    b_qc = B("qc"); b_qs = B("qs"); b_qw = B("qw")
    b_qc2 = [B("qc_h0"), B("qc_h1")]
    b_qs2 = [B("qs_h0"), B("qs_h1")]
    b_qw2 = [B("qw_h0"), B("qw_h1")]
    _blPA2 = (B("P_A2"), b_qs2[0], b_qs2[1])
    _blPB2 = (B("P_B2"), b_qc2[0], b_qc2[1])
    b_lparts = B("lparts"); b_lsum = B("lsum"); b_rl = B("rl")
    bl_P = [[(B("P_A"),), (B("P_B"),)], [_blPA2, _blPB2]]
    b_vfm = bl_P[0][0][0]
    b_PT = [b_xln[0], b_xln[1]]
    bias_all = bias_c
    b_biasP = [B(f"bias{p}") for p in range(NP)]
    b_xr = b_xbuf
    b_rbuf = [b_scr0, b_scr1]
    b_SA = [B("psSA0"), B("psSA1")]
    b_SB = [B("psSB0"), B("psSB1")]
    b_psT = [B("psTA"), B("psTB")]
    b_psY = B("psY")
    b_psU = [B("psU0"), B("psU1")]
    b_out0 = B("out0_hbm")

    def dump(nm, ap, bufs):
        if not dbg:
            return
        s.op("sync", lambda e, nm=nm, ap=ap: e.dma_start(out=dbg_out[nm], in_=ap),
             reads=tuple(bufs), sem="ST", inc=16)

    for dst, src in [(ccs, cc_h), (sss, ss_h), (mbias, mskb_h), (ident, ident_h)]:
        s.op("sync", lambda e, d=dst, so=src: e.dma_start(out=d[:], in_=so),
             writes=(b_consts,), sem="LD", inc=16)
    s.op("sync", lambda e: e.dma_start(out=bcols[:], in_=bin_p),
         writes=(b_consts,), sem="LD", inc=16)
    s.op("gpsimd", lambda e: e.dma_start(out=bout_s[:], in_=bout_h),
         writes=(b_consts,), sem="GD", inc=16)
    s.op("gpsimd", lambda e: e.dma_start(
        out=woutT[:].rearrange("p (t d) -> p t d", t=8),
        in_=woutT_h.rearrange("(t p) d -> p t d", p=128)),
        writes=(b_consts,), sem="GD", inc=16)
    s.op("gpsimd", lambda e: e.memset(ones_r[:], 1.0), writes=(b_consts,))
    s.op("gpsimd", lambda e: e.memset(ones_c[:], 1.0), writes=(b_consts,))
    s.op("gpsimd", lambda e: e.memset(stat[:], 0.0), writes=(b_stat[0], b_stat[1]))
    if dbg:
        s.op("gpsimd", lambda e: e.memset(XT[:], 0.0), writes=(b_XT,))
        s.op("gpsimd", lambda e: e.memset(QK[:], 0.0), writes=tuple(b_Qt + b_Kt))
        s.op("gpsimd", lambda e: e.memset(Vt[:], 0.0), writes=tuple(b_Vt))
        s.op("gpsimd", lambda e: e.memset(P_A[:], 0.0), writes=(bl_P[0][0][0],))
        s.op("gpsimd", lambda e: e.memset(bias_c[:], 0.0), writes=tuple(b_biasP))
        s.op("gpsimd", lambda e: e.memset(lparts[:], 0.0), writes=(b_lparts,))
        s.op("gpsimd", lambda e: e.memset(lsum[:], 0.0), writes=(b_lsum,))
        s.op("gpsimd", lambda e: e.memset(rl[:], 0.0), writes=(b_rl,))

    def phase_ln(src_hbm, dst_XT, b_dst, src_dep=None):
        def ln_s1(tc):
            xb, bx = xbuf[tc % 2], b_xbuf[tc % 2]
            o = (tc % 2) * 8
            bst = b_stat[tc % 2]
            s.op("sync", lambda e, xb=xb, tc=tc: e.dma_start(
                out=xb[:], in_=src_hbm[tc * 128:(tc + 1) * 128, :]),
                reads=(src_dep,) if src_dep is not None else (),
                writes=(bx,), sem="LD", inc=16)
            s.op("vector", lambda e, xb=xb, o=o: e.tensor_reduce(
                stat[:, o:o + 1], xb[:], AX.X, ALU.add), reads=(bx,), writes=(bst,))
            s.op("scalar", lambda e, xb=xb, o=o: e.activation(
                sqx[:], xb[:], AF.Square, accum_out=stat[:, o + 1:o + 2]),
                reads=(bx, bst), writes=(b_sqx, bst))
            s.op("scalar", lambda e, o=o: e.copy(sqx[:, 0:1], stat[:, o + 1:o + 2]),
                 reads=(bst,), writes=(bst, b_sqx))

        def ln_s23(tc):
            xb, bx = xbuf[tc % 2], b_xbuf[tc % 2]
            o = (tc % 2) * 8
            bst = b_stat[tc % 2]
            xlc, bxl = xln_c[tc % 2], b_xln[tc % 2]
            s.op("vector", lambda e, o=o: e.tensor_scalar(
                stat[:, o + 2:o + 3], stat[:, o:o + 1], 1.0 / D, None, ALU.mult),
                reads=(bst,), writes=(bst,))
            s.op("vector", lambda e, o=o: e.tensor_scalar(
                stat[:, o + 3:o + 4], stat[:, o + 2:o + 3], stat[:, o + 2:o + 3],
                1e-5, ALU.mult, ALU.subtract), reads=(), scalar_reads=(bst,),
                writes=(bst,))
            s.op("vector", lambda e, o=o: e.scalar_tensor_tensor(
                stat[:, o + 4:o + 5], stat[:, o + 1:o + 2], 1.0 / D, stat[:, o + 3:o + 4],
                ALU.mult, ALU.subtract), reads=(bst,), writes=(bst,))
            s.op("scalar", lambda e, o=o: e.activation(
                stat[:, o + 5:o + 6], stat[:, o + 4:o + 5], AF.Ln),
                reads=(bst,), writes=(bst,))
            s.op("scalar", lambda e, o=o: e.activation(
                stat[:, o + 6:o + 7], stat[:, o + 5:o + 6], AF.Exp, scale=-0.5),
                reads=(bst,), writes=(bst,))
            s.op("vector", lambda e, o=o: e.tensor_scalar(
                stat[:, o + 7:o + 8], stat[:, o + 2:o + 3], -1.0, None, ALU.mult),
                reads=(bst,), writes=(bst,))
            s.op("vector", lambda e, xb=xb, xlc=xlc, o=o: e.tensor_scalar(
                xlc[:], xb[:], stat[:, o + 7:o + 8], stat[:, o + 6:o + 7],
                ALU.add, ALU.mult), reads=(bx,), scalar_reads=(bst,), writes=(bxl,))
            for dc in range(8):
                s.op("tensor", lambda e, dc=dc, xlc=xlc: e.transpose(
                    psTA[:, dc * 128:(dc + 1) * 128], xlc[:, dc * 128:(dc + 1) * 128],
                    ident[:]), reads=(bxl, b_consts), writes=(b_psT[0],))
            s.op("vector", lambda e, tc=tc: e.tensor_copy(
                dst_XT[:].rearrange("p (dc t) -> p dc t", dc=8)[:, :, tc * 128:(tc + 1) * 128],
                psTA[:].rearrange("p (dc t) -> p dc t", dc=8)),
                reads=(b_psT[0],), writes=(b_dst,))

        ln_s1(0)
        for tc in range(nch):
            if tc + 1 < nch:
                ln_s1(tc + 1)
            ln_s23(tc)

    def phase_qkv_start(si):
        ntc = nch // 4
        fc_order = []
        for t_ in range(n_pairs):
            fc_order += [t_, 8 + t_, 16 + t_]
        nfc = len(fc_order)
        state = {"fc_order": fc_order, "nfc": nfc, "ntc": ntc,
                 "pending": [None]}

        def wload(step):
            fc = fc_order[step]
            wb, bw = wbuf[step % 2], b_wbuf[step % 2]
            s.op("gpsimd", lambda e, fc=fc, wb=wb: e.dma_start(
                out=wb[:].rearrange("p (dc n) -> p dc n", dc=8),
                in_=winT[:, fc * 128:(fc + 1) * 128].rearrange("(dc p) n -> p dc n", p=128)),
                writes=(bw,), sem="GD", inc=16)

        state["wload"] = wload
        wload(0)
        if nfc > 1:
            wload(1)
        return state

    def phase_qkv_pair(si, pair_idx, state):
        fc_order, nfc, ntc = state["fc_order"], state["nfc"], state["ntc"]
        wload = state["wload"]
        pending_add = state["pending"]

        def flush_add():
            if pending_add[0] is not None:
                pending_add[0]()
                pending_add[0] = None

        for step in range(pair_idx * 3, pair_idx * 3 + 3):
            fc = fc_order[step]
            wb, bw = wbuf[step % 2], b_wbuf[step % 2]
            is_v = fc >= 16
            pair = fc % 8
            for tc4 in range(ntc):
                bankT, bb = (psSA, b_SA) if tc4 % 2 == 0 else (psSB, b_SB)
                bank, bbank = bankT[:, 0:512], bb[0]
                loc = (tc4 % 2) * 512
                for dc in range(8):
                    s.op("tensor", lambda e, dc=dc, tc4=tc4, wb=wb, bank=bank: e.matmul(
                        bank, wb[:, dc * 128:(dc + 1) * 128],
                        XT[:, dc * T + tc4 * 512: dc * T + (tc4 + 1) * 512],
                        start=(dc == 0), stop=(dc == 7)),
                        reads=(bw, b_XT), writes=(bbank,) if dc in (0, 7) else ())
                if tc4 == ntc - 1 and step + 2 < nfc:
                    wload(step + 2)
                if is_v:
                    s.op("vector", lambda e, tc4=tc4, fc=fc, bank=bank: e.tensor_scalar(
                        vfm[:, tc4 * 512:(tc4 + 1) * 512], bank,
                        bcols[:, fc:fc + 1], None, ALU.add),
                        reads=(bbank, b_consts), writes=(b_vfm,))
                else:
                    flush_add()
                    bqc = b_qc2[tc4 % 2]
                    bqs = b_qs2[tc4 % 2]
                    bqw = b_qw2[tc4 % 2]
                    s.op("vector", lambda e, tc4=tc4, fc=fc, bank=bank, loc=loc: e.scalar_tensor_tensor(
                        qc_b[:, loc:loc + 512], bank,
                        bcols[:, fc:fc + 1], ccs[:, tc4 * 512:(tc4 + 1) * 512],
                        ALU.add, ALU.mult), reads=(bbank, b_consts), writes=(bqc,))
                    s.op("vector", lambda e, tc4=tc4, fc=fc, bank=bank, loc=loc: e.scalar_tensor_tensor(
                        qs_b[:, loc:loc + 512], bank,
                        bcols[:, fc:fc + 1], sss[:, tc4 * 512:(tc4 + 1) * 512],
                        ALU.add, ALU.mult), reads=(bbank, b_consts), writes=(bqs,))
                    for g in range(4):
                        src_g = [1, 0, 3, 2][g]
                        eng_c = "vector" if g < 2 else "scalar"
                        cp_f = ((lambda e, g=g, src_g=src_g, loc=loc: e.tensor_copy(
                                    qw_b[g * 32:(g + 1) * 32, loc:loc + 512],
                                    qs_b[src_g * 32:(src_g + 1) * 32, loc:loc + 512]))
                                if g < 2 else
                                (lambda e, g=g, src_g=src_g, loc=loc: e.copy(
                                    qw_b[g * 32:(g + 1) * 32, loc:loc + 512],
                                    qs_b[src_g * 32:(src_g + 1) * 32, loc:loc + 512])))
                        s.op(eng_c, cp_f, reads=(bqs,), writes=(bqw,))
                    dst, bdst = (Qt, b_Qt[pair]) if fc < 8 else (Kt, b_Kt[pair])
                    eng = "vector" if fc < 8 else "gpsimd"

                    def mk_add(dst=dst, bdst=bdst, pair=pair, tc4=tc4, loc=loc,
                               bqc=bqc, bqw=bqw, eng=eng):
                        def do():
                            s.op(eng, lambda e: e.tensor_tensor(
                                dst[:, pair * T + tc4 * 512: pair * T + tc4 * 512 + 512],
                                qc_b[:, loc:loc + 512], qw_b[:, loc:loc + 512], ALU.add),
                                reads=(bqc, bqw), writes=(bdst,))
                        return do
                    pending_add[0] = mk_add()
            if is_v:
                flush_add()
                for half in range(ceil_div(nch, 8)):
                    w8 = min(8, nch - half * 8)
                    for j in range(w8):
                        tcv = half * 8 + j
                        s.op("tensor", lambda e, tcv=tcv, j=j: e.transpose(
                            psTB[:, j * 128:(j + 1) * 128], vfm[:, tcv * 128:(tcv + 1) * 128],
                            ident[:]), reads=(b_vfm, b_consts), writes=(b_psT[1],))
                    s.op("vector", lambda e, half=half, pair=pair, w8=w8: e.tensor_copy(
                        Vt[:].rearrange("p (tc hv) -> p tc hv", hv=1024)
                          [:, half * 8:half * 8 + w8, pair * 128:(pair + 1) * 128],
                        psTB[:, 0:w8 * 128].rearrange("p (j n) -> p j n", j=w8)),
                        reads=(b_psT[1],), writes=(b_Vt[pair],))
                nqb = nch
                w0 = min(1024, nch * 128)
                s.op("scalar", lambda e, pair=pair, w0=w0: e.activation(
                    PT_A[:, 0:w0], Qt[:, pair * T: pair * T + w0], AF.Square),
                    reads=(b_Qt[pair],), writes=(b_PT[0],))
                if nch * 128 > 1024:
                    s.op("scalar", lambda e, pair=pair: e.activation(
                        PT_B[:, 0:nch * 128 - 1024],
                        Qt[:, pair * T + 1024: pair * T + nch * 128], AF.Square),
                        reads=(b_Qt[pair],), writes=(b_PT[1],))
                for hh in range(2):
                    bank = psU if hh == 0 else psY
                    bb = b_psU[0] if hh == 0 else b_psY
                    for qb in range(nqb):
                        sq_t = PT_A if qb < 8 else PT_B
                        bsq = b_PT[0] if qb < 8 else b_PT[1]
                        s.op("tensor", lambda e, qb=qb, hh=hh, bank=bank, sq_t=sq_t: e.matmul(
                            bank[:, qb:qb + 1],
                            sq_t[64 * hh:64 * hh + 64, (qb % 8) * 128:(qb % 8) * 128 + 128],
                            ones_c[64 * hh:64 * hh + 64, :],
                            start=(qb == 0), stop=(qb == nqb - 1)),
                            reads=(bsq, b_consts),
                            writes=(bb,) if qb in (0, nqb - 1) else ())
                for hh in range(2):
                    bank = psU if hh == 0 else psY
                    bb = b_psU[0] if hh == 0 else b_psY
                    s.op("vector", lambda e, hh=hh, bank=bank, pair=pair, nqb=nqb: e.tensor_scalar(
                        bias_all[:, pair * 32 + hh * 16:pair * 32 + hh * 16 + nqb],
                        bank[:, 0:nqb], -0.125, -SHIFT, ALU.mult, ALU.add),
                        reads=(bb,), writes=(b_biasP[pair],))
                flush_add()
        flush_add()
    def phase_attn_pair(si, pair):
        nqb = nch
        if True:
            def emit_S_mm(qb, kc, pair=pair):
                L = (qb + 1) * 128
                w = min(512, L - kc * 512)
                for hh in (1, 0):
                    bankT, bbs = (psSA, b_SA) if hh == 0 else (psSB, b_SB)
                    bbank = bbs[kc % 2]
                    bank = bankT[:, (kc % 2) * 512:(kc % 2) * 512 + w]
                    rows = slice(64 * hh, 64 * hh + 64)
                    s.op("tensor", lambda e, rows=rows, qb=qb, kc=kc, w=w, bank=bank: e.matmul(
                        bank, Qt[rows, pair * T + qb * 128: pair * T + (qb + 1) * 128],
                        Kt[rows, pair * T + kc * 512: pair * T + kc * 512 + w],
                        start=True, stop=True),
                        reads=(b_Qt[pair], b_Kt[pair]), writes=(bbank,))

            def stage_S(qb, pair=pair):
                nkc = ceil_div((qb + 1) * 128, 512)
                for kc in range(min(2, nkc)):
                    emit_S_mm(qb, kc)

            def softmax_stage(qb, pair=pair):
                L = (qb + 1) * 128
                nkc = ceil_div(L, 512)
                ngr = ceil_div(L, 1024)
                par = qb % 2
                PbA = P_A if par == 0 else P_A2
                PbB = P_B if par == 0 else P_B2
                for g in range(ngr):
                    w = min(1024, L - g * 1024)
                    kc_hi = min(2 * g + 1, nkc - 1)
                    for hh in range(2):
                        bankT, bbs = (psSA, b_SA) if hh == 0 else (psSB, b_SB)
                        bb0, bb1 = bbs[0], bbs[1]
                        if kc_hi == nkc - 1:
                            moff = qb * 128 - g * 1024
                            s.op("vector", lambda e, bankT=bankT, moff=moff: e.tensor_tensor(
                                bankT[:, moff:moff + 128], bankT[:, moff:moff + 128],
                                mbias[:], ALU.add),
                                reads=(bb0, bb1, b_consts),
                                writes=(bb0,) if moff < 512 else (bb1,))
                        s.op("scalar", lambda e, bankT=bankT, hh=hh, g=g, w=w, qb=qb, PbA=PbA, PbB=PbB: e.activation(
                            (PbA if hh == 0 else PbB)[:, g * 1024:g * 1024 + w],
                            bankT[:, 0:w], AF.Exp, scale=0.125,
                            bias=bias_all[:, pair * 32 + hh * 16 + qb:pair * 32 + hh * 16 + qb + 1],
                            accum_out=lparts[:, hh * 4 + g:hh * 4 + g + 1]),
                            reads=(bb0, bb1, b_biasP[pair], b_lparts),
                            writes=bl_P[par][hh] + (b_lparts,))
                    if 2 * g + 2 < nkc:
                        emit_S_mm(qb, 2 * g + 2)
                    if 2 * g + 3 < nkc:
                        emit_S_mm(qb, 2 * g + 3)
                s.op("scalar", lambda e: e.copy(lsum[:, 0:1], lparts[:, 0:1]),
                     reads=(b_lparts,), writes=(b_lparts, b_lsum))
                s.op("vector", lambda e, ngr=ngr: e.tensor_reduce(
                    lsum[:, 0:2], lparts[:].rearrange("p (h k) -> p h k", h=2)[:, :, 0:ngr],
                    AX.X, ALU.add), reads=(b_lparts,), writes=(b_lsum,))
                s.op("vector", lambda e: e.reciprocal(rl[:, 0:2], lsum[:, 0:2]),
                     reads=(b_lsum,), writes=(b_rl,))
                for hh in range(2):
                    Pbuf = PbA if hh == 0 else PbB
                    s.op("vector", lambda e, hh=hh, Pbuf=Pbuf, L=L: e.tensor_scalar(
                        Pbuf[:, 0:L], Pbuf[:, 0:L], rl[:, hh:hh + 1], None, ALU.mult),
                        reads=bl_P[par][hh], scalar_reads=(b_rl,),
                        writes=bl_P[par][hh])

            def pv_stage(qb, pair=pair):
                L = (qb + 1) * 128
                npk = ceil_div(L, 1024)
                par = qb % 2
                PbA = P_A if par == 0 else P_A2
                PbB = P_B if par == 0 else P_B2
                for pk in range(npk):
                    wpk = min(8, L // 128 - pk * 8)
                    for hh in range(2):
                        Pbuf = PbA if hh == 0 else PbB
                        psT = psTA if hh == 0 else psTB
                        for j in range(wpk):
                            kcb = pk * 8 + j
                            s.op("tensor", lambda e, j=j, kcb=kcb, Pbuf=Pbuf, psT=psT: e.transpose(
                                psT[:, j * 128:(j + 1) * 128],
                                Pbuf[:, kcb * 128:(kcb + 1) * 128], ident[:]),
                                reads=bl_P[par][hh] + (b_consts,), writes=(b_psT[hh],))
                    for hh in range(2):
                        psT = psTA if hh == 0 else psTB
                        PTs = PT_A if hh == 0 else PT_B
                        s.op("vector", lambda e, psT=psT, PTs=PTs, wpk=wpk: e.tensor_copy(
                            PTs[:, 0:wpk * 128], psT[:, 0:wpk * 128]),
                            reads=(b_psT[hh],), writes=(b_PT[hh],))
                    for hh in range(2):
                        PTs = PT_A if hh == 0 else PT_B
                        ybase = 64 * hh
                        for j in range(wpk):
                            kcb = pk * 8 + j
                            s.op("tensor", lambda e, j=j, kcb=kcb, hh=hh, PTs=PTs, ybase=ybase, L=L: e.matmul(
                                psY[ybase:ybase + 64, 0:128],
                                Vt[:, kcb * 1024 + (2 * pair + hh) * 64: kcb * 1024 + (2 * pair + hh) * 64 + 64],
                                PTs[:, j * 128:(j + 1) * 128],
                                start=(kcb == 0), stop=(kcb == L // 128 - 1),
                                skip_group_check=True),
                                reads=(b_PT[hh], b_Vt[pair]),
                                writes=(b_psY,) if kcb in (0, L // 128 - 1) else ())
                s.op("vector", lambda e, qb=qb: e.tensor_copy(
                    Qt[:, pair * T + qb * 128: pair * T + (qb + 1) * 128], psY[:, 0:128]),
                    reads=(b_psY,), writes=(b_Qt[pair],))

            stage_S(0)
            softmax_stage(0)
            if nqb > 1:
                stage_S(1)
            for qb in range(1, nqb):
                softmax_stage(qb)
                if qb + 1 < nqb:
                    stage_S(qb + 1)
                pv_stage(qb - 1)
            pv_stage(nqb - 1)

    def phase_out(si):
        for tb in range(nch):
            xb, bx = xr[tb % 2], b_xr[tb % 2]
            rb, br = rbuf[tb % 2], b_rbuf[tb % 2]
            s.op("sync", lambda e, xb=xb, si=si, tb=tb: e.dma_start(
                out=xb[:], in_=x_hbm[si, tb * 128:(tb + 1) * 128, :]),
                writes=(bx,), sem="LD", inc=16)
            for dh in range(2):
                bankT, bbs = (psSA, b_SA) if dh == 0 else (psSB, b_SB)
                bank, bbank = bankT[:, 0:512], bbs[0]
                for t_ in range(n_pairs):
                    s.op("tensor", lambda e, t_=t_, tb=tb, dh=dh, bank=bank: e.matmul(
                        bank, Qt[:, t_ * T + tb * 128: t_ * T + tb * 128 + 128],
                        woutT[:, t_ * D + dh * 512: t_ * D + dh * 512 + 512],
                        start=(t_ == 0), stop=False),
                        reads=(b_Qt[t_], b_consts), writes=(bbank,) if t_ == 0 else ())
                s.op("tensor", lambda e, dh=dh, bank=bank: e.matmul(
                    bank, ones_r[0:1, :], bout_s[0:1, dh * 512:dh * 512 + 512],
                    start=False, stop=True), reads=(b_consts,), writes=(bbank,))
                s.op("vector", lambda e, rb=rb, xb=xb, dh=dh, bank=bank: e.tensor_tensor(
                    rb[:, dh * 512:dh * 512 + 512], bank,
                    xb[:, dh * 512:dh * 512 + 512], ALU.add),
                    reads=(bbank, bx), writes=(br,))
            tok = s.op("sync", lambda e, rb=rb, si=si, tb=tb: e.dma_start(
                out=out_hbm[si, tb * 128:(tb + 1) * 128, :], in_=rb[:]),
                reads=(br,), writes=(b_out0,) if si == 0 else (), sem="ST", inc=16)
        return tok

    def phase_ffn():
        ew_ap = bias_c[:, 0:32]
        sv = stat[:, 0:16]
        w1b = [Vt[:, 0:4096], Vt[:, 4096:8192]]
        w2b = [Vt[:, 8192:12288], Vt[:, 12288:16384]]
        y2 = QK[:].bitcast(F32)
        sq_g = qc_b
        u_g = qs_b
        t_g = qw_b
        h_sb = PT_A[:]
        b_ew = B("ew")
        b_sv = b_PT[1]
        b_w1b = [b_Vt[0], b_Vt[1]]
        b_w2b = [b_Vt[2], b_Vt[3]]
        b_y2 = B("y2")
        b_g = [B(f"g{i}") for i in range(6)]
        b_psH = [b_SA[0], b_SA[1], b_SB[0], b_SB[1]]
        b_psO = [b_psY, b_psU[0]]

        s.op("sync", lambda e: e.dma_start(out=ew_ap, in_=ew_h),
            writes=tuple([b_ew] + b_biasP), sem="LD", inc=16)
        for _mi in range(4):
            s.op("gpsimd", lambda e, _mi=_mi: e.memset(y2[:, _mi * 4096:(_mi + 1) * 4096], 0.0),
                 writes=tuple([b_y2] + b_Qt + b_Kt))
        import os as _os
        _stage = int(_os.environ.get("FFN_STAGE", "4"))
        if _stage <= 1:
            return

        phase_ln(out_hbm[0], XT, b_XT, src_dep=b_out0)
        dump("d_xln2", XT[:], [b_XT])
        if _stage <= 2:
            return
        s.op("vector", lambda e: e.tensor_tensor(
            sv, ew_ap.rearrange("p (c k) -> p c k", k=2)[:, :, 0],
            ew_ap.rearrange("p (c k) -> p c k", k=2)[:, :, 1], ALU.add),
            reads=(b_ew,), writes=(b_sv, b_stat[0], b_stat[1]))

        def ffn_mm1(t):
            ic2, tw = divmod(t, nch // 2)
            wb1, bw1 = w1b[ic2 % 2], b_w1b[ic2 % 2]
            if tw == 0:
                s.op("gpsimd", lambda e, ic2=ic2, wb1=wb1: e.dma_start(
                    out=wb1.rearrange("p (dc i) -> p dc i", dc=8),
                    in_=w1_h[:, ic2 * 512:(ic2 + 1) * 512].rearrange("(dc p) i -> p dc i", p=128)),
                    writes=(bw1,), sem="GD", inc=16)
                wb2, bw2 = w2b[ic2 % 2], b_w2b[ic2 % 2]
                s.op("gpsimd", lambda e, ic2=ic2, wb2=wb2: e.dma_start(
                    out=wb2.rearrange("p (i4 d) -> p i4 d", i4=4),
                    in_=w2_h[ic2 * 512:(ic2 + 1) * 512, :].rearrange("(i4 p) d -> p i4 d", p=128)),
                    writes=(bw2,), sem="GD", inc=16)
            for icsub in range(4):
                bankT = (psSA, psSA, psSB, psSB)[icsub]
                bank = bankT[:, (icsub % 2) * 512:(icsub % 2) * 512 + 256]
                for dc in range(8):
                    s.op("tensor", lambda e, icsub=icsub, dc=dc, tw=tw, wb1=wb1, bank=bank: e.matmul(
                        bank, wb1[:, dc * 512 + icsub * 128: dc * 512 + (icsub + 1) * 128],
                        XT[:, dc * T + tw * 256:dc * T + tw * 256 + 256],
                        start=(dc == 0), stop=(dc == 7)),
                        reads=(bw1, b_XT), writes=(b_psH[icsub],) if dc in (0, 7) else ())

        def ffn_gelu(t):
            h_sb = (PT_A if t % 2 == 0 else PT_B)[:]
            bh = b_PT[t % 2]
            for half, bankT, rd in ((0, psSA, b_psH[0:2]), (1, psSB, b_psH[2:4])):
                hv = bankT[:].rearrange("p (i w) -> p i w", i=2)[:, :, 0:256]
                s.op("scalar", lambda e, hv=hv, half=half: e.activation(
                    sq_g[:, half * 512:half * 512 + 512].rearrange("p (i w) -> p i w", i=2),
                    hv, AF.Square), reads=tuple(rd), writes=(b_g[0],) + bl_P[1][1])
            s.op("vector", lambda e: e.tensor_scalar(
                sq_g[:], sq_g[:], GC3, GC1, ALU.mult, ALU.add),
                reads=(b_g[0],), writes=(b_g[1],))
            for half, bankT, rd in ((0, psSA, b_psH[0:2]), (1, psSB, b_psH[2:4])):
                hv = bankT[:].rearrange("p (i w) -> p i w", i=2)[:, :, 0:256]
                s.op("vector", lambda e, hv=hv, half=half: e.tensor_tensor(
                    u_g[:, half * 512:half * 512 + 512].rearrange("p (i w) -> p i w", i=2),
                    sq_g[:, half * 512:half * 512 + 512].rearrange("p (i w) -> p i w", i=2),
                    hv, ALU.mult),
                    reads=(b_g[1],) + tuple(rd), writes=(b_g[2],) + bl_P[1][0])
            s.op("scalar", lambda e: e.activation(t_g[:], u_g[:], AF.Tanh),
                reads=(b_g[2],), writes=(b_g[3], b_qw2[0], b_qw2[1]))
            s.op("vector", lambda e: e.tensor_scalar(
                t_g[:], t_g[:], 0.5, 0.5, ALU.mult, ALU.add),
                reads=(b_g[3],), writes=(b_g[4],))
            for half, bankT, rd in ((0, psSA, b_psH[0:2]), (1, psSB, b_psH[2:4])):
                hv = bankT[:].rearrange("p (i w) -> p i w", i=2)[:, :, 0:256]
                s.op("vector", lambda e, hv=hv, half=half, h_sb=h_sb: e.tensor_tensor(
                    h_sb[:, half * 512:half * 512 + 512].rearrange("p (i w) -> p i w", i=2),
                    t_g[:, half * 512:half * 512 + 512].rearrange("p (i w) -> p i w", i=2),
                    hv, ALU.mult),
                    reads=(b_g[4],) + tuple(rd), writes=(bh,))

        def ffn_mm2(t):
            ic2, tw = divmod(t, nch // 2)
            wb2, bw2 = w2b[ic2 % 2], b_w2b[ic2 % 2]
            h_sb = (PT_A if t % 2 == 0 else PT_B)[:]
            bh = b_PT[t % 2]
            for dh in range(2):
                for tb2 in range(2):
                    psO = psY if tb2 == 0 else psU
                    bO = b_psO[tb2]
                    for icsub in range(4):
                        s.op("tensor", lambda e, icsub=icsub, dh=dh, tb2=tb2, psO=psO, wb2=wb2, h_sb=h_sb: e.matmul(
                            psO[:, 0:512],
                            h_sb[:, icsub * 256 + tb2 * 128: icsub * 256 + tb2 * 128 + 128],
                            wb2[:, icsub * 1024 + dh * 512: icsub * 1024 + dh * 512 + 512],
                            start=(icsub == 0), stop=(icsub == 3)),
                            reads=(bh, bw2), writes=(bO,) if icsub in (0, 3) else ())
                    s.op("vector", lambda e, tw=tw, dh=dh, tb2=tb2, psO=psO: e.tensor_tensor(
                        y2[:, (tw * 2 + tb2) * 1024 + dh * 512: (tw * 2 + tb2) * 1024 + dh * 512 + 512],
                        y2[:, (tw * 2 + tb2) * 1024 + dh * 512: (tw * 2 + tb2) * 1024 + dh * 512 + 512],
                        psO[:, 0:512], ALU.add),
                        reads=(b_psO[tb2], b_y2), writes=(b_y2,))

        n_iter = n_ic2 * (nch // 2)
        ffn_mm1(0)
        ffn_gelu(0)
        for t in range(1, n_iter + 1):
            if t < n_iter:
                ffn_mm1(t)
            ffn_mm2(t - 1)
            if t < n_iter:
                ffn_gelu(t)
        dump("d_y2", y2, [b_y2])
        for tb in range(nch):
            xb, bx = xr[tb % 2], b_xr[tb % 2]
            rb, br = rbuf[tb % 2], b_rbuf[tb % 2]
            s.op("sync", lambda e, xb=xb, tb=tb: e.dma_start(
                out=xb[:], in_=out_hbm[0, tb * 128:(tb + 1) * 128, :]),
                reads=(b_out0,), writes=(bx,), sem="LD", inc=16)
            s.op("vector", lambda e, rb=rb, xb=xb, tb=tb: e.scalar_tensor_tensor(
                rb[:], y2[:, tb * 1024:(tb + 1) * 1024], sv[:, tb:tb + 1], xb[:],
                ALU.mult, ALU.add), reads=(b_y2, bx), scalar_reads=(b_sv,),
                writes=(br,))
            s.op("sync", lambda e, rb=rb, tb=tb: e.dma_start(
                out=out_hbm[0, tb * 128:(tb + 1) * 128, :], in_=rb[:]),
                reads=(br,), writes=(b_out0,), sem="ST", inc=16)

    for si in range(n_seqs):
        ph = seq1_phases if si > 0 else seq0_phases
        if 'l' not in ph:
            continue
        phase_ln(x_hbm[si], XT, b_XT)
        if 'q' not in ph:
            continue
        if 'a' in ph:
            qkv_state = phase_qkv_start(si)
            for pair in range(n_pairs):
                phase_qkv_pair(si, pair, qkv_state)
                if pair >= 1:
                    phase_attn_pair(si, pair - 1)
            phase_attn_pair(si, n_pairs - 1)
        else:
            qkv_state = phase_qkv_start(si)
            for pair in range(n_pairs):
                phase_qkv_pair(si, pair, qkv_state)
        if 'a' not in ph or 'o' not in ph:
            continue
        phase_out(si)
    if do_ffn:
        phase_ffn()

    if "ST" in s.cnt:
        s.raw_wait("sync", "ST", s.cnt["ST"])

    sems = {}
    for name in ["LD", "SW", "ST", "GD", "PE", "DV", "AC", "GP"]:
        sems[name] = ctx.enter_context(nc.semaphore(name))
    with nc.Block() as blk:
        @blk.sync
        def _(e):
            s.play("sync", e, sems)

        @blk.tensor
        def _(e):
            s.play("tensor", e, sems)

        @blk.vector
        def _(e):
            s.play("vector", e, sems)

        @blk.scalar
        def _(e):
            s.play("scalar", e, sems)

        @blk.gpsimd
        def _(e):
            s.play("gpsimd", e, sems)

    ctx.close()
    return nc



def _prep_consts():
    half = 32
    p = np.arange(half, dtype=np.float64) / (half - 1)
    thetas = 1.0 * (10000.0 / 1.0) ** p
    pos = np.arange(T, dtype=np.float64)
    ang = pos[None, :] / thetas[:, None]
    cos32 = np.cos(ang).astype(np.float32)
    sin32 = np.sin(ang).astype(np.float32)
    CC = np.tile(cos32, (4, 1)).astype(ml_dtypes.bfloat16)
    SS = np.concatenate([-sin32, sin32, -sin32, sin32], axis=0).astype(ml_dtypes.bfloat16)
    i = np.arange(128)
    M = np.where(i[None, :] <= i[:, None], 0.0, MASKVAL).astype(np.float32)
    I = np.eye(128, dtype=np.float32).astype(ml_dtypes.bfloat16)
    return CC, SS, M, I


def _perm3072():
    perm = []
    for sec in range(2):
        base = sec * 1024
        for h in range(NH):
            perm += list(range(base + h * 64 + 32, base + h * 64 + 64))
            perm += list(range(base + h * 64, base + h * 64 + 32))
    perm += list(range(2048, 3072))
    return np.array(perm)


def make_in_maps(x, W_in, b_in, W_out, b_out_, ew, w1, w2, n_cores=8):
    perm = _perm3072()
    winT = np.ascontiguousarray(W_in[perm].T.astype(np.float32))
    bin_p = np.ascontiguousarray(b_in[perm].astype(np.float32).reshape(24, 128).T)
    woutT = np.ascontiguousarray(W_out.T.astype(np.float32))
    bout = np.ascontiguousarray(b_out_.astype(np.float32)).reshape(1, D)
    CC, SS, M, I = _prep_consts()
    in_maps = []
    for c in range(n_cores):
        batches = [c, c + 8, c + 16, c + 24]
        ewc = (ew[c * T:(c + 1) * T].astype(np.float32) if c < 2
               else np.zeros((T, 2), np.float32))
        ewc = np.ascontiguousarray(
            ewc.reshape(16, 128, 2).transpose(1, 0, 2).reshape(128, 32))
        in_maps.append({
            "x": np.ascontiguousarray(x[batches].astype(np.float32)),
            "winT": winT, "bin_p": bin_p, "woutT": woutT, "bout": bout,
            "cc": CC, "ss": SS, "mskb": M, "ident": I,
            "w1": np.ascontiguousarray(w1.astype(np.float32)),
            "w2": np.ascontiguousarray(w2.astype(np.float32)),
            "ew": np.ascontiguousarray(ewc),
        })
    return in_maps


def kernel(L0_attn_x, L0_attn_W_in, L0_attn_b_in, L0_attn_W_out, L0_attn_b_out,
           L0_ffn_scores, L0_ffn_expert_weights, L0_ffn_top_experts,
           L0_ffn_w1, L0_ffn_w2):
    x = np.asarray(L0_attn_x)
    in_maps = make_in_maps(x, np.asarray(L0_attn_W_in), np.asarray(L0_attn_b_in),
                           np.asarray(L0_attn_W_out), np.asarray(L0_attn_b_out),
                           np.asarray(L0_ffn_expert_weights),
                           np.asarray(L0_ffn_w1), np.asarray(L0_ffn_w2))
    nc = build_program()
    res = run_bass_kernel_spmd(nc, in_maps, core_ids=list(range(8)))
    out = np.empty((32, T, D), np.float32)
    for c in range(8):
        out[[c, c + 8, c + 16, c + 24]] = res.results[c]["out"]
    return out


# version 22 (final)
# speedup vs baseline: 1.3271x; 1.0152x over previous
import contextlib
import numpy as np
import ml_dtypes
import concourse.bass as bass
import concourse.mybir as mybir
from concourse.bass_utils import run_bass_kernel_spmd

F32 = mybir.dt.float32
BF16 = mybir.dt.bfloat16
AF = mybir.ActivationFunctionType
ALU = mybir.AluOpType
AX = mybir.AxisListType

T = 2048
D = 1024
NH = 16
NP = NH // 2
INTER = 8192
SHIFT = 20.0
MASKVAL = -30000.0
GC1 = 0.7978845608028654
GC3 = GC1 * 0.044715

ENGS = ("sync", "tensor", "vector", "scalar", "gpsimd")
ENG_SEM = {"sync": None, "tensor": "PE", "vector": "DV", "scalar": "AC", "gpsimd": "GP"}


class Buf:
    __slots__ = ("name", "writes", "reads")

    def __init__(self, name):
        self.name = name
        self.writes = {}
        self.reads = {}


class Sched:
    def __init__(self):
        self.ops = {e: [] for e in ENGS}
        self.cnt = {}
        self.waited = {e: {} for e in ENGS}

    def _wait(self, eng, sem, val):
        w = self.waited[eng]
        if w.get(sem, 0) >= val:
            return
        w[sem] = val
        self.ops[eng].append(("wait", sem, val))

    def wait_tok(self, eng, tok):
        if tok is not None:
            self._wait(eng, tok[0], tok[1])

    def op(self, eng, fn, reads=(), writes=(), sem=None, inc=None, extra_waits=(),
           scalar_reads=()):
        if sem is None:
            sem = ENG_SEM[eng]
            assert sem is not None, f"op on {eng} needs explicit sem"
            if inc is None:
                inc = 1
        own = ENG_SEM[eng]

        def w_(s_, v, force=False):
            if not force and s_ == own and s_ == "PE":
                return
            self._wait(eng, s_, v)
        for b in reads:
            for s_, v in b.writes.items():
                w_(s_, v)
        for b in scalar_reads:
            for s_, v in b.writes.items():
                w_(s_, v, force=True)
        for b in writes:
            for s_, v in b.writes.items():
                w_(s_, v)
            for s_, v in b.reads.items():
                w_(s_, v)
        for tok in extra_waits:
            self.wait_tok(eng, tok)
        self.cnt[sem] = self.cnt.get(sem, 0) + inc
        tok = (sem, self.cnt[sem])
        self.ops[eng].append(("op", fn, sem, inc))
        for b in tuple(reads) + tuple(scalar_reads):
            b.reads[sem] = tok[1]
        for b in writes:
            b.writes[sem] = tok[1]
            b.reads = {}
        return tok

    def raw_wait(self, eng, sem, val):
        self._wait(eng, sem, val)

    def play(self, name, e, sems):
        for item in self.ops[name]:
            if item[0] == "wait":
                _, sem, val = item
                e.wait_ge(sems[sem], val)
            else:
                _, fn, sem, inc = item
                inst = fn(e)
                if sem is not None:
                    assert inst is not None, f"op on {name} returned None ({sem})"
                    inst.then_inc(sems[sem], inc)


def ceil_div(a, b):
    return -(-a // b)


def build_program(n_seqs=4, n_pairs=NP, nch=16, do_ffn=True, n_ic2=16, dbg=False, seq1_phases='lqao', seq0_phases='lqao'):
    nc = bass.Bass("TRN2", target_bir_lowering=False, detect_race_conditions=False)

    x_hbm = nc.dram_tensor("x", [n_seqs, T, D], F32, kind="ExternalInput").ap()
    winT = nc.dram_tensor("winT", [D, 3072], F32, kind="ExternalInput").ap()
    bin_p = nc.dram_tensor("bin_p", [128, 24], F32, kind="ExternalInput").ap()
    woutT_h = nc.dram_tensor("woutT", [D, D], F32, kind="ExternalInput").ap()
    bout_h = nc.dram_tensor("bout", [1, D], F32, kind="ExternalInput").ap()
    cc_h = nc.dram_tensor("cc", [128, T], BF16, kind="ExternalInput").ap()
    ss_h = nc.dram_tensor("ss", [128, T], BF16, kind="ExternalInput").ap()
    mskb_h = nc.dram_tensor("mskb", [128, 128], F32, kind="ExternalInput").ap()
    ident_h = nc.dram_tensor("ident", [128, 128], BF16, kind="ExternalInput").ap()
    w1_h = nc.dram_tensor("w1", [D, INTER], F32, kind="ExternalInput").ap()
    w2_h = nc.dram_tensor("w2", [INTER, D], F32, kind="ExternalInput").ap()
    ew_h = nc.dram_tensor("ew", [128, 32], F32, kind="ExternalInput").ap()
    out_hbm = nc.dram_tensor("out", [n_seqs, T, D], F32, kind="ExternalOutput").ap()
    dbg_out = {}
    if dbg:
        for nm, shape, dt in [("d_xln", [128, 8 * T], BF16), ("d_qt", [128, T], BF16),
                              ("d_kt", [128, T], BF16), ("d_vt", [128, 16 * D], BF16),
                              ("d_bias", [128, 256], F32), ("d_lparts", [128, 8], F32),
                              ("d_lsum", [128, 2], F32), ("d_rl", [128, 2], F32),
                              ("d_pa", [128, T], BF16), ("d_y", [128, 8 * T], BF16)]:
            dbg_out[nm] = nc.dram_tensor(nm, shape, dt, kind="ExternalOutput").ap()
        dbg_out["d_y2"] = nc.dram_tensor("d_y2", [128, 16 * 1024], F32, kind="ExternalOutput").ap()
        dbg_out["d_xln2"] = nc.dram_tensor("d_xln2", [128, 8 * T], BF16, kind="ExternalOutput").ap()

    s = Sched()
    ctx = contextlib.ExitStack()
    sbt = lambda name, shape, dt: ctx.enter_context(nc.sbuf_tensor(name, shape, dt))
    pst = lambda name, shape, dt: ctx.enter_context(nc.psum_tensor(name, shape, dt))

    ccs = sbt("ccs", [128, T], BF16)
    sss = sbt("sss", [128, T], BF16)
    mbias = sbt("mbias", [128, 128], F32)
    ident = sbt("ident_s", [128, 128], BF16)
    bcols = sbt("bcols", [128, 24], F32)
    bout_s = sbt("bout_s", [1, D], BF16)
    ones_r = sbt("ones_r", [1, 128], BF16)
    ones_c = sbt("ones_c", [128, 1], BF16)
    woutT = sbt("woutT_s", [128, 8 * D], BF16)

    XT = sbt("XT", [128, 8 * T], BF16)
    QK = sbt("QK", [128, 2 * NP * T], BF16)
    Qt = QK[:, 0:NP * T]
    Kt = QK[:, NP * T:2 * NP * T]
    Vt = sbt("Vt", [128, 16 * D], BF16)

    xbuf = [sbt(f"xbuf{i}", [128, D], F32) for i in range(2)]
    scr0 = sbt("scr0", [128, D], F32)
    scr1 = sbt("scr1", [128, D], F32)
    stat = sbt("stat", [128, 16], F32)
    xln_c = [sbt(f"xln_c{i}", [128, D], BF16) for i in range(2)]

    wbuf = [sbt(f"wbuf{i}", [128, 8 * 128], BF16) for i in range(2)]
    qc_b = sbt("qc_b", [128, 1024], F32)
    qs_b = sbt("qs_b", [128, 1024], F32)
    qw_b = sbt("qw_b", [128, 1024], F32)

    bias_c = sbt("bias_c", [128, 256], F32)
    lparts = sbt("lparts", [128, 8], F32)
    lsum = sbt("lsum", [128, 2], F32)
    rl = sbt("rl", [128, 2], F32)
    P_A = sbt("P_A", [128, T], BF16)
    P_B = sbt("P_B", [128, T], BF16)
    P_A2 = qs_b[:].bitcast(BF16)
    P_B2 = qc_b[:].bitcast(BF16)
    PT_A = xln_c[0]
    PT_B = xln_c[1]

    sqx = scr0
    xr = xbuf
    rbuf = [scr0, scr1]
    vfm = P_A

    psSA = pst("psSA", [128, 1024], F32)
    psSB = pst("psSB", [128, 1024], F32)
    psTA = pst("psTA", [128, 1024], BF16)
    psTB = pst("psTB", [128, 1024], BF16)
    psY = pst("psY", [128, 512], F32)
    psU = pst("psU", [128, 512], F32)

    B = lambda n: Buf(n)
    b_consts = B("consts")
    b_XT = B("XT")
    b_Qt = [B(f"Qt{t}") for t in range(NP)]
    b_Kt = [B(f"Kt{t}") for t in range(NP)]
    b_Vt = [B(f"Vt{t}") for t in range(NP)]
    b_xbuf = [B(f"xbuf{i}") for i in range(2)]
    b_scr0 = B("scr0")
    b_scr1 = B("scr1")
    b_sqx = b_scr0
    b_stat = [B(f"stat{i}") for i in range(2)]
    b_xln = [B(f"xln{i}") for i in range(2)]
    b_wbuf = [B(f"wbuf{i}") for i in range(2)]
    b_qc = B("qc"); b_qs = B("qs"); b_qw = B("qw")
    b_qc2 = [B("qc_h0"), B("qc_h1")]
    b_qs2 = [B("qs_h0"), B("qs_h1")]
    b_qw2 = [B("qw_h0"), B("qw_h1")]
    _blPA2 = (B("P_A2"), b_qs2[0], b_qs2[1])
    _blPB2 = (B("P_B2"), b_qc2[0], b_qc2[1])
    b_lparts = B("lparts"); b_lsum = B("lsum"); b_rl = B("rl")
    bl_P = [[(B("P_A"),), (B("P_B"),)], [_blPA2, _blPB2]]
    b_vfm = bl_P[0][0][0]
    b_PT = [b_xln[0], b_xln[1]]
    bias_all = bias_c
    b_biasP = [B(f"bias{p}") for p in range(NP)]
    b_xr = b_xbuf
    b_rbuf = [b_scr0, b_scr1]
    b_SA = [B("psSA0"), B("psSA1")]
    b_SB = [B("psSB0"), B("psSB1")]
    b_psT = [B("psTA"), B("psTB")]
    b_psY = B("psY")
    b_psU = [B("psU0"), B("psU1")]
    b_out0 = B("out0_hbm")

    def dump(nm, ap, bufs):
        if not dbg:
            return
        s.op("sync", lambda e, nm=nm, ap=ap: e.dma_start(out=dbg_out[nm], in_=ap),
             reads=tuple(bufs), sem="ST", inc=16)

    for dst, src in [(ccs, cc_h), (sss, ss_h), (mbias, mskb_h), (ident, ident_h)]:
        s.op("sync", lambda e, d=dst, so=src: e.dma_start(out=d[:], in_=so),
             writes=(b_consts,), sem="LD", inc=16)
    s.op("sync", lambda e: e.dma_start(out=bcols[:], in_=bin_p),
         writes=(b_consts,), sem="LD", inc=16)
    s.op("gpsimd", lambda e: e.dma_start(out=bout_s[:], in_=bout_h),
         writes=(b_consts,), sem="GD", inc=16)
    s.op("gpsimd", lambda e: e.dma_start(
        out=woutT[:].rearrange("p (t d) -> p t d", t=8),
        in_=woutT_h.rearrange("(t p) d -> p t d", p=128)),
        writes=(b_consts,), sem="GD", inc=16)
    s.op("gpsimd", lambda e: e.memset(ones_r[:], 1.0), writes=(b_consts,))
    s.op("gpsimd", lambda e: e.memset(ones_c[:], 1.0), writes=(b_consts,))
    s.op("gpsimd", lambda e: e.memset(stat[:], 0.0), writes=(b_stat[0], b_stat[1]))
    if dbg:
        s.op("gpsimd", lambda e: e.memset(XT[:], 0.0), writes=(b_XT,))
        s.op("gpsimd", lambda e: e.memset(QK[:], 0.0), writes=tuple(b_Qt + b_Kt))
        s.op("gpsimd", lambda e: e.memset(Vt[:], 0.0), writes=tuple(b_Vt))
        s.op("gpsimd", lambda e: e.memset(P_A[:], 0.0), writes=(bl_P[0][0][0],))
        s.op("gpsimd", lambda e: e.memset(bias_c[:], 0.0), writes=tuple(b_biasP))
        s.op("gpsimd", lambda e: e.memset(lparts[:], 0.0), writes=(b_lparts,))
        s.op("gpsimd", lambda e: e.memset(lsum[:], 0.0), writes=(b_lsum,))
        s.op("gpsimd", lambda e: e.memset(rl[:], 0.0), writes=(b_rl,))

    def phase_ln(src_hbm, dst_XT, b_dst, src_dep=None):
        def ln_s1(tc):
            xb, bx = xbuf[tc % 2], b_xbuf[tc % 2]
            o = (tc % 2) * 8
            bst = b_stat[tc % 2]
            s.op("sync", lambda e, xb=xb, tc=tc: e.dma_start(
                out=xb[:], in_=src_hbm[tc * 128:(tc + 1) * 128, :]),
                reads=(src_dep,) if src_dep is not None else (),
                writes=(bx,), sem="LD", inc=16)
            s.op("vector", lambda e, xb=xb, o=o: e.tensor_reduce(
                stat[:, o:o + 1], xb[:], AX.X, ALU.add), reads=(bx,), writes=(bst,))
            s.op("scalar", lambda e, xb=xb, o=o: e.activation(
                sqx[:], xb[:], AF.Square, accum_out=stat[:, o + 1:o + 2]),
                reads=(bx, bst), writes=(b_sqx, bst))
            s.op("scalar", lambda e, o=o: e.copy(sqx[:, 0:1], stat[:, o + 1:o + 2]),
                 reads=(bst,), writes=(bst, b_sqx))

        def ln_s23(tc):
            xb, bx = xbuf[tc % 2], b_xbuf[tc % 2]
            o = (tc % 2) * 8
            bst = b_stat[tc % 2]
            xlc, bxl = xln_c[tc % 2], b_xln[tc % 2]
            s.op("vector", lambda e, o=o: e.tensor_scalar(
                stat[:, o + 2:o + 3], stat[:, o:o + 1], 1.0 / D, None, ALU.mult),
                reads=(bst,), writes=(bst,))
            s.op("vector", lambda e, o=o: e.tensor_scalar(
                stat[:, o + 3:o + 4], stat[:, o + 2:o + 3], stat[:, o + 2:o + 3],
                1e-5, ALU.mult, ALU.subtract), reads=(), scalar_reads=(bst,),
                writes=(bst,))
            s.op("vector", lambda e, o=o: e.scalar_tensor_tensor(
                stat[:, o + 4:o + 5], stat[:, o + 1:o + 2], 1.0 / D, stat[:, o + 3:o + 4],
                ALU.mult, ALU.subtract), reads=(bst,), writes=(bst,))
            s.op("scalar", lambda e, o=o: e.activation(
                stat[:, o + 5:o + 6], stat[:, o + 4:o + 5], AF.Ln),
                reads=(bst,), writes=(bst,))
            s.op("scalar", lambda e, o=o: e.activation(
                stat[:, o + 6:o + 7], stat[:, o + 5:o + 6], AF.Exp, scale=-0.5),
                reads=(bst,), writes=(bst,))
            s.op("vector", lambda e, o=o: e.tensor_scalar(
                stat[:, o + 7:o + 8], stat[:, o + 2:o + 3], -1.0, None, ALU.mult),
                reads=(bst,), writes=(bst,))
            s.op("vector", lambda e, xb=xb, xlc=xlc, o=o: e.tensor_scalar(
                xlc[:], xb[:], stat[:, o + 7:o + 8], stat[:, o + 6:o + 7],
                ALU.add, ALU.mult), reads=(bx,), scalar_reads=(bst,), writes=(bxl,))
            for dc in range(8):
                s.op("tensor", lambda e, dc=dc, xlc=xlc: e.transpose(
                    psTA[:, dc * 128:(dc + 1) * 128], xlc[:, dc * 128:(dc + 1) * 128],
                    ident[:]), reads=(bxl, b_consts), writes=(b_psT[0],))
            s.op("vector", lambda e, tc=tc: e.tensor_copy(
                dst_XT[:].rearrange("p (dc t) -> p dc t", dc=8)[:, :, tc * 128:(tc + 1) * 128],
                psTA[:].rearrange("p (dc t) -> p dc t", dc=8)),
                reads=(b_psT[0],), writes=(b_dst,))

        ln_s1(0)
        for tc in range(nch):
            if tc + 1 < nch:
                ln_s1(tc + 1)
            ln_s23(tc)

    def phase_qkv_start(si):
        ntc = nch // 4
        fc_order = []
        for t_ in range(n_pairs):
            fc_order += [t_, 8 + t_, 16 + t_]
        nfc = len(fc_order)
        state = {"fc_order": fc_order, "nfc": nfc, "ntc": ntc,
                 "pending": [None]}

        def wload(step):
            fc = fc_order[step]
            wb, bw = wbuf[step % 2], b_wbuf[step % 2]
            s.op("gpsimd", lambda e, fc=fc, wb=wb: e.dma_start(
                out=wb[:].rearrange("p (dc n) -> p dc n", dc=8),
                in_=winT[:, fc * 128:(fc + 1) * 128].rearrange("(dc p) n -> p dc n", p=128)),
                writes=(bw,), sem="GD", inc=16)

        state["wload"] = wload
        wload(0)
        if nfc > 1:
            wload(1)
        return state

    def phase_qkv_pair(si, pair_idx, state):
        fc_order, nfc, ntc = state["fc_order"], state["nfc"], state["ntc"]
        wload = state["wload"]
        pending_add = state["pending"]

        def flush_add():
            if pending_add[0] is not None:
                pending_add[0]()
                pending_add[0] = None

        for step in range(pair_idx * 3, pair_idx * 3 + 3):
            fc = fc_order[step]
            wb, bw = wbuf[step % 2], b_wbuf[step % 2]
            is_v = fc >= 16
            pair = fc % 8
            for tc4 in range(ntc):
                bankT, bb = (psSA, b_SA) if tc4 % 2 == 0 else (psSB, b_SB)
                bank, bbank = bankT[:, 0:512], bb[0]
                loc = (tc4 % 2) * 512
                for dc in range(8):
                    s.op("tensor", lambda e, dc=dc, tc4=tc4, wb=wb, bank=bank: e.matmul(
                        bank, wb[:, dc * 128:(dc + 1) * 128],
                        XT[:, dc * T + tc4 * 512: dc * T + (tc4 + 1) * 512],
                        start=(dc == 0), stop=(dc == 7)),
                        reads=(bw, b_XT), writes=(bbank,) if dc in (0, 7) else ())
                if tc4 == ntc - 1 and step + 2 < nfc:
                    wload(step + 2)
                if is_v:
                    s.op("vector", lambda e, tc4=tc4, fc=fc, bank=bank: e.tensor_scalar(
                        vfm[:, tc4 * 512:(tc4 + 1) * 512], bank,
                        bcols[:, fc:fc + 1], None, ALU.add),
                        reads=(bbank, b_consts), writes=(b_vfm,))
                else:
                    flush_add()
                    bqc = b_qc2[tc4 % 2]
                    bqs = b_qs2[tc4 % 2]
                    bqw = b_qw2[tc4 % 2]
                    s.op("vector", lambda e, tc4=tc4, fc=fc, bank=bank, loc=loc: e.scalar_tensor_tensor(
                        qc_b[:, loc:loc + 512], bank,
                        bcols[:, fc:fc + 1], ccs[:, tc4 * 512:(tc4 + 1) * 512],
                        ALU.add, ALU.mult), reads=(bbank, b_consts), writes=(bqc,))
                    s.op("vector", lambda e, tc4=tc4, fc=fc, bank=bank, loc=loc: e.scalar_tensor_tensor(
                        qs_b[:, loc:loc + 512], bank,
                        bcols[:, fc:fc + 1], sss[:, tc4 * 512:(tc4 + 1) * 512],
                        ALU.add, ALU.mult), reads=(bbank, b_consts), writes=(bqs,))
                    for g in range(4):
                        src_g = [1, 0, 3, 2][g]
                        eng_c = "vector" if g < 2 else "scalar"
                        cp_f = ((lambda e, g=g, src_g=src_g, loc=loc: e.tensor_copy(
                                    qw_b[g * 32:(g + 1) * 32, loc:loc + 512],
                                    qs_b[src_g * 32:(src_g + 1) * 32, loc:loc + 512]))
                                if g < 2 else
                                (lambda e, g=g, src_g=src_g, loc=loc: e.copy(
                                    qw_b[g * 32:(g + 1) * 32, loc:loc + 512],
                                    qs_b[src_g * 32:(src_g + 1) * 32, loc:loc + 512])))
                        s.op(eng_c, cp_f, reads=(bqs,), writes=(bqw,))
                    dst, bdst = (Qt, b_Qt[pair]) if fc < 8 else (Kt, b_Kt[pair])
                    eng = "gpsimd"

                    def mk_add(dst=dst, bdst=bdst, pair=pair, tc4=tc4, loc=loc,
                               bqc=bqc, bqw=bqw, eng=eng):
                        def do():
                            s.op(eng, lambda e: e.tensor_tensor(
                                dst[:, pair * T + tc4 * 512: pair * T + tc4 * 512 + 512],
                                qc_b[:, loc:loc + 512], qw_b[:, loc:loc + 512], ALU.add),
                                reads=(bqc, bqw), writes=(bdst,))
                        return do
                    pending_add[0] = mk_add()
            if is_v:
                flush_add()
                for half in range(ceil_div(nch, 8)):
                    w8 = min(8, nch - half * 8)
                    for j in range(w8):
                        tcv = half * 8 + j
                        s.op("tensor", lambda e, tcv=tcv, j=j: e.transpose(
                            psTB[:, j * 128:(j + 1) * 128], vfm[:, tcv * 128:(tcv + 1) * 128],
                            ident[:]), reads=(b_vfm, b_consts), writes=(b_psT[1],))
                    s.op("vector", lambda e, half=half, pair=pair, w8=w8: e.tensor_copy(
                        Vt[:].rearrange("p (tc hv) -> p tc hv", hv=1024)
                          [:, half * 8:half * 8 + w8, pair * 128:(pair + 1) * 128],
                        psTB[:, 0:w8 * 128].rearrange("p (j n) -> p j n", j=w8)),
                        reads=(b_psT[1],), writes=(b_Vt[pair],))
                nqb = nch
                w0 = min(1024, nch * 128)
                s.op("scalar", lambda e, pair=pair, w0=w0: e.activation(
                    PT_A[:, 0:w0], Qt[:, pair * T: pair * T + w0], AF.Square),
                    reads=(b_Qt[pair],), writes=(b_PT[0],))
                if nch * 128 > 1024:
                    s.op("scalar", lambda e, pair=pair: e.activation(
                        PT_B[:, 0:nch * 128 - 1024],
                        Qt[:, pair * T + 1024: pair * T + nch * 128], AF.Square),
                        reads=(b_Qt[pair],), writes=(b_PT[1],))
                for hh in range(2):
                    bank = psU if hh == 0 else psY
                    bb = b_psU[0] if hh == 0 else b_psY
                    for qb in range(nqb):
                        sq_t = PT_A if qb < 8 else PT_B
                        bsq = b_PT[0] if qb < 8 else b_PT[1]
                        s.op("tensor", lambda e, qb=qb, hh=hh, bank=bank, sq_t=sq_t: e.matmul(
                            bank[:, qb:qb + 1],
                            sq_t[64 * hh:64 * hh + 64, (qb % 8) * 128:(qb % 8) * 128 + 128],
                            ones_c[64 * hh:64 * hh + 64, :],
                            start=(qb == 0), stop=(qb == nqb - 1)),
                            reads=(bsq, b_consts),
                            writes=(bb,) if qb in (0, nqb - 1) else ())
                for hh in range(2):
                    bank = psU if hh == 0 else psY
                    bb = b_psU[0] if hh == 0 else b_psY
                    s.op("vector", lambda e, hh=hh, bank=bank, pair=pair, nqb=nqb: e.tensor_scalar(
                        bias_all[:, pair * 32 + hh * 16:pair * 32 + hh * 16 + nqb],
                        bank[:, 0:nqb], -0.125, -SHIFT, ALU.mult, ALU.add),
                        reads=(bb,), writes=(b_biasP[pair],))
                flush_add()
        flush_add()
    def phase_attn_pair(si, pair):
        nqb = nch
        if True:
            def emit_S_mm(qb, kc, pair=pair):
                L = (qb + 1) * 128
                w = min(512, L - kc * 512)
                for hh in (1, 0):
                    bankT, bbs = (psSA, b_SA) if hh == 0 else (psSB, b_SB)
                    bbank = bbs[kc % 2]
                    bank = bankT[:, (kc % 2) * 512:(kc % 2) * 512 + w]
                    rows = slice(64 * hh, 64 * hh + 64)
                    s.op("tensor", lambda e, rows=rows, qb=qb, kc=kc, w=w, bank=bank: e.matmul(
                        bank, Qt[rows, pair * T + qb * 128: pair * T + (qb + 1) * 128],
                        Kt[rows, pair * T + kc * 512: pair * T + kc * 512 + w],
                        start=True, stop=True),
                        reads=(b_Qt[pair], b_Kt[pair]), writes=(bbank,))

            def stage_S(qb, pair=pair):
                nkc = ceil_div((qb + 1) * 128, 512)
                for kc in range(min(2, nkc)):
                    emit_S_mm(qb, kc)

            def softmax_stage(qb, pair=pair):
                L = (qb + 1) * 128
                nkc = ceil_div(L, 512)
                ngr = ceil_div(L, 1024)
                par = qb % 2
                PbA = P_A if par == 0 else P_A2
                PbB = P_B if par == 0 else P_B2
                for g in range(ngr):
                    w = min(1024, L - g * 1024)
                    kc_hi = min(2 * g + 1, nkc - 1)
                    for hh in range(2):
                        bankT, bbs = (psSA, b_SA) if hh == 0 else (psSB, b_SB)
                        bb0, bb1 = bbs[0], bbs[1]
                        if kc_hi == nkc - 1:
                            moff = qb * 128 - g * 1024
                            s.op("vector", lambda e, bankT=bankT, moff=moff: e.tensor_tensor(
                                bankT[:, moff:moff + 128], bankT[:, moff:moff + 128],
                                mbias[:], ALU.add),
                                reads=(bb0, bb1, b_consts),
                                writes=(bb0,) if moff < 512 else (bb1,))
                        s.op("scalar", lambda e, bankT=bankT, hh=hh, g=g, w=w, qb=qb, PbA=PbA, PbB=PbB: e.activation(
                            (PbA if hh == 0 else PbB)[:, g * 1024:g * 1024 + w],
                            bankT[:, 0:w], AF.Exp, scale=0.125,
                            bias=bias_all[:, pair * 32 + hh * 16 + qb:pair * 32 + hh * 16 + qb + 1],
                            accum_out=lparts[:, hh * 4 + g:hh * 4 + g + 1]),
                            reads=(bb0, bb1, b_biasP[pair], b_lparts),
                            writes=bl_P[par][hh] + (b_lparts,))
                    if 2 * g + 2 < nkc:
                        emit_S_mm(qb, 2 * g + 2)
                    if 2 * g + 3 < nkc:
                        emit_S_mm(qb, 2 * g + 3)
                s.op("scalar", lambda e: e.copy(lsum[:, 0:1], lparts[:, 0:1]),
                     reads=(b_lparts,), writes=(b_lparts, b_lsum))
                s.op("vector", lambda e, ngr=ngr: e.tensor_reduce(
                    lsum[:, 0:2], lparts[:].rearrange("p (h k) -> p h k", h=2)[:, :, 0:ngr],
                    AX.X, ALU.add), reads=(b_lparts,), writes=(b_lsum,))
                s.op("vector", lambda e: e.reciprocal(rl[:, 0:2], lsum[:, 0:2]),
                     reads=(b_lsum,), writes=(b_rl,))
                for hh in range(2):
                    Pbuf = PbA if hh == 0 else PbB
                    s.op("vector", lambda e, hh=hh, Pbuf=Pbuf, L=L: e.tensor_scalar(
                        Pbuf[:, 0:L], Pbuf[:, 0:L], rl[:, hh:hh + 1], None, ALU.mult),
                        reads=bl_P[par][hh], scalar_reads=(b_rl,),
                        writes=bl_P[par][hh])

            def pv_stage(qb, pair=pair):
                L = (qb + 1) * 128
                npk = ceil_div(L, 1024)
                par = qb % 2
                PbA = P_A if par == 0 else P_A2
                PbB = P_B if par == 0 else P_B2
                for pk in range(npk):
                    wpk = min(8, L // 128 - pk * 8)
                    for hh in range(2):
                        Pbuf = PbA if hh == 0 else PbB
                        psT = psTA if hh == 0 else psTB
                        for j in range(wpk):
                            kcb = pk * 8 + j
                            s.op("tensor", lambda e, j=j, kcb=kcb, Pbuf=Pbuf, psT=psT: e.transpose(
                                psT[:, j * 128:(j + 1) * 128],
                                Pbuf[:, kcb * 128:(kcb + 1) * 128], ident[:]),
                                reads=bl_P[par][hh] + (b_consts,), writes=(b_psT[hh],))
                    for hh in range(2):
                        psT = psTA if hh == 0 else psTB
                        PTs = PT_A if hh == 0 else PT_B
                        s.op("vector", lambda e, psT=psT, PTs=PTs, wpk=wpk: e.tensor_copy(
                            PTs[:, 0:wpk * 128], psT[:, 0:wpk * 128]),
                            reads=(b_psT[hh],), writes=(b_PT[hh],))
                    for hh in range(2):
                        PTs = PT_A if hh == 0 else PT_B
                        ybase = 64 * hh
                        for j in range(wpk):
                            kcb = pk * 8 + j
                            s.op("tensor", lambda e, j=j, kcb=kcb, hh=hh, PTs=PTs, ybase=ybase, L=L: e.matmul(
                                psY[ybase:ybase + 64, 0:128],
                                Vt[:, kcb * 1024 + (2 * pair + hh) * 64: kcb * 1024 + (2 * pair + hh) * 64 + 64],
                                PTs[:, j * 128:(j + 1) * 128],
                                start=(kcb == 0), stop=(kcb == L // 128 - 1),
                                skip_group_check=True),
                                reads=(b_PT[hh], b_Vt[pair]),
                                writes=(b_psY,) if kcb in (0, L // 128 - 1) else ())
                s.op("vector", lambda e, qb=qb: e.tensor_copy(
                    Qt[:, pair * T + qb * 128: pair * T + (qb + 1) * 128], psY[:, 0:128]),
                    reads=(b_psY,), writes=(b_Qt[pair],))

            stage_S(0)
            softmax_stage(0)
            if nqb > 1:
                stage_S(1)
            for qb in range(1, nqb):
                softmax_stage(qb)
                if qb + 1 < nqb:
                    stage_S(qb + 1)
                pv_stage(qb - 1)
            pv_stage(nqb - 1)

    def phase_out(si):
        for tb in range(nch):
            xb, bx = xr[tb % 2], b_xr[tb % 2]
            rb, br = rbuf[tb % 2], b_rbuf[tb % 2]
            s.op("sync", lambda e, xb=xb, si=si, tb=tb: e.dma_start(
                out=xb[:], in_=x_hbm[si, tb * 128:(tb + 1) * 128, :]),
                writes=(bx,), sem="LD", inc=16)
            for dh in range(2):
                bankT, bbs = (psSA, b_SA) if dh == 0 else (psSB, b_SB)
                bank, bbank = bankT[:, 0:512], bbs[0]
                for t_ in range(n_pairs):
                    s.op("tensor", lambda e, t_=t_, tb=tb, dh=dh, bank=bank: e.matmul(
                        bank, Qt[:, t_ * T + tb * 128: t_ * T + tb * 128 + 128],
                        woutT[:, t_ * D + dh * 512: t_ * D + dh * 512 + 512],
                        start=(t_ == 0), stop=False),
                        reads=(b_Qt[t_], b_consts), writes=(bbank,) if t_ == 0 else ())
                s.op("tensor", lambda e, dh=dh, bank=bank: e.matmul(
                    bank, ones_r[0:1, :], bout_s[0:1, dh * 512:dh * 512 + 512],
                    start=False, stop=True), reads=(b_consts,), writes=(bbank,))
                s.op("vector", lambda e, rb=rb, xb=xb, dh=dh, bank=bank: e.tensor_tensor(
                    rb[:, dh * 512:dh * 512 + 512], bank,
                    xb[:, dh * 512:dh * 512 + 512], ALU.add),
                    reads=(bbank, bx), writes=(br,))
            tok = s.op("sync", lambda e, rb=rb, si=si, tb=tb: e.dma_start(
                out=out_hbm[si, tb * 128:(tb + 1) * 128, :], in_=rb[:]),
                reads=(br,), writes=(b_out0,) if si == 0 else (), sem="ST", inc=16)
        return tok

    def phase_ffn():
        ew_ap = bias_c[:, 0:32]
        sv = stat[:, 0:16]
        w1b = [Vt[:, 0:4096], Vt[:, 4096:8192]]
        w2b = [Vt[:, 8192:12288], Vt[:, 12288:16384]]
        y2 = QK[:].bitcast(F32)
        sq_g = qc_b
        u_g = qs_b
        t_g = qw_b
        h_sb = PT_A[:]
        b_ew = B("ew")
        b_sv = b_PT[1]
        b_w1b = [b_Vt[0], b_Vt[1]]
        b_w2b = [b_Vt[2], b_Vt[3]]
        b_y2 = B("y2")
        b_g = [B(f"g{i}") for i in range(6)]
        b_psH = [b_SA[0], b_SA[1], b_SB[0], b_SB[1]]
        b_psO = [b_psY, b_psU[0]]

        s.op("sync", lambda e: e.dma_start(out=ew_ap, in_=ew_h),
            writes=tuple([b_ew] + b_biasP), sem="LD", inc=16)
        for _mi in range(4):
            s.op("gpsimd", lambda e, _mi=_mi: e.memset(y2[:, _mi * 4096:(_mi + 1) * 4096], 0.0),
                 writes=tuple([b_y2] + b_Qt + b_Kt))
        import os as _os
        _stage = int(_os.environ.get("FFN_STAGE", "4"))
        if _stage <= 1:
            return

        phase_ln(out_hbm[0], XT, b_XT, src_dep=b_out0)
        dump("d_xln2", XT[:], [b_XT])
        if _stage <= 2:
            return
        s.op("vector", lambda e: e.tensor_tensor(
            sv, ew_ap.rearrange("p (c k) -> p c k", k=2)[:, :, 0],
            ew_ap.rearrange("p (c k) -> p c k", k=2)[:, :, 1], ALU.add),
            reads=(b_ew,), writes=(b_sv, b_stat[0], b_stat[1]))

        def ffn_mm1(t):
            ic2, tw = divmod(t, nch // 2)
            wb1, bw1 = w1b[ic2 % 2], b_w1b[ic2 % 2]
            if tw == 0:
                s.op("gpsimd", lambda e, ic2=ic2, wb1=wb1: e.dma_start(
                    out=wb1.rearrange("p (dc i) -> p dc i", dc=8),
                    in_=w1_h[:, ic2 * 512:(ic2 + 1) * 512].rearrange("(dc p) i -> p dc i", p=128)),
                    writes=(bw1,), sem="GD", inc=16)
                wb2, bw2 = w2b[ic2 % 2], b_w2b[ic2 % 2]
                s.op("gpsimd", lambda e, ic2=ic2, wb2=wb2: e.dma_start(
                    out=wb2.rearrange("p (i4 d) -> p i4 d", i4=4),
                    in_=w2_h[ic2 * 512:(ic2 + 1) * 512, :].rearrange("(i4 p) d -> p i4 d", p=128)),
                    writes=(bw2,), sem="GD", inc=16)
            for icsub in range(4):
                bankT = (psSA, psSA, psSB, psSB)[icsub]
                bank = bankT[:, (icsub % 2) * 512:(icsub % 2) * 512 + 256]
                for dc in range(8):
                    s.op("tensor", lambda e, icsub=icsub, dc=dc, tw=tw, wb1=wb1, bank=bank: e.matmul(
                        bank, wb1[:, dc * 512 + icsub * 128: dc * 512 + (icsub + 1) * 128],
                        XT[:, dc * T + tw * 256:dc * T + tw * 256 + 256],
                        start=(dc == 0), stop=(dc == 7)),
                        reads=(bw1, b_XT), writes=(b_psH[icsub],) if dc in (0, 7) else ())

        def ffn_gelu(t):
            h_sb = (PT_A if t % 2 == 0 else PT_B)[:]
            bh = b_PT[t % 2]
            for half, bankT, rd in ((0, psSA, b_psH[0:2]), (1, psSB, b_psH[2:4])):
                hv = bankT[:].rearrange("p (i w) -> p i w", i=2)[:, :, 0:256]
                s.op("scalar", lambda e, hv=hv, half=half: e.activation(
                    sq_g[:, half * 512:half * 512 + 512].rearrange("p (i w) -> p i w", i=2),
                    hv, AF.Square), reads=tuple(rd), writes=(b_g[0],) + bl_P[1][1])
            s.op("vector", lambda e: e.tensor_scalar(
                sq_g[:], sq_g[:], GC3, GC1, ALU.mult, ALU.add),
                reads=(b_g[0],), writes=(b_g[1],))
            for half, bankT, rd in ((0, psSA, b_psH[0:2]), (1, psSB, b_psH[2:4])):
                hv = bankT[:].rearrange("p (i w) -> p i w", i=2)[:, :, 0:256]
                s.op("vector", lambda e, hv=hv, half=half: e.tensor_tensor(
                    u_g[:, half * 512:half * 512 + 512].rearrange("p (i w) -> p i w", i=2),
                    sq_g[:, half * 512:half * 512 + 512].rearrange("p (i w) -> p i w", i=2),
                    hv, ALU.mult),
                    reads=(b_g[1],) + tuple(rd), writes=(b_g[2],) + bl_P[1][0])
            s.op("scalar", lambda e: e.activation(t_g[:], u_g[:], AF.Tanh),
                reads=(b_g[2],), writes=(b_g[3], b_qw2[0], b_qw2[1]))
            s.op("vector", lambda e: e.tensor_scalar(
                t_g[:], t_g[:], 0.5, 0.5, ALU.mult, ALU.add),
                reads=(b_g[3],), writes=(b_g[4],))
            for half, bankT, rd in ((0, psSA, b_psH[0:2]), (1, psSB, b_psH[2:4])):
                hv = bankT[:].rearrange("p (i w) -> p i w", i=2)[:, :, 0:256]
                s.op("vector", lambda e, hv=hv, half=half, h_sb=h_sb: e.tensor_tensor(
                    h_sb[:, half * 512:half * 512 + 512].rearrange("p (i w) -> p i w", i=2),
                    t_g[:, half * 512:half * 512 + 512].rearrange("p (i w) -> p i w", i=2),
                    hv, ALU.mult),
                    reads=(b_g[4],) + tuple(rd), writes=(bh,))

        def ffn_mm2(t):
            ic2, tw = divmod(t, nch // 2)
            wb2, bw2 = w2b[ic2 % 2], b_w2b[ic2 % 2]
            h_sb = (PT_A if t % 2 == 0 else PT_B)[:]
            bh = b_PT[t % 2]
            for dh in range(2):
                for tb2 in range(2):
                    psO = psY if tb2 == 0 else psU
                    bO = b_psO[tb2]
                    for icsub in range(4):
                        s.op("tensor", lambda e, icsub=icsub, dh=dh, tb2=tb2, psO=psO, wb2=wb2, h_sb=h_sb: e.matmul(
                            psO[:, 0:512],
                            h_sb[:, icsub * 256 + tb2 * 128: icsub * 256 + tb2 * 128 + 128],
                            wb2[:, icsub * 1024 + dh * 512: icsub * 1024 + dh * 512 + 512],
                            start=(icsub == 0), stop=(icsub == 3)),
                            reads=(bh, bw2), writes=(bO,) if icsub in (0, 3) else ())
                    s.op("vector", lambda e, tw=tw, dh=dh, tb2=tb2, psO=psO: e.tensor_tensor(
                        y2[:, (tw * 2 + tb2) * 1024 + dh * 512: (tw * 2 + tb2) * 1024 + dh * 512 + 512],
                        y2[:, (tw * 2 + tb2) * 1024 + dh * 512: (tw * 2 + tb2) * 1024 + dh * 512 + 512],
                        psO[:, 0:512], ALU.add),
                        reads=(b_psO[tb2], b_y2), writes=(b_y2,))

        n_iter = n_ic2 * (nch // 2)
        ffn_mm1(0)
        ffn_gelu(0)
        for t in range(1, n_iter + 1):
            if t < n_iter:
                ffn_mm1(t)
            ffn_mm2(t - 1)
            if t < n_iter:
                ffn_gelu(t)
        dump("d_y2", y2, [b_y2])
        for tb in range(nch):
            xb, bx = xr[tb % 2], b_xr[tb % 2]
            rb, br = rbuf[tb % 2], b_rbuf[tb % 2]
            s.op("sync", lambda e, xb=xb, tb=tb: e.dma_start(
                out=xb[:], in_=out_hbm[0, tb * 128:(tb + 1) * 128, :]),
                reads=(b_out0,), writes=(bx,), sem="LD", inc=16)
            s.op("vector", lambda e, rb=rb, xb=xb, tb=tb: e.scalar_tensor_tensor(
                rb[:], y2[:, tb * 1024:(tb + 1) * 1024], sv[:, tb:tb + 1], xb[:],
                ALU.mult, ALU.add), reads=(b_y2, bx), scalar_reads=(b_sv,),
                writes=(br,))
            s.op("sync", lambda e, rb=rb, tb=tb: e.dma_start(
                out=out_hbm[0, tb * 128:(tb + 1) * 128, :], in_=rb[:]),
                reads=(br,), writes=(b_out0,), sem="ST", inc=16)

    for si in range(n_seqs):
        ph = seq1_phases if si > 0 else seq0_phases
        if 'l' not in ph:
            continue
        phase_ln(x_hbm[si], XT, b_XT)
        if 'q' not in ph:
            continue
        if 'a' in ph:
            qkv_state = phase_qkv_start(si)
            for pair in range(n_pairs):
                phase_qkv_pair(si, pair, qkv_state)
                if pair >= 1:
                    phase_attn_pair(si, pair - 1)
            phase_attn_pair(si, n_pairs - 1)
        else:
            qkv_state = phase_qkv_start(si)
            for pair in range(n_pairs):
                phase_qkv_pair(si, pair, qkv_state)
        if 'a' not in ph or 'o' not in ph:
            continue
        phase_out(si)
    if do_ffn:
        phase_ffn()

    if "ST" in s.cnt:
        s.raw_wait("sync", "ST", s.cnt["ST"])

    sems = {}
    for name in ["LD", "SW", "ST", "GD", "PE", "DV", "AC", "GP"]:
        sems[name] = ctx.enter_context(nc.semaphore(name))
    with nc.Block() as blk:
        @blk.sync
        def _(e):
            s.play("sync", e, sems)

        @blk.tensor
        def _(e):
            s.play("tensor", e, sems)

        @blk.vector
        def _(e):
            s.play("vector", e, sems)

        @blk.scalar
        def _(e):
            s.play("scalar", e, sems)

        @blk.gpsimd
        def _(e):
            s.play("gpsimd", e, sems)

    ctx.close()
    return nc



def _prep_consts():
    half = 32
    p = np.arange(half, dtype=np.float64) / (half - 1)
    thetas = 1.0 * (10000.0 / 1.0) ** p
    pos = np.arange(T, dtype=np.float64)
    ang = pos[None, :] / thetas[:, None]
    cos32 = np.cos(ang).astype(np.float32)
    sin32 = np.sin(ang).astype(np.float32)
    CC = np.tile(cos32, (4, 1)).astype(ml_dtypes.bfloat16)
    SS = np.concatenate([-sin32, sin32, -sin32, sin32], axis=0).astype(ml_dtypes.bfloat16)
    i = np.arange(128)
    M = np.where(i[None, :] <= i[:, None], 0.0, MASKVAL).astype(np.float32)
    I = np.eye(128, dtype=np.float32).astype(ml_dtypes.bfloat16)
    return CC, SS, M, I


def _perm3072():
    perm = []
    for sec in range(2):
        base = sec * 1024
        for h in range(NH):
            perm += list(range(base + h * 64 + 32, base + h * 64 + 64))
            perm += list(range(base + h * 64, base + h * 64 + 32))
    perm += list(range(2048, 3072))
    return np.array(perm)


def make_in_maps(x, W_in, b_in, W_out, b_out_, ew, w1, w2, n_cores=8):
    perm = _perm3072()
    winT = np.ascontiguousarray(W_in[perm].T.astype(np.float32))
    bin_p = np.ascontiguousarray(b_in[perm].astype(np.float32).reshape(24, 128).T)
    woutT = np.ascontiguousarray(W_out.T.astype(np.float32))
    bout = np.ascontiguousarray(b_out_.astype(np.float32)).reshape(1, D)
    CC, SS, M, I = _prep_consts()
    in_maps = []
    for c in range(n_cores):
        batches = [c, c + 8, c + 16, c + 24]
        ewc = (ew[c * T:(c + 1) * T].astype(np.float32) if c < 2
               else np.zeros((T, 2), np.float32))
        ewc = np.ascontiguousarray(
            ewc.reshape(16, 128, 2).transpose(1, 0, 2).reshape(128, 32))
        in_maps.append({
            "x": np.ascontiguousarray(x[batches].astype(np.float32)),
            "winT": winT, "bin_p": bin_p, "woutT": woutT, "bout": bout,
            "cc": CC, "ss": SS, "mskb": M, "ident": I,
            "w1": np.ascontiguousarray(w1.astype(np.float32)),
            "w2": np.ascontiguousarray(w2.astype(np.float32)),
            "ew": np.ascontiguousarray(ewc),
        })
    return in_maps


def kernel(L0_attn_x, L0_attn_W_in, L0_attn_b_in, L0_attn_W_out, L0_attn_b_out,
           L0_ffn_scores, L0_ffn_expert_weights, L0_ffn_top_experts,
           L0_ffn_w1, L0_ffn_w2):
    x = np.asarray(L0_attn_x)
    in_maps = make_in_maps(x, np.asarray(L0_attn_W_in), np.asarray(L0_attn_b_in),
                           np.asarray(L0_attn_W_out), np.asarray(L0_attn_b_out),
                           np.asarray(L0_ffn_expert_weights),
                           np.asarray(L0_ffn_w1), np.asarray(L0_ffn_w2))
    nc = build_program()
    res = run_bass_kernel_spmd(nc, in_maps, core_ids=list(range(8)))
    out = np.empty((32, T, D), np.float32)
    for c in range(8):
        out[[c, c + 8, c + 16, c + 24]] = res.results[c]["out"]
    return out
